# version 06
# baseline (speedup 1.0000x reference)
import numpy as np
import ml_dtypes
from contextlib import ExitStack
import dataclasses

import concourse.bass as bass
import concourse.mybir as mybir
from concourse.bass_utils import run_bass_kernel_spmd

F32 = mybir.dt.float32
BF16 = mybir.dt.bfloat16
U32 = mybir.dt.uint32
AF = mybir.ActivationFunctionType
ALU = mybir.AluOpType
AX = mybir.AxisListType

D = 1024
NCD = 8
T = 2048
NSEQ = 2
TT = 512
NTT = T // TT
NCORE = 8

LN_EPS = 1e-5
RMS_EPS = 1e-6
L0_E, L0_F, L0_FC = 8, 2048, 16
L1_E, L1_F, L1_FC = 16, 1024, 8
NFREQ = 640
NKC = 5



NQ = 8


class Chains:

    def __init__(self, nc, ctx):
        self.nc = nc
        self.sems = {}
        self.counts = {}
        self.waited = {}
        self.dma_rr = 0
        names = ["v", "s", "t", "p"] + [f"q{i}" for i in range(NQ)]
        for name in names:
            self.sems[name] = ctx.enter_context(nc.semaphore("ch_" + name))
            self.counts[name] = 0

    def eng(self, chn):
        return {"v": self.nc.vector, "s": self.nc.scalar, "t": self.nc.tensor,
                "p": self.nc.gpsimd, "qs": self.nc.sync, "qc": self.nc.scalar}[chn]

    def mark(self, chn, inst, n=1):
        inst.then_inc(self.sems[chn], n)
        self.counts[chn] += n
        return self.counts[chn]

    def wait(self, waiter, producer, tok=None):
        if tok is None:
            tok = self.counts[producer]
        if tok <= 0:
            return
        key = (waiter, producer)
        if self.waited.get(key, -1) >= tok:
            return
        self.eng(waiter).wait_ge(self.sems[producer], tok)
        self.waited[key] = tok

    def dma(self, issuer, out, in_, **kw):
        name = f"q{self.dma_rr}"
        self.dma_rr = (self.dma_rr + 1) % NQ
        self.wait(issuer, name)
        inst = self.eng(issuer).dma_start(out=out, in_=in_, **kw)
        inst.then_inc(self.sems[name], 16)
        self.counts[name] += 16
        return (name, self.counts[name])

    def barrier(self):
        chans = ["v", "s", "t", "p"]
        qs = [f"q{i}" for i in range(NQ)]
        for w in chans + ["qs"]:
            for pr in chans + qs:
                self.wait(w, pr)



class Prog:

    def __init__(self, nc, ctx, ch):
        self.nc = nc
        self.ctx = ctx
        self.ch = ch
        self.sb = lambda name, shape, dt: ctx.enter_context(
            nc.sbuf_tensor(name, shape, dt))
        self.uid = 0

    def name(self, base):
        self.uid += 1
        return f"{base}_{self.uid}"


def diag_ap(base_ap, tensor_cols, offset, nrows, ncols):
    return dataclasses.replace(
        base_ap, offset=offset, ap=[[tensor_cols - 1, nrows], [1, ncols]])



def emit_consts(p):
    nc, ch = p.nc, p.ch
    c = {}
    c["ones_col_bf"] = p.sb("ones_col_bf", [128, 1], BF16)
    c["ones_row_f"] = p.sb("ones_row_f", [1, 128], F32)
    c["eps_ln"] = p.sb("eps_ln", [1, 1], F32)
    c["eps_rms"] = p.sb("eps_rms", [1, 1], F32)
    nc.vector.memset(c["ones_col_bf"][:], 1.0)
    nc.vector.memset(c["ones_row_f"][:], 1.0)
    nc.vector.memset(c["eps_ln"][:], LN_EPS)
    i = nc.vector.memset(c["eps_rms"][:], RMS_EPS)
    ch.mark("v", i)
    return c



def xview3(X, t):
    return dataclasses.replace(
        X[:], offset=t * TT, ap=[[NCD * T, 128], [T, NCD], [1, TT]])


def _half(v, h):
    return dataclasses.replace(
        v, offset=v.offset + h * (NCD // 2) * v.ap[1][0],
        ap=[v.ap[0], [v.ap[1][0], NCD // 2], v.ap[2]])


def cview(tile):
    return dataclasses.replace(
        tile[:], ap=[[NCD * 512, 128], [512, NCD], [1, 512]])


def bview(tile, n):
    return dataclasses.replace(tile[:], ap=[tile[:].ap[0], [0, n], [1, TT]])


def emit_ln_tile(p, c, X, t, xn_view, scr, rms_fold=False):
    nc, ch = p.nc, p.ch
    V, S, TE, P = nc.vector, nc.scalar, nc.tensor, nc.gpsimd
    xb, xsq, ps_stat, ps_b, ab_sb, bb_sb, t1 = (
        scr["xb"], scr["xsq"], scr["ps_stat"], scr["ps_b"],
        scr["ab_sb"], scr["bb_sb"], scr["t1"])
    rb = scr["rowbuf"]
    s1, s2 = rb[0:1, 0:TT], rb[0:1, TT:2 * TT]
    aa, bb = rb[0:1, 2 * TT:3 * TT], rb[0:1, 3 * TT:4 * TT]

    if scr.get("reuse_tok_t") is not None:
        ch.wait("p", "t", scr["reuse_tok_t"])
        ch.wait("s", "t", scr["reuse_tok_t"])
    cast_tok = ch.mark("p", P.tensor_copy(out=xb[:], in_=xview3(X, t)))
    sq_tok = ch.mark("s", S.activation(out=xsq[:], in_=xview3(X, t),
                                       func=AF.Square))

    ch.wait("t", "p", cast_tok)
    ch.wait("t", "s", sq_tok)
    if scr.get("stat_read_tok") is not None:
        ch.wait("t", "v", scr["stat_read_tok"])
    for cc in range(NCD):
        TE.matmul(out=ps_stat[0:1, 0:TT], lhsT=c["ones_col_bf"][:],
                  rhs=xb[:, cc * TT:(cc + 1) * TT],
                  start=(cc == 0), stop=(cc == NCD - 1))
    for cc in range(NCD):
        i = TE.matmul(out=ps_stat[64:65, 0:TT], lhsT=c["ones_col_bf"][:],
                      rhs=xsq[:, cc * TT:(cc + 1) * TT],
                      start=(cc == 0), stop=(cc == NCD - 1),
                      skip_group_check=True)
    stat_tok = ch.mark("t", i)

    ch.wait("v", "t", stat_tok)
    ch.mark("v", V.tensor_scalar(out=s1, in0=ps_stat[0:1, 0:TT],
                                 scalar1=1.0 / D, scalar2=None, op0=ALU.mult))
    srd = ch.mark("v", V.tensor_scalar(out=s2, in0=ps_stat[64:65, 0:TT],
                                       scalar1=1.0 / D, scalar2=None,
                                       op0=ALU.mult))
    scr["stat_read_tok"] = srd
    ch.wait("v", "v")
    ch.mark("v", V.tensor_tensor(out=bb, in0=s1, in1=s1, op=ALU.mult))
    ch.wait("v", "v")
    var_tok = ch.mark("v", V.tensor_tensor(out=s2, in0=s2, in1=bb, op=ALU.subtract))
    ch.wait("s", "v", var_tok)
    lt = ch.mark("s", S.activation(out=aa, in_=s2, func=AF.Ln, bias=c["eps_ln"][:]))
    ch.wait("s", "s", lt)
    sq2_tok = ch.mark("s", S.activation(out=aa, in_=aa, func=AF.Exp, scale=-0.5))
    ch.wait("v", "s", sq2_tok)
    if rms_fold:
        ch.mark("v", V.tensor_tensor(out=s2, in0=s2, in1=aa, op=ALU.mult))
        ch.wait("v", "v")
        rt = ch.mark("v", V.tensor_tensor(out=s2, in0=s2, in1=aa, op=ALU.mult))
        ch.wait("s", "v", rt)
        lt = ch.mark("s", S.activation(out=s2, in_=s2, func=AF.Ln,
                                       bias=c["eps_rms"][:]))
        ch.wait("s", "s", lt)
        st = ch.mark("s", S.activation(out=s2, in_=s2, func=AF.Exp, scale=-0.5))
        ch.wait("v", "s", st)
        ch.mark("v", V.tensor_tensor(out=aa, in0=aa, in1=s2, op=ALU.mult))
        ch.wait("v", "v")
    ch.mark("v", V.tensor_tensor(out=bb, in0=s1, in1=aa, op=ALU.mult))
    ch.wait("v", "v")
    row_tok = ch.mark("v", V.tensor_scalar(out=bb, in0=bb, scalar1=-1.0,
                                           scalar2=None, op0=ALU.mult))

    ch.wait("t", "v", row_tok)
    if scr.get("bcast_read_tok") is not None:
        ch.wait("t", "v", scr["bcast_read_tok"])
    TE.matmul(out=ps_b[:, 0:TT], lhsT=c["ones_row_f"][:], rhs=aa, start=True,
              stop=True)
    i = TE.matmul(out=ps_b[:, TT:2 * TT], lhsT=c["ones_row_f"][:], rhs=bb,
                  start=True, stop=True)
    bc_tok = ch.mark("t", i)

    ch.wait("v", "t", bc_tok)
    if scr.get("pass2_tok") is not None:
        ch.wait("v", "p", scr["pass2_tok"])
    ch.mark("v", V.tensor_copy(out=ab_sb[:], in_=ps_b[:, 0:TT]))
    evb_tok = ch.mark("v", V.tensor_copy(out=bb_sb[:], in_=ps_b[:, TT:2 * TT]))
    scr["bcast_read_tok"] = evb_tok

    ch.wait("v", "v", evb_tok)
    p2_tok = None
    for h in range(2):
        if h == 1 or scr.get("pass2_tok") is not None:
            ch.wait("v", "p", p2_tok if h == 1 else scr["pass2_tok"])
        i = V.tensor_tensor(out=t1[:], in0=_half(xview3(X, t), h),
                            in1=bview(ab_sb, NCD // 2), op=ALU.mult)
        p1 = ch.mark("v", i)
        ch.wait("p", "v", p1)
        i = P.tensor_tensor(out=_half(xn_view, h), in0=t1[:],
                            in1=bview(bb_sb, NCD // 2), op=ALU.add)
        p2_tok = ch.mark("p", i)
    scr["reuse_tok_t"] = stat_tok
    scr["pass2_tok"] = p2_tok
    return p2_tok


def emit_ln_scratch(p, c, sctx, sep_stat=False):
    sb = lambda name, shape, dt: sctx.enter_context(
        p.nc.sbuf_tensor(p.name(name), shape, dt))
    ps = lambda name, shape, dt: sctx.enter_context(
        p.nc.psum_tensor(p.name(name), shape, dt))
    scr = {
        "rowbuf": sb("ln_rowbuf", [1, 4 * TT], F32),
        "xb": sb("ln_xb", [128, NCD * TT], BF16),
        "xsq": sb("ln_xsq", [128, NCD * TT], BF16),
        "t1": sb("ln_t1", [128, (NCD // 2) * TT], F32),
        "ab_sb": sb("ln_ab", [128, TT], F32),
        "bb_sb": sb("ln_bb", [128, TT], F32),
        "ps_b": ps("ln_ps_b", [128, 2 * TT], F32),
        "reuse_tok_t": None,
        "stat_read_tok": None,
        "bcast_read_tok": None,
        "pass2_tok": None,
    }
    if sep_stat:
        scr["ps_stat"] = ps("ln_ps_stat", [128, TT], F32)
    else:
        scr["ps_stat"] = scr["ps_b"]
    return scr



class WLoader:

    def __init__(self, p, sctx, tag, cols, nbuf=2, nstage=1):
        self.p = p
        self.cols = cols
        self.nbuf = nbuf
        self.nstage = nstage
        sb = lambda name, shape, dt: sctx.enter_context(
            p.nc.sbuf_tensor(p.name(name), shape, dt))
        self.stage = sb(f"{tag}_stg", [128, nstage * cols], F32)
        self.bf = sb(f"{tag}_bf", [128, nbuf * cols], BF16)
        self.slot_cast_tok = [None] * nbuf
        self.slot_read_tok = [None] * nbuf
        self.seq = 0

    def load(self, src_ap, read_done=None, cast="p", scale_aps=None,
             chunk_cols=128, ncols=None):
        p, ch, nc = self.p, self.p.ch, self.p.nc
        n = self.cols if ncols is None else ncols
        slot = self.seq % self.nbuf
        sslot = self.seq % self.nstage
        self.seq += 1
        st = self.stage[:, sslot * self.cols: sslot * self.cols + n]
        bf = self.bf[:, slot * self.cols: slot * self.cols + n]
        prev = self.seq - 1 - self.nstage
        if prev >= 0:
            cc, tt = self.slot_cast_tok[prev % self.nbuf]
            ch.wait("qs", cc, tt)
        dma_tok = ch.dma("qs", st, src_ap)
        ch.wait("s", *dma_tok)
        if self.slot_read_tok[slot] is not None:
            cc, tt = self.slot_read_tok[slot]
            ch.wait("s", cc, tt)
        if scale_aps is None:
            i = nc.scalar.copy(out=bf, in_=st)
        else:
            n = self.cols // chunk_cols
            for k in range(n):
                i = nc.scalar.activation(
                    out=bf[:, k * chunk_cols:(k + 1) * chunk_cols],
                    in_=st[:, k * chunk_cols:(k + 1) * chunk_cols],
                    func=AF.Copy, scale=scale_aps[k])
        tok = ch.mark("s", i)
        self.slot_cast_tok[slot] = ("s", tok)
        return slot, ("s", tok)

    def view(self, slot, ncols=None):
        n = self.cols if ncols is None else ncols
        return self.bf[:, slot * self.cols: slot * self.cols + n]

    def set_read(self, slot, chain, tok):
        self.slot_read_tok[slot] = (chain, tok)



def emit_ffn(p, c, X, w, layer):
    nc, ch = p.nc, p.ch
    V, S, TE, P = nc.vector, nc.scalar, nc.tensor, nc.gpsimd
    E = L0_E if layer == "l0" else L1_E
    FC = L0_FC if layer == "l0" else L1_FC

    sctx = ExitStack()
    with sctx:
        sb = lambda name, shape, dt: sctx.enter_context(
            nc.sbuf_tensor(p.name(name), shape, dt))
        ps = lambda name, shape, dt: sctx.enter_context(
            nc.psum_tensor(p.name(name), shape, dt))

        XN = sb("ffn_xn", [128, NCD * T], BF16)
        E16 = E if E >= 8 else 8
        selc = sb("selc", [E16, E * 128], BF16)
        sc_tok = ch.dma("qs", selc[:], w["selc"][:])
        rw = sb("rw", [128, NCD * E], BF16)
        rw_st = sb("rw_st", [128, NCD * E], F32)
        tk = ch.dma("qs", rw_st[:], w["rw"].rearrange("(c p) e -> p c e", p=128))
        ch.wait("p", *tk)
        ch.wait("p", "t")
        rw_tok = ch.mark("p", nc.gpsimd.tensor_copy(out=rw[:], in_=rw_st[:]))
        if layer == "l1":
            ebias_sb = sb("ebias_sb", [1, 16], F32)
            eb_tok = ch.dma("qs", ebias_sb[:], w["ebias"][:])

        xn_toks = []
        scr = emit_ln_scratch(p, c, sctx, sep_stat=True)
        for t in range(NTT):
            xn_toks.append(emit_ln_tile(p, c, X, t, xview3(XN, t), scr))

        nsl = T // 128
        c_fm = sb("c_fm", [E if E >= 8 else 8, nsl * 128], BF16)
        c_all = sb("c_all", [128, nsl * E], F32)
        r_sc = sb("r_sc", [128, 16 + 16 + 4], F32)
        r_sc2 = sb("r_sc2", [128, 16], F32)
        if layer == "l1":
            bias_b = sb("bias_b", [128, 16], F32)
        ps_cb = ps("ps_cb", [128, 512], F32)
        for tk_ in c["consts_tok"]:
            ch.wait("t", *tk_)
        ch.wait("t", "p", rw_tok)
        for sl_ in range(nsl):
            ch.wait("t", "p", xn_toks[sl_ // 4])
            for dc in range(NCD):
                i = TE.matmul(out=ps_cb[:, sl_ * 16: sl_ * 16 + E],
                              lhsT=XN[:, dc * T + sl_ * 128: dc * T + (sl_ + 1) * 128],
                              rhs=rw[:, dc * E:(dc + 1) * E],
                              start=(dc == 0), stop=(dc == NCD - 1))
        lg_tok = ch.mark("t", i)

        def emit_router_coeffs():
            if layer == "l1":
                ch.wait("t", *eb_tok)
                i = TE.matmul(out=ps_cb[:, 496:512], lhsT=c["ones_row_f"][:],
                              rhs=ebias_sb[0:1, :], start=True, stop=True,
                              skip_group_check=True)
                bt = ch.mark("t", i)
                ch.wait("v", "t", bt)
                bias_ev = ch.mark("v", V.tensor_copy(out=bias_b[:],
                                                     in_=ps_cb[:, 496:512]))
                ch.wait("v", "v")
                cb_guard = ("v", bias_ev)
            else:
                cb_guard = None
            ch.wait("v", "t")
            ch.wait("s", "t")
            for s_ in range(nsl):
                lgs = ps_cb[:, s_ * 16: s_ * 16 + E]
                logits = r_sc[:, 0:E]
                probs = r_sc[:, 16:16 + E]
                vals = r_sc2[:, 0:8]
                nmax = r_sc[:, 32:33]
                den = r_sc[:, 33:34]
                cdst = c_all[:, s_ * E: (s_ + 1) * E]
                ch.wait("v", "s")
                if layer == "l0":
                    ch.mark("v", V.reduce_max(out=nmax, in_=lgs, axis=AX.X,
                                              negate=True))
                    ch.wait("s", "v")
                    et = ch.mark("s", S.activation(out=logits, in_=lgs,
                                                   func=AF.Exp, bias=nmax))
                    ch.wait("v", "s", et)
                    ch.mark("v", V.reduce_sum(out=den, in_=logits, axis=AX.X))
                    ch.wait("v", "v")
                    ch.mark("v", V.reciprocal(out=den, in_=den))
                    ch.wait("v", "v")
                    ch.mark("v", V.tensor_scalar(out=probs, in0=logits,
                                                 scalar1=den, scalar2=None,
                                                 op0=ALU.mult))
                    ch.wait("v", "v")
                    ch.mark("v", V.max(out=vals, in_=probs))
                    ch.wait("v", "v")
                    ch.mark("v", V.tensor_scalar(out=logits, in0=probs,
                                                 scalar1=vals[:, 1:2],
                                                 scalar2=None, op0=ALU.is_ge))
                    ch.wait("s", "v")
                    et = ch.mark("s", S.activation(out=r_sc2[:, 8:8 + E],
                                                   in_=probs, func=AF.Exp))
                    ch.wait("v", "s", et)
                    ch.mark("v", V.tensor_tensor(out=probs,
                                                 in0=r_sc2[:, 8:8 + E],
                                                 in1=logits, op=ALU.mult))
                    ch.wait("v", "v")
                    ch.mark("v", V.reduce_sum(out=den, in_=probs, axis=AX.X))
                    ch.wait("v", "v")
                    ch.mark("v", V.reciprocal(out=den, in_=den))
                    ch.wait("v", "v")
                    ch.mark("v", V.tensor_scalar(out=cdst, in0=probs,
                                                 scalar1=den, scalar2=0.5,
                                                 op0=ALU.mult, op1=ALU.mult))
                else:
                    ch.wait("s", "v")
                    st = ch.mark("s", S.activation(out=probs, in_=lgs,
                                                   func=AF.Sigmoid))
                    ch.wait("v", "s", st)
                    ch.mark("v", V.tensor_tensor(out=logits, in0=probs,
                                                 in1=bias_b[:], op=ALU.add))
                    ch.wait("v", "v")
                    ch.mark("v", V.max(out=vals, in_=logits))
                    ch.wait("v", "v")
                    ch.mark("v", V.tensor_scalar(out=logits, in0=logits,
                                                 scalar1=vals[:, 7:8],
                                                 scalar2=None, op0=ALU.is_ge))
                    ch.wait("v", "v")
                    ch.mark("v", V.tensor_tensor(out=probs, in0=probs,
                                                 in1=logits, op=ALU.mult))
                    ch.wait("v", "v")
                    ch.mark("v", V.reduce_sum(out=den, in_=probs, axis=AX.X))
                    ch.wait("v", "v")
                    ch.mark("v", V.reciprocal(out=den, in_=den))
                    ch.wait("v", "v")
                    ch.mark("v", V.tensor_scalar(out=cdst, in0=probs,
                                                 scalar1=den,
                                                 scalar2=2.826 * 0.5,
                                                 op0=ALU.mult, op1=ALU.mult))
            cw_tok = ch.counts["v"]
            ch.wait("t", "v", cw_tok)
            if cb_guard is not None:
                ch.wait("t", cb_guard[0], cb_guard[1])
            tr_rd = None
            for s_ in range(nsl):
                if tr_rd is not None:
                    ch.wait("t", "v", tr_rd)
                i = TE.transpose(out=ps_cb[0:E, 0:128],
                                 in_=c_all[:, s_ * E: (s_ + 1) * E],
                                 identity=c["ident_f"][:])
                tt = ch.mark("t", i)
                ch.wait("v", "t", tt)
                tr_rd = ch.mark("v", V.tensor_copy(
                    out=c_fm[:, s_ * 128:(s_ + 1) * 128],
                    in_=ps_cb[0:E, 0:128]))
            nonlocal cb_read
            cb_read = ("v", tr_rd)

        fuse = sb("fuse", [128, (FC // 2) * T], BF16)
        act = sb("act", [128, TT], BF16)
        csb = sb("csb", [128, NTT * TT], BF16)
        dtmp = sb("dtmp", [128, TT], F32)
        ps_g = ps("ps_g", [128, 2 * 512], F32)
        ps_u = ps("ps_u", [128, 2 * 512], F32)
        ps_d = scr["ps_b"]
        wl_g = WLoader(p, sctx, "wg", NCD * 128)
        wl_u = WLoader(p, sctx, "wu", NCD * 128)

        gu_read = {}
        d_read = {}
        act_read = {}
        fuse_rd_tok = None
        csb_read = None
        dtmp_read = {}
        x_acc = {}
        gk = w["gk"]
        uk = w["uk"]
        dk = w["dk"]

        n_exp = E + (1 if layer == "l1" else 0)
        NH = 2
        FCH = FC // NH
        cb_read = None
        tr_done = None

        def emit_csb(e, shared):
            nonlocal cb_read, csb_read
            if shared:
                if csb_read is not None:
                    ch.wait("v", csb_read[0], csb_read[1])
                rdy = ch.mark("v", V.memset(csb[:], 0.5))
                csb_read = None
                return rdy
            for t in range(NTT):
                ch.wait("t", "v", tr_done)
                ch.wait("t", *sc_tok)
                if cb_read is not None:
                    ch.wait("t", cb_read[0], cb_read[1])
                sel = selc[:, e * 128:(e + 1) * 128]
                i = TE.matmul(out=ps_cb[:, 0:TT], lhsT=sel,
                              rhs=c_fm[:, t * TT:(t + 1) * TT],
                              start=True, stop=True)
                bt = ch.mark("t", i)
                ch.wait("v", "t", bt)
                if csb_read is not None:
                    ch.wait("v", csb_read[0], csb_read[1])
                cb_tok = ch.mark("v", V.tensor_copy(
                    out=csb[:, t * TT:(t + 1) * TT], in_=ps_cb[:, 0:TT]))
                cb_read = ("v", cb_tok)
            csb_read = None
            return ch.counts["v"]

        for e in range(n_exp):
            shared = (e == E)
            def gsl_src(fc):
                if shared:
                    return (w["sg"][:, fc * 128:(fc + 1) * 128].rearrange(
                        "(c p) f -> p c f", p=128),
                        w["su"][:, fc * 128:(fc + 1) * 128].rearrange(
                        "(c p) f -> p c f", p=128))
                return (gk[e, :, fc * 128:(fc + 1) * 128].rearrange(
                    "(c p) f -> p c f", p=128),
                    uk[e, :, fc * 128:(fc + 1) * 128].rearrange(
                    "(c p) f -> p c f", p=128))

            for half in range(NH):
                step = 0
                g0_, u0_ = gsl_src(half * FCH)
                nxt = (wl_g.load(g0_), wl_u.load(u0_))
                for fcl in range(FCH):
                    if e == 0 and half == 0 and fcl == 1:
                        emit_router_coeffs()
                        tr_done = ch.counts["v"]
                    fc = half * FCH + fcl
                    (sg, gtok), (su, utok) = nxt
                    if fcl + 1 < FCH:
                        gn, un = gsl_src(fc + 1)
                        nxt = (wl_g.load(gn), wl_u.load(un))
                    wgb, wub = wl_g.view(sg), wl_u.view(su)
                    for t in range(NTT):
                        bk = step % 2
                        gsl = ps_g[:, bk * 512: bk * 512 + TT]
                        usl = ps_u[:, bk * 512: bk * 512 + TT]
                        ch.wait("t", "p", xn_toks[t])
                        ch.wait("t", gtok[0], gtok[1])
                        ch.wait("t", utok[0], utok[1])
                        if bk in gu_read:
                            ch.wait("t", gu_read[bk][0], gu_read[bk][1])
                        for dc in range(NCD):
                            TE.matmul(out=gsl,
                                      lhsT=wgb[:, dc * 128:(dc + 1) * 128],
                                      rhs=XN[:, dc * T + t * TT: dc * T + t * TT + TT],
                                      start=(dc == 0), stop=(dc == NCD - 1))
                        for dc in range(NCD):
                            i = TE.matmul(out=usl,
                                          lhsT=wub[:, dc * 128:(dc + 1) * 128],
                                          rhs=XN[:, dc * T + t * TT: dc * T + t * TT + TT],
                                          start=(dc == 0), stop=(dc == NCD - 1))
                        mm_tok = ch.mark("t", i)
                        asl = act[:, 0:TT]
                        ch.wait("s", "t", mm_tok)
                        if 0 in act_read:
                            ch.wait("s", act_read[0][0], act_read[0][1])
                        sa = ch.mark("s", S.activation(out=asl, in_=gsl, func=AF.Silu))
                        ch.wait("v", "s", sa)
                        if fuse_rd_tok is not None:
                            ch.wait("v", "t", fuse_rd_tok)
                        fv = ch.mark("v", V.tensor_tensor(
                            out=fuse[:, fcl * T + t * TT: fcl * T + t * TT + TT],
                            in0=asl, in1=usl, op=ALU.mult))
                        act_read[0] = ("v", fv)
                        gu_read[bk] = ("v", fv)
                        step += 1
                    wl_g.set_read(sg, "t", ch.counts["t"])
                    wl_u.set_read(su, "t", ch.counts["t"])
                fuse_v_tok = ch.counts["v"]
                if half == 0:
                    csb_ready = emit_csb(e, shared)

                dbanks = [ps_d[:, 0:512], ps_d[:, 512:1024],
                          ps_g[:, 0:512], ps_g[:, 512:1024],
                          ps_u[:, 0:512], ps_u[:, 512:1024]]
                dstep = 0
                r0, r1 = half * FCH * 128, (half + 1) * FCH * 128

                def dsl_src(dc):
                    if shared:
                        return w["sd"][r0:r1, dc * 128:(dc + 1) * 128].rearrange(
                            "(c p) f -> p c f", p=128)
                    return dk[e, r0:r1, dc * 128:(dc + 1) * 128].rearrange(
                        "(c p) f -> p c f", p=128)

                dnxt = wl_g.load(dsl_src(0), ncols=FCH * 128)
                for dc in range(NCD):
                    sd_, dtok = dnxt
                    if dc + 1 < NCD:
                        dnxt = wl_g.load(dsl_src(dc + 1), ncols=FCH * 128)
                    wdb = wl_g.view(sd_, ncols=FCH * 128)
                    for t in range(NTT):
                        bk = dstep % 6
                        dsl = dbanks[bk]
                        ch.wait("t", dtok[0], dtok[1])
                        ch.wait("t", "v", fuse_v_tok)
                        prev = d_read.get(bk)
                        if prev is None or prev[1] is None:
                            if bk < 2:
                                ch.wait("t", "v", scr["bcast_read_tok"])
                            elif bk < 4:
                                gr = gu_read.get(0)
                                if gr:
                                    ch.wait("t", gr[0], gr[1])
                            else:
                                gr = gu_read.get(1)
                                if gr:
                                    ch.wait("t", gr[0], gr[1])
                        else:
                            ch.wait("t", prev[0], prev[1])
                        for fcl in range(FCH):
                            i = TE.matmul(
                                out=dsl,
                                lhsT=wdb[:, fcl * 128:(fcl + 1) * 128],
                                rhs=fuse[:, fcl * T + t * TT: fcl * T + t * TT + TT],
                                start=(fcl == 0), stop=(fcl == FCH - 1))
                        dmm_tok = ch.mark("t", i)
                        tsl = dtmp[:, 0:TT]
                        ch.wait("v", "t", dmm_tok)
                        ch.wait("v", "v", csb_ready)
                        if 0 in dtmp_read:
                            ch.wait("v", dtmp_read[0][0], dtmp_read[0][1])
                        tv = ch.mark("v", V.tensor_tensor(
                            out=tsl, in0=dsl, in1=csb[:, t * TT:(t + 1) * TT],
                            op=ALU.mult))
                        d_read[bk] = ("v", tv)
                        csb_read = ("v", tv)
                        ch.wait("p", "v", tv)
                        if (dc, t) in x_acc:
                            ch.wait("p", "p", x_acc[(dc, t)])
                        xc = X[:, dc * T + t * TT: dc * T + t * TT + TT]
                        px = ch.mark("p", P.tensor_tensor(out=xc, in0=xc, in1=tsl,
                                                          op=ALU.add))
                        x_acc[(dc, t)] = px
                        dtmp_read[0] = ("p", px)
                        dstep += 1
                    wl_g.set_read(sd_, "t", ch.counts["t"])
                for gb, pair in ((0, (2, 4)), (1, (3, 5))):
                    toks = [d_read[i][1] for i in pair if i in d_read]
                    if toks:
                        cur = gu_read.get(gb, ("v", 0))[1]
                        gu_read[gb] = ("v", max([cur] + toks))
                fuse_rd_tok = ch.counts["t"]
        ch.barrier()



def emit_l0attn_setup(p, c, aps, sctx):
    nc, ch = p.nc, p.ch
    V, S, TE, P = nc.vector, nc.scalar, nc.tensor, nc.gpsimd
    sb = lambda name, shape, dt: sctx.enter_context(
        nc.sbuf_tensor(p.name(name), shape, dt))
    a = {}
    a["gln"] = sb("gln", [128, NCD], F32)
    a["glnq"] = sb("glnq", [128, NCD], F32)
    a["xlu"] = sb("xlu", [128, 1], F32)
    a["xlv"] = sb("xlv", [128, 1], F32)
    a["binmask"] = sb("binmask", [128, 4 * 512], BF16)
    a["wq_bf"] = sb("wq_bf", [128, NCD * 128], BF16)
    a["wk_bf"] = sb("wk_bf", [128, NCD * 128], BF16)
    a["r_sb"] = sb("r_sb", [128, T], BF16)

    t1 = ch.dma("qs", a["gln"][:], aps["gln"][:])
    ch.dma("qs", a["xlu"][:], aps["xlu"][:])
    ch.dma("qs", a["xlv"][:], aps["xlv"][:])
    ch.dma("qs", a["binmask"][:],
           aps["binmask"].rearrange("(r p) j -> p r j", p=128))
    ch.wait("v", *t1)
    glt = ch.mark("v", V.tensor_scalar(out=a["glnq"][:], in0=a["gln"][:],
                                       scalar1=float(1.0 / np.sqrt(128.0)),
                                       scalar2=None, op0=ALU.mult))

    setup_ctx = ExitStack()
    setup_ctx.__enter__()
    stg = setup_ctx.enter_context(
        nc.sbuf_tensor(p.name("wqk_stg"), [128, NCD * 128], F32))
    for nm, dst, svec, vtok in (("wq", a["wq_bf"], a["glnq"], glt),
                                ("wk", a["wk_bf"], a["gln"], t1)):
        tk = ch.dma("qs", stg[:], aps[nm].rearrange("(c p) k -> p c k", p=128))
        ch.wait("s", *tk)
        ch.wait("s", "v", glt)
        for dc in range(NCD):
            i = S.activation(out=dst[:, dc * 128:(dc + 1) * 128],
                             in_=stg[:, dc * 128:(dc + 1) * 128],
                             func=AF.Copy, scale=svec[:, dc:dc + 1])
        stok = ch.mark("s", i)
        ch.wait("qs", "s", stok)

    wr_bf = setup_ctx.enter_context(
        nc.sbuf_tensor(p.name("wr_bf"), [128, NCD * 128], BF16))
    tk = ch.dma("qs", stg[:], aps["wr"].rearrange("(c p) k -> p c k", p=128))
    ch.wait("p", *tk)
    wrt = ch.mark("p", P.tensor_copy(out=wr_bf[:], in_=stg[:]))

    sin_st = setup_ctx.enter_context(
        nc.sbuf_tensor(p.name("sin_st"), [128, NCD * 512], BF16))
    pctx = ExitStack()
    with pctx:
        ps_r = pctx.enter_context(nc.psum_tensor(p.name("ps_rsb"), [128, 512], F32))
        ev_tok = None
        for pg in range(4):
            tk = ch.dma("qs", sin_st[:],
                        aps["sinembT"].rearrange("(c p) t -> p c t", p=128)
                        [:, :, pg * 512:(pg + 1) * 512])
            ch.wait("t", *tk)
            ch.wait("t", "p", wrt)
            if ev_tok is not None:
                ch.wait("t", "v", ev_tok)
            for dc in range(NCD):
                i = TE.matmul(out=ps_r[:, 0:512],
                              lhsT=wr_bf[:, dc * 128:(dc + 1) * 128],
                              rhs=sin_st[:, dc * 512:(dc + 1) * 512],
                              start=(dc == 0), stop=(dc == NCD - 1))
            mt = ch.mark("t", i)
            ch.wait("v", "t", mt)
            ev_tok = ch.mark("v", V.tensor_copy(
                out=a["r_sb"][:, pg * 512:(pg + 1) * 512], in_=ps_r[:, 0:512]))
            ch.wait("qs", "v", ev_tok)
        ch.barrier()
    setup_ctx.close()
    return a


def emit_l0attn(p, c, X, aps):
    nc, ch = p.nc, p.ch
    V, S, TE, P = nc.vector, nc.scalar, nc.tensor, nc.gpsimd
    sctx = ExitStack()
    with sctx:
        sb = lambda name, shape, dt: sctx.enter_context(
            nc.sbuf_tensor(p.name(name), shape, dt))
        ps = lambda name, shape, dt: sctx.enter_context(
            nc.psum_tensor(p.name(name), shape, dt))

        a = emit_l0attn_setup(p, c, aps, sctx)
        A_fm = sb("A_fm", [128, NCD * T], BF16)
        s01ctx = ExitStack()
        s01ctx.__enter__()
        sb01 = lambda name, shape, dt: s01ctx.enter_context(
            nc.sbuf_tensor(p.name(name), shape, dt))
        qu = sb01("qu", [128, T], BF16)
        qv = sb01("qv", [128, T], BF16)
        k_sb = sb01("k_sb", [128, T], BF16)
        xn_tm = sb01("xn_tm", [128, 16 * 1024], BF16)

        l0ctx = ExitStack()
        with l0ctx:
            scr = emit_ln_scratch(p, c, l0ctx, sep_stat=True)
            xn_t = l0ctx.enter_context(
                nc.sbuf_tensor(p.name("xn_t"), [128, NCD * 512], BF16))
            ps_qk = l0ctx.enter_context(
                nc.psum_tensor(p.name("ps_qk"), [128, 1024], F32))
            ps_tp = l0ctx.enter_context(
                nc.psum_tensor(p.name("ps_tp"), [128, 1024], BF16))
            qk_read = {}
            tp_read = None
            xn_read = [None]
            for t in range(NTT):
                slot = 0
                base = 0
                if xn_read[slot] is not None:
                    ch.wait("p", "t", xn_read[slot])
                xn_tok = emit_ln_tile(p, c, X, t, cview(xn_t), scr,
                                      rms_fold=True)
                for which, (wbf, bank) in enumerate(
                        ((a["wq_bf"], 0), (a["wk_bf"], 1))):
                    sl = ps_qk[:, bank * 512: bank * 512 + 512]
                    ch.wait("t", "p", xn_tok)
                    if bank in qk_read:
                        ch.wait("t", qk_read[bank][0], qk_read[bank][1])
                    for dc in range(NCD):
                        i = TE.matmul(out=sl, lhsT=wbf[:, dc * 128:(dc + 1) * 128],
                                      rhs=xn_t[:, base + dc * 512: base + dc * 512 + 512],
                                      start=(dc == 0), stop=(dc == NCD - 1))
                    mt = ch.mark("t", i)
                    ch.wait("s", "t", mt)
                    if which == 0:
                        S.activation(out=qu[:, t * 512:(t + 1) * 512], in_=sl,
                                     func=AF.Identity, bias=a["xlu"][:])
                        i = S.activation(out=qv[:, t * 512:(t + 1) * 512], in_=sl,
                                         func=AF.Identity, bias=a["xlv"][:])
                    else:
                        i = S.activation(out=k_sb[:, t * 512:(t + 1) * 512],
                                         in_=sl, func=AF.Identity)
                    qk_read[bank] = ("s", ch.mark("s", i))
                for sl4 in range(4):
                    if tp_read is not None:
                        ch.wait("t", "v", tp_read)
                    ch.wait("t", "p", xn_tok)
                    for dc in range(NCD):
                        i = TE.transpose(
                            out=ps_tp[:, dc * 128:(dc + 1) * 128],
                            in_=xn_t[:, base + dc * 512 + sl4 * 128:
                                     base + dc * 512 + sl4 * 128 + 128],
                            identity=c["ident_bf"][:])
                    tt = ch.mark("t", i)
                    ch.wait("v", "t", tt)
                    tp_read = ch.mark("v", V.tensor_copy(
                        out=xn_tm[:, (t * 4 + sl4) * 1024:(t * 4 + sl4 + 1) * 1024],
                        in_=ps_tp[:]))
                xn_read[slot] = ch.counts["t"]
            ch.barrier()

        l1ctx = ExitStack()
        with l1ctx:
            sb1 = lambda name, shape, dt: l1ctx.enter_context(
                nc.sbuf_tensor(p.name(name), shape, dt))
            raw = sb1("raw", [128, 2560], BF16)
            shifted = sb1("shifted", [128, T], BF16)
            s_sb = sb1("s_sb", [128, 2 * T], F32)
            w_sb = sb1("w_sb", [128, 2 * T], BF16)
            wT_sb = sb1("wT_sb", [128, 16 * 128], BF16)
            a_tm = sb1("a_tm", [128, 1024], BF16)
            rows = sb1("rows", [128, 4], F32)
            ps_pool = l1ctx.enter_context(
                nc.psum_tensor(p.name("ps_pool"), [128, 2048], F32))
            ps_tp1 = l1ctx.enter_context(
                nc.psum_tensor(p.name("ps_tp1"), [128, 1024], BF16))
            ps_a = l1ctx.enter_context(
                nc.psum_tensor(p.name("ps_a"), [128, 1024], F32))

            zt = ch.mark("v", V.memset(raw[:, 2048:2560], 0.0))
            st1 = {
                "bank_read": {}, "shift_read": None, "wT_read": None,
                "wsb_read": {}, "tp1_read": None, "a_read": None,
                "atm_read": None, "raw_dma": None, "s_tok": {}, "exp": {},
            }

            def emit_scores(qi):
                I0 = qi * 128
                ng = qi // 4 + 1
                g0 = (2047 - I0 - 127) // 512
                buf = qi % 2
                br = st1["bank_read"]
                for g in range(g0, 4):
                    if g in br:
                        ch.wait("t", br[g][0], br[g][1])
                    i = TE.matmul(out=ps_pool[:, g * 512:(g + 1) * 512],
                                  lhsT=qv[:, I0:I0 + 128],
                                  rhs=a["r_sb"][:, g * 512:(g + 1) * 512],
                                  start=True, stop=True)
                    br.pop(g, None)
                bd_tok = ch.mark("t", i)
                ch.wait("v", "t", bd_tok)
                ch.wait("v", "v", zt)
                if st1["raw_dma"] is not None:
                    ch.wait("v", *st1["raw_dma"])
                for g in range(g0, 4):
                    i = V.tensor_copy(out=raw[:, g * 512:(g + 1) * 512],
                                      in_=ps_pool[:, g * 512:(g + 1) * 512])
                raw_tok = ch.mark("v", i)
                ch.wait("qs", "v", raw_tok)
                if st1["shift_read"] is not None:
                    ch.wait("qs", "v", st1["shift_read"])
                dtok = ch.dma("qs", shifted[:, 0:ng * 512],
                              diag_ap(raw[:], 2560, 2047 - I0, 128, ng * 512))
                st1["raw_dma"] = dtok
                for g in range(ng):
                    ch.wait("t", "v", raw_tok)
                    if g in br:
                        ch.wait("t", br[g][0], br[g][1])
                    i = TE.matmul(out=ps_pool[:, g * 512:(g + 1) * 512],
                                  lhsT=qu[:, I0:I0 + 128],
                                  rhs=k_sb[:, g * 512:(g + 1) * 512],
                                  start=True, stop=True)
                ac_tok = ch.mark("t", i)
                ch.wait("v", "t", ac_tok)
                ch.wait("v", *dtok)
                ch.wait("v", "s")
                for g in range(ng):
                    i = V.tensor_tensor(
                        out=s_sb[:, buf * T + g * 512: buf * T + (g + 1) * 512],
                        in0=ps_pool[:, g * 512:(g + 1) * 512],
                        in1=shifted[:, g * 512:(g + 1) * 512],
                        op=ALU.add)
                    br[g] = ("v", ch.mark("v", i))
                st1["s_tok"][qi] = ch.counts["v"]
                st1["shift_read"] = ch.counts["v"]

            def emit_out(qi):
                I0 = qi * 128
                ng = qi // 4 + 1
                nj = qi + 1
                buf = qi % 2
                ssl = s_sb[:, buf * T: buf * T + ng * 512]
                wsl = w_sb[:, buf * T: buf * T + ng * 512]
                nmax = rows[:, 2 * buf: 2 * buf + 1]
                rden = rows[:, 2 * buf + 1: 2 * buf + 2]
                ch.wait("v", "v", st1["s_tok"][qi])
                ch.wait("v", "s")
                nm_tok = ch.mark("v", V.reduce_max(out=nmax, in_=ssl,
                                                   axis=AX.X, negate=True))
                ch.wait("s", "v", nm_tok)
                wg = st1["wsb_read"].get(buf)
                if wg is not None:
                    ch.wait("s", "t", wg)
                e_tok = ch.mark("s", S.activation(out=wsl, in_=ssl,
                                                  func=AF.Exp, bias=nmax))
                st1["exp"][qi] = e_tok
                ch.wait("v", "s", e_tok)
                m_tok = ch.mark("v", V.tensor_tensor(
                    out=w_sb[:, buf * T + (ng - 1) * 512: buf * T + ng * 512],
                    in0=w_sb[:, buf * T + (ng - 1) * 512: buf * T + ng * 512],
                    in1=a["binmask"][:, (qi % 4) * 512:(qi % 4 + 1) * 512],
                    op=ALU.mult))
                ch.wait("v", "v", m_tok)
                ch.mark("v", V.reduce_sum(out=rden, in_=wsl, axis=AX.X))
                ch.wait("v", "v")
                rd_tok = ch.mark("v", V.reciprocal(out=rden, in_=rden))
                for j0 in range(0, nj, 8):
                    jn = min(8, nj - j0)
                    ch.wait("t", "v", m_tok)
                    ch.wait("t", "s", e_tok)
                    if st1["tp1_read"] is not None:
                        ch.wait("t", "v", st1["tp1_read"])
                    for jj in range(jn):
                        i = TE.transpose(
                            out=ps_tp1[:, jj * 128:(jj + 1) * 128],
                            in_=w_sb[:, buf * T + (j0 + jj) * 128:
                                     buf * T + (j0 + jj + 1) * 128],
                            identity=c["ident_bf"][:])
                    tt = ch.mark("t", i)
                    st1["wsb_read"][buf] = tt
                    ch.wait("v", "t", tt)
                    st1["wT_read"] = ch.mark("v", V.tensor_copy(
                        out=wT_sb[:, j0 * 128:(j0 + jn) * 128],
                        in_=ps_tp1[:, 0:jn * 128]))
                    st1["tp1_read"] = st1["wT_read"]
                ch.wait("t", "v", st1["wT_read"])
                if st1["atm_read"] is not None:
                    ch.wait("t", "s", st1["atm_read"])
                for dh in range(2):
                    for j in range(nj):
                        i = TE.matmul(
                            out=ps_a[:, dh * 512:(dh + 1) * 512],
                            lhsT=wT_sb[:, j * 128:(j + 1) * 128],
                            rhs=xn_tm[:, j * 1024 + dh * 512:
                                      j * 1024 + dh * 512 + 512],
                            start=(j == 0), stop=(j == nj - 1))
                a_tok = ch.mark("t", i)
                ch.wait("s", "t", a_tok)
                ch.wait("s", "v", rd_tok)
                if st1["a_read"] is not None:
                    ch.wait("s", "t", st1["a_read"])
                i = S.activation(out=a_tm[:, 0:512], in_=ps_a[:, 0:512],
                                 func=AF.Copy, scale=rden)
                i = S.activation(out=a_tm[:, 512:1024], in_=ps_a[:, 512:1024],
                                 func=AF.Copy, scale=rden)
                st1["atm_read"] = ch.mark("s", i)
                ch.wait("t", "s", st1["atm_read"])
                if st1["tp1_read"] is not None:
                    ch.wait("t", "v", st1["tp1_read"])
                for dc in range(NCD):
                    i = TE.transpose(out=ps_tp1[:, dc * 128:(dc + 1) * 128],
                                     in_=a_tm[:, dc * 128:(dc + 1) * 128],
                                     identity=c["ident_bf"][:])
                st1["a_read"] = ch.mark("t", i)
                ch.wait("v", "t", st1["a_read"])
                dst = dataclasses.replace(
                    A_fm[:], offset=qi * 128,
                    ap=[[NCD * T, 128], [T, NCD], [1, 128]])
                st1["tp1_read"] = ch.mark("v", V.tensor_copy(out=dst,
                                                             in_=ps_tp1[:]))
            emit_scores(0)
            for qi in range(1, 16):
                emit_scores(qi)
                emit_out(qi - 1)
            emit_out(15)
            ch.barrier()
        s01ctx.close()

        l2ctx = ExitStack()
        with l2ctx:
            scr = emit_ln_scratch(p, c, l2ctx)
            xn_t = l2ctx.enter_context(
                nc.sbuf_tensor(p.name("xn_t2"), [128, NCD * 512], BF16))
            gt_sb = l2ctx.enter_context(
                nc.sbuf_tensor(p.name("gt_sb"), [128, 2 * 512], BF16))
            wvg = l2ctx.enter_context(
                nc.sbuf_tensor(p.name("wvg"), [128, 16 * 512], BF16))
            ps_wv = l2ctx.enter_context(
                nc.psum_tensor(p.name("ps_wv"), [128, 1024], F32))
            ps_gt = l2ctx.enter_context(
                nc.psum_tensor(p.name("ps_gt"), [128, 1024], F32))
            wl_wv = WLoader(p, l2ctx, "wwv", NCD * 128)
            wl_wg = WLoader(p, l2ctx, "wwg", NCD * 128)
            wl_wo = WLoader(p, l2ctx, "wwo", 8 * 128)
            ps_o = scr["ps_b"]
            gscale = [a["gln"][:, dc:dc + 1] for dc in range(NCD)]
            wv_read = {}
            gt_read = {}
            gts_read = {}
            o_read = {}
            wvg_rd = None
            xn2_read = None
            for t in range(NTT):
                if xn2_read is not None:
                    ch.wait("p", "t", xn2_read)
                xn_tok = emit_ln_tile(p, c, X, t, cview(xn_t), scr,
                                      rms_fold=True)
                step = 0

                def vg_src(vc):
                    return (aps["wv"][:, vc * 128:(vc + 1) * 128].rearrange(
                                "(c p) f -> p c f", p=128),
                            aps["wg"][:, vc * 128:(vc + 1) * 128].rearrange(
                                "(c p) f -> p c f", p=128))

                v0_, g0_ = vg_src(0)
                vnxt = (wl_wv.load(v0_, cast="s", scale_aps=gscale),
                        wl_wg.load(g0_, cast="s", scale_aps=gscale))
                for vc in range(16):
                    (sv, vtok), (sg, gtok) = vnxt
                    if vc + 1 < 16:
                        vn_, gn_ = vg_src(vc + 1)
                        vnxt = (wl_wv.load(vn_, cast="s", scale_aps=gscale),
                                wl_wg.load(gn_, cast="s", scale_aps=gscale))
                    bk = step % 2
                    wsl = ps_wv[:, bk * 512: bk * 512 + 512]
                    gsl = ps_gt[:, bk * 512: bk * 512 + 512]
                    ch.wait("t", vtok[0], vtok[1])
                    if bk in wv_read:
                        ch.wait("t", wv_read[bk][0], wv_read[bk][1])
                    for dc in range(NCD):
                        i = TE.matmul(
                            out=wsl,
                            lhsT=wl_wv.view(sv)[:, dc * 128:(dc + 1) * 128],
                            rhs=A_fm[:, dc * T + t * 512: dc * T + t * 512 + 512],
                            start=(dc == 0), stop=(dc == NCD - 1))
                    wv_tok = ch.mark("t", i)
                    ch.wait("t", gtok[0], gtok[1])
                    ch.wait("t", "p", xn_tok)
                    if bk in gt_read:
                        ch.wait("t", gt_read[bk][0], gt_read[bk][1])
                    for dc in range(NCD):
                        i = TE.matmul(
                            out=gsl,
                            lhsT=wl_wg.view(sg)[:, dc * 128:(dc + 1) * 128],
                            rhs=xn_t[:, dc * 512:(dc + 1) * 512],
                            start=(dc == 0), stop=(dc == NCD - 1))
                    gt_tok = ch.mark("t", i)
                    wl_wv.set_read(sv, "t", wv_tok)
                    wl_wg.set_read(sg, "t", gt_tok)
                    ch.wait("s", "t", gt_tok)
                    if bk in gts_read:
                        ch.wait("s", gts_read[bk][0], gts_read[bk][1])
                    st = ch.mark("s", S.activation(
                        out=gt_sb[:, bk * 512: bk * 512 + 512], in_=gsl,
                        func=AF.Silu))
                    ch.wait("v", "s", st)
                    ch.wait("v", "t", wv_tok)
                    if wvg_rd is not None:
                        ch.wait("v", "t", wvg_rd)
                    i = V.tensor_tensor(out=wvg[:, vc * 512:(vc + 1) * 512],
                                        in0=gt_sb[:, bk * 512: bk * 512 + 512],
                                        in1=wsl, op=ALU.mult)
                    fv = ch.mark("v", i)
                    wv_read[bk] = ("v", fv)
                    gt_read[bk] = ("v", fv)
                    gts_read[bk] = ("v", fv)
                    step += 1
                wvg_tok = ch.counts["v"]
                xn2_read = ch.counts["t"]
                def wo_src(idx):
                    dc_, vh_ = idx // 2, idx % 2
                    return aps["wo"][vh_ * 1024:(vh_ + 1) * 1024,
                                     dc_ * 128:(dc_ + 1) * 128].rearrange(
                        "(c p) f -> p c f", p=128)

                onxt = wl_wo.load(wo_src(0))
                for dc in range(NCD):
                    bk = dc % 2
                    osl = ps_o[:, bk * 512: bk * 512 + 512]
                    if dc < 2:
                        o_read.setdefault(bk, ("v", scr["bcast_read_tok"]))
                    for vh in range(2):
                        so, otok = onxt
                        if dc * 2 + vh + 1 < 16:
                            onxt = wl_wo.load(wo_src(dc * 2 + vh + 1))
                        ch.wait("t", otok[0], otok[1])
                        ch.wait("t", "v", wvg_tok)
                        if vh == 0 and bk in o_read:
                            ch.wait("t", o_read[bk][0], o_read[bk][1])
                        for vcl in range(8):
                            vc = vh * 8 + vcl
                            i = TE.matmul(
                                out=osl,
                                lhsT=wl_wo.view(so)[:, vcl * 128:(vcl + 1) * 128],
                                rhs=wvg[:, vc * 512:(vc + 1) * 512],
                                start=(vc == 0), stop=(vc == 15))
                        o_tok = ch.mark("t", i)
                        wl_wo.set_read(so, "t", o_tok)
                    ch.wait("v", "t", o_tok)
                    xc = X[:, dc * T + t * 512: dc * T + t * 512 + 512]
                    i = V.tensor_tensor(out=xc, in0=xc, in1=osl, op=ALU.add)
                    o_read[bk] = ("v", ch.mark("v", i))
                wvg_rd = ch.counts["t"]
                scr["bcast_read_tok"] = ch.counts["v"]
                scr["stat_read_tok"] = ch.counts["v"]
                o_read = {}
            ch.barrier()


def build_l0attn_prog(host_map):
    nc = bass.Bass("TRN2", target_bir_lowering=False)
    names = ["x", "ident_bf", "gln", "xlu", "xlv", "binmask", "sinembT",
             "wq", "wk", "wv", "wg", "wr", "wo"]
    aps = declare_inputs(nc, names, host_map)
    y = nc.dram_tensor("y", [NSEQ, D, T], F32, kind="ExternalOutput").ap()
    ctx = ExitStack()
    with ctx:
        ch = Chains(nc, ctx)
        p = Prog(nc, ctx, ch)
        c = emit_consts(p)
        c["ident_bf"] = ctx.enter_context(nc.sbuf_tensor("identb", [128, 128], BF16))
        c["consts_tok"] = [ch.dma("qs", c["ident_bf"][:], aps["ident_bf"][:])]
        X = ctx.enter_context(nc.sbuf_tensor("X", [128, NCD * T], F32))
        st_tok = None
        for seq in range(NSEQ):
            ch.barrier()
            if st_tok is not None:
                ch.wait("qs", *st_tok)
            xt = ch.dma("qs", X[:],
                        aps["x"][seq].rearrange("(c p) t -> p c t", p=128))
            for e in ("v", "s", "t", "p"):
                ch.wait(e, *xt)
            emit_l0attn(p, c, X, aps)
            ch.wait("qs", "v")
            ch.wait("qs", "p")
            st_tok = ch.dma("qs", y[seq].rearrange("(c p) t -> p c t", p=128),
                            X[:])
        nc.sync.wait_ge(ch.sems[st_tok[0]], st_tok[1])
    return nc



def emit_l1attn_setup(p, c, aps, sctx):
    nc, ch = p.nc, p.ch
    sb = lambda name, shape, dt: sctx.enter_context(
        nc.sbuf_tensor(p.name(name), shape, dt))
    a = {}
    for nm in ("bq", "bk", "bv"):
        a[nm] = sb(nm, [128, NCD], F32)
        tk = ch.dma("qs", a[nm][:], aps["l1_" + nm][:])
        ch.wait("s", *tk)
    a["carry"] = sb("carry", [128, 16], F32)
    a["zeros"] = sb("zeros", [128, 512], BF16)
    a["eps8"] = sb("eps8", [128, 1], F32)
    nc.vector.memset(a["zeros"][:], 0.0)
    ch.mark("v", nc.vector.memset(a["eps8"][:], 1e-8))
    ch.wait("s", "v")
    return a


def emit_l1attn(p, c, X, aps):
    nc, ch = p.nc, p.ch
    V, S, TE, P = nc.vector, nc.scalar, nc.tensor, nc.gpsimd
    KSEG = NKC * 512

    sctx = ExitStack()
    with sctx:
        sb = lambda name, shape, dt: sctx.enter_context(
            nc.sbuf_tensor(p.name(name), shape, dt))
        a = emit_l1attn_setup(p, c, aps, sctx)
        tb = sb("dft_tb", [128, 10240], BF16)
        cf = tb[:, 0:NCD * NFREQ]
        sf = tb[:, NCD * NFREQ:2 * NCD * NFREQ]
        ci = tb[:, 0:NKC * 1024]
        si = tb[:, NKC * 1024:2 * NKC * 1024]
        qkv = sb("qkv", [128, 3 * NCD * 512], BF16)
        wl_p = WLoader(p, sctx, "wp", NCD * 128)
        scr = emit_ln_scratch(p, c, sctx, sep_stat=False)
        xn_t = sb("xn_t1a", [128, NCD * 512], BF16)
        G = [sb(f"G{i_}", [128, 2 * KSEG], BF16) for i_ in range(3)]
        m2 = sb("m2", [128, KSEG], F32)
        tt2 = sb("tt2", [128, 1024], F32)
        sq_scr = tt2[:, 512:1024]
        cs = sb("cs", [128, 1], F32)
        ps_p = sctx.enter_context(nc.psum_tensor(p.name("ps_p"), [128, 1024], F32))
        ps_f = sctx.enter_context(nc.psum_tensor(p.name("ps_f"), [128, 2048], F32))

        sigs = (("l1_wqT", a["bq"]), ("l1_wkT", a["bk"]), ("l1_wvT", a["bv"]))

        def pj_src(idx):
            return aps[sigs[idx // NCD][0]][
                :, (idx % NCD) * 128:(idx % NCD + 1) * 128].rearrange(
                "(c p) f -> p c f", p=128)

        st = {"p_read": {}, "f_read": {}, "xn_read": None, "qkv_read": None,
              "proj_tok": None, "inv_tok": None, "z_tok": None, "pnxt": None}

        def emit_lnproj(t):
            if st["xn_read"] is not None:
                ch.wait("p", "t", st["xn_read"])
            xn_tok = emit_ln_tile(p, c, X, t, cview(xn_t), scr)
            if st["pnxt"] is None:
                st["pnxt"] = wl_p.load(pj_src(0))
            step = 0
            for sig, (wnm, bias) in enumerate(sigs):
                for mc in range(NCD):
                    sw, wtok = st["pnxt"]
                    nidx = sig * NCD + mc + 1
                    st["pnxt"] = wl_p.load(pj_src(nidx % (3 * NCD)))
                    bk = step % 2
                    sl = ps_p[:, bk * 512: bk * 512 + 512]
                    ch.wait("t", wtok[0], wtok[1])
                    ch.wait("t", "p", xn_tok)
                    pr = st["p_read"].get(bk)
                    if pr is not None:
                        ch.wait("t", pr[0], pr[1])
                    for dc in range(NCD):
                        i = TE.matmul(
                            out=sl,
                            lhsT=wl_p.view(sw)[:, dc * 128:(dc + 1) * 128],
                            rhs=xn_t[:, dc * 512:(dc + 1) * 512],
                            start=(dc == 0), stop=(dc == NCD - 1))
                    mt = ch.mark("t", i)
                    wl_p.set_read(sw, "t", mt)
                    ch.wait("s", "t", mt)
                    if st["qkv_read"] is not None:
                        ch.wait("s", "t", st["qkv_read"])
                    i = S.activation(
                        out=qkv[:, sig * NCD * 512 + mc * 512:
                                sig * NCD * 512 + (mc + 1) * 512],
                        in_=sl, func=AF.Identity, bias=bias[:, mc:mc + 1])
                    st["p_read"][bk] = ("s", ch.mark("s", i))
                    step += 1
            st["proj_tok"] = ch.counts["s"]
            st["xn_read"] = ch.counts["t"]

        def emit_fwd(t):
            ch.wait("qs", "t")
            ttok1 = ch.dma("qs", cf, aps["dft_c"].rearrange(
                "(c p) k -> p c k", p=128))
            ttok2 = ch.dma("qs", sf, aps["dft_s"].rearrange(
                "(c p) k -> p c k", p=128))
            ch.wait("t", *ttok1)
            ch.wait("t", *ttok2)
            ch.wait("t", "s", st["proj_tok"])
            if st["inv_tok"] is not None:
                ch.wait("v", "t", st["inv_tok"])
            for sig in range(3):
                sbase = sig * NCD * 512
                for kc in range(NKC):
                    bk = kc % 2
                    pre = ps_f[:, bk * 1024: bk * 1024 + 512]
                    pim = ps_f[:, bk * 1024 + 512: bk * 1024 + 1024]
                    for bsl, tbl in ((pre, cf), (pim, sf)):
                        fr = st["f_read"].get(bk)
                        if fr is not None:
                            ch.wait("t", fr[0], fr[1])
                        for dc in range(NCD):
                            i = TE.matmul(
                                out=bsl,
                                lhsT=tbl[:, dc * NFREQ + kc * 128:
                                         dc * NFREQ + (kc + 1) * 128],
                                rhs=qkv[:, sbase + dc * 512:
                                        sbase + (dc + 1) * 512],
                                start=(dc == 0), stop=(dc == NCD - 1))
                    ft = ch.mark("t", i)
                    ch.wait("v", "t", ft)
                    V.tensor_copy(out=G[sig][:, kc * 512:(kc + 1) * 512],
                                  in_=pre)
                    ev = ch.mark("v", V.tensor_copy(
                        out=G[sig][:, KSEG + kc * 512: KSEG + (kc + 1) * 512],
                        in_=pim))
                    ch.wait("s", "t", ft)
                    ch.wait("s", "v")
                    S.activation(out=m2[:, kc * 512:(kc + 1) * 512],
                                 in_=pre, func=AF.Square)
                    stok = ch.mark("s", S.activation(out=sq_scr,
                                                     in_=pim, func=AF.Square))
                    ch.wait("v", "s", stok)
                    mv = ch.mark("v", V.tensor_tensor(
                        out=m2[:, kc * 512:(kc + 1) * 512],
                        in0=m2[:, kc * 512:(kc + 1) * 512],
                        in1=sq_scr, op=ALU.add))
                    st["f_read"][bk] = ("v", mv)
                ch.wait("s", "v")
                stok = ch.mark("s", S.activation(out=m2[:], in_=m2[:],
                                                 func=AF.Sqrt))
                ch.wait("s", "s", stok)
                stok = ch.mark("s", S.activation(out=m2[:], in_=m2[:],
                                                 func=AF.Ln, bias=a["eps8"][:]))
                ch.wait("s", "s", stok)
                stok = ch.mark("s", S.activation(out=m2[:], in_=m2[:],
                                                 func=AF.Exp, scale=-1.0))
                ch.wait("v", "s", stok)
                V.tensor_tensor(out=G[sig][:, 0:KSEG], in0=G[sig][:, 0:KSEG],
                                in1=m2[:], op=ALU.mult)
                gt = ch.mark("v", V.tensor_tensor(
                    out=G[sig][:, KSEG:2 * KSEG],
                    in0=G[sig][:, KSEG:2 * KSEG], in1=m2[:], op=ALU.mult))
                ch.wait("s", "v", gt)

        def emit_vphase(t):
            ch.wait("v", "v")
            for kc in range(NKC):
                ksl = slice(kc * 512, (kc + 1) * 512)
                isl = slice(KSEG + kc * 512, KSEG + (kc + 1) * 512)
                kr, ki_ = G[1][:, ksl], G[1][:, isl]
                vr, vi = G[2][:, ksl], G[2][:, isl]
                V.tensor_tensor(out=tt2[:, 0:512], in0=kr, in1=vr, op=ALU.mult)
                i = V.tensor_tensor(out=tt2[:, 512:1024], in0=kr, in1=vi,
                                    op=ALU.mult)
                ch.wait("v", "v", ch.mark("v", i))
                i = V.tensor_tensor(out=kr, in0=ki_, in1=vi, op=ALU.mult)
                ch.wait("v", "v", ch.mark("v", i))
                i = V.tensor_tensor(out=vr, in0=ki_, in1=vr, op=ALU.mult)
                ch.wait("v", "v", ch.mark("v", i))
                i = V.tensor_tensor(out=kr, in0=tt2[:, 0:512], in1=kr,
                                    op=ALU.subtract)
                ch.wait("v", "v", ch.mark("v", i))
                i = V.tensor_tensor(out=ki_, in0=tt2[:, 512:1024], in1=vr,
                                    op=ALU.add)
                ch.wait("v", "v", ch.mark("v", i))
            for idx in range(2 * NKC):
                seg = slice(idx * 512, (idx + 1) * 512)
                init = 0.0 if t == 0 else a["carry"][:, idx:idx + 1]
                i = V.reduce_sum(out=cs[:], in_=G[1][:, seg], axis=AX.X)
                ch.wait("v", "v", ch.mark("v", i))
                i = V.tensor_tensor_scan(
                    out=G[1][:, seg], data0=a["zeros"][:], data1=G[1][:, seg],
                    initial=init, op0=ALU.add, op1=ALU.add)
                ch.wait("v", "v", ch.mark("v", i))
                if t == 0:
                    i = V.tensor_copy(out=a["carry"][:, idx:idx + 1], in_=cs[:])
                else:
                    i = V.tensor_tensor(out=a["carry"][:, idx:idx + 1],
                                        in0=a["carry"][:, idx:idx + 1],
                                        in1=cs[:], op=ALU.add)
                ch.wait("v", "v", ch.mark("v", i))
            for kc in range(NKC):
                ksl = slice(kc * 512, (kc + 1) * 512)
                isl = slice(KSEG + kc * 512, KSEG + (kc + 1) * 512)
                qr, qi_ = G[0][:, ksl], G[0][:, isl]
                sre, sim = G[1][:, ksl], G[1][:, isl]
                V.tensor_tensor(out=tt2[:, 0:512], in0=sre, in1=qr, op=ALU.mult)
                i = V.tensor_tensor(out=tt2[:, 512:1024], in0=sim, in1=qr,
                                    op=ALU.mult)
                ch.wait("v", "v", ch.mark("v", i))
                i = V.tensor_tensor(out=qr, in0=sre, in1=qi_, op=ALU.mult)
                ch.wait("v", "v", ch.mark("v", i))
                i = V.tensor_tensor(out=qi_, in0=sim, in1=qi_, op=ALU.mult)
                ch.wait("v", "v", ch.mark("v", i))
                i = V.tensor_tensor(out=G[2][:, ksl], in0=tt2[:, 0:512],
                                    in1=qi_, op=ALU.add)
                ch.wait("v", "v", ch.mark("v", i))
                i = V.tensor_tensor(out=G[2][:, isl], in0=tt2[:, 512:1024],
                                    in1=qr, op=ALU.subtract)
                ch.wait("v", "v", ch.mark("v", i))
            st["z_tok"] = ch.counts["v"]

        def emit_inv(t):
            ch.wait("qs", "t")
            tti1 = ch.dma("qs", ci, aps["idft_c"].rearrange(
                "(c p) n -> p c n", p=128))
            tti2 = ch.dma("qs", si, aps["idft_s"].rearrange(
                "(c p) n -> p c n", p=128))
            ch.wait("t", *tti1)
            ch.wait("t", *tti2)
            ch.wait("t", "v", st["z_tok"])
            o_read = {}
            for nch in range(NCD):
                bk = nch % 2
                osl = ps_f[:, bk * 512: bk * 512 + 512]
                prev = o_read.get(bk) or st["f_read"].get(bk)
                if prev is not None:
                    ch.wait("t", prev[0], prev[1])
                first = True
                for kc in range(NKC):
                    ksl = slice(kc * 512, (kc + 1) * 512)
                    isl = slice(KSEG + kc * 512, KSEG + (kc + 1) * 512)
                    TE.matmul(out=osl,
                              lhsT=ci[:, kc * 1024 + nch * 128:
                                      kc * 1024 + (nch + 1) * 128],
                              rhs=G[2][:, ksl], start=first, stop=False)
                    first = False
                    i = TE.matmul(out=osl,
                                  lhsT=si[:, kc * 1024 + nch * 128:
                                          kc * 1024 + (nch + 1) * 128],
                                  rhs=G[2][:, isl], start=False,
                                  stop=(kc == NKC - 1))
                ot = ch.mark("t", i)
                ch.wait("v", "t", ot)
                xc = X[:, nch * T + t * 512: nch * T + t * 512 + 512]
                i = V.tensor_tensor(out=xc, in0=xc, in1=osl, op=ALU.add)
                o_read[bk] = ("v", ch.mark("v", i))
            mx = ("v", max(o_read[0][1], o_read[1][1]))
            st["f_read"][0] = st["f_read"][1] = mx
            st["inv_tok"] = ch.counts["t"]

        emit_lnproj(0)
        for t in range(NTT):
            emit_fwd(t)
            if t + 1 < NTT:
                emit_lnproj(t + 1)
            emit_vphase(t)
            emit_inv(t)
        ch.barrier()


def build_l1attn_prog(host_map):
    nc = bass.Bass("TRN2", target_bir_lowering=False)
    names = ["x", "l1_bq", "l1_bk", "l1_bv", "l1_wqT", "l1_wkT", "l1_wvT",
             "dft_c", "dft_s", "idft_c", "idft_s"]
    aps = declare_inputs(nc, names, host_map)
    y = nc.dram_tensor("y", [NSEQ, D, T], F32, kind="ExternalOutput").ap()
    ctx = ExitStack()
    with ctx:
        ch = Chains(nc, ctx)
        p = Prog(nc, ctx, ch)
        c = emit_consts(p)
        X = ctx.enter_context(nc.sbuf_tensor("X", [128, NCD * T], F32))
        st_tok = None
        for seq in range(NSEQ):
            ch.barrier()
            if st_tok is not None:
                ch.wait("qs", *st_tok)
            xt = ch.dma("qs", X[:],
                        aps["x"][seq].rearrange("(c p) t -> p c t", p=128))
            for e in ("v", "s", "t", "p"):
                ch.wait(e, *xt)
            emit_l1attn(p, c, X, aps)
            ch.wait("qs", "v")
            ch.wait("qs", "p")
            st_tok = ch.dma("qs", y[seq].rearrange("(c p) t -> p c t", p=128),
                            X[:])
        nc.sync.wait_ge(ch.sems[st_tok[0]], st_tok[1])
    return nc



def _bf16(a):
    return np.ascontiguousarray(a.astype(ml_dtypes.bfloat16))


def _f32(a):
    return np.ascontiguousarray(a.astype(np.float32))


def host_consts():
    c = {}
    c["ident_bf"] = _bf16(np.eye(128))
    c["ident_f"] = _f32(np.eye(128))
    for nm, E in (("selc8", 8), ("selc16", 16)):
        sel = np.zeros((E, E * 128), np.float32)
        for e in range(E):
            sel[e, e * 128:(e + 1) * 128] = 1.0
        c[nm] = _bf16(sel)
    pos = np.arange(T, dtype=np.float64)
    inv = 1.0 / (10000.0 ** (np.arange(0, D, 2, dtype=np.float64) / D))
    pre = pos[:, None] * inv[None, :]
    cat = np.concatenate([np.sin(pre), np.cos(pre)], axis=-1)
    sinemb = cat[::-1]
    c["sinembT"] = _bf16(sinemb.T)
    m = np.zeros((4, 128, 512), np.float32)
    for r in range(4):
        for i in range(128):
            m[r, i, :r * 128 + i + 1] = 1.0
    c["binmask"] = _bf16(m.reshape(4 * 128, 512))
    N = D
    n = np.arange(N, dtype=np.float64)
    k = np.arange(NFREQ, dtype=np.float64)
    ang = 2.0 * np.pi * np.outer(n, k) / N
    c["dft_c"] = _bf16(np.cos(ang))
    c["dft_s"] = _bf16(-np.sin(ang))
    wk = np.zeros(NFREQ)
    wk[0] = 1.0
    wk[1:512] = 2.0
    wk[512] = 1.0
    angi = 2.0 * np.pi * np.outer(k, n) / N
    c["idft_c"] = _bf16((wk[:, None] / N) * np.cos(angi))
    c["idft_s"] = _bf16(-(wk[:, None] / N) * np.sin(angi))
    return c


def shard_inputs(inputs):
    x = np.asarray(inputs["L0_attn_x"])
    cores = []
    consts = host_consts()
    w = {}
    w["rtr"] = _f32(inputs["L0_ffn_router_kernel_DE"])
    w["l0_gk"] = _f32(inputs["L0_ffn_kernel_gating_EDF"])
    w["l0_uk"] = _f32(inputs["L0_ffn_kernel_up_EDF"])
    w["l0_dk"] = _f32(inputs["L0_ffn_kernel_down_EFD"])
    w["gln"] = _f32(np.asarray(inputs["L0_attn_g_ln"]).reshape(NCD, 128).T)
    w["wq"] = _f32(inputs["L0_attn_W_q"])
    w["wk"] = _f32(inputs["L0_attn_W_k"])
    w["wv"] = _f32(inputs["L0_attn_W_v"])
    w["wg"] = _f32(inputs["L0_attn_W_g"])
    w["wr"] = _f32(inputs["L0_attn_W_r"])
    w["xlu"] = _f32(np.asarray(inputs["L0_attn_xl_u"]).reshape(128, 1))
    w["xlv"] = _f32(np.asarray(inputs["L0_attn_xl_v"]).reshape(128, 1))
    w["wo"] = _f32(inputs["L0_attn_W_o"])
    w["l1_wqT"] = _f32(np.asarray(inputs["L1_attn_Wq"]).T)
    w["l1_wkT"] = _f32(np.asarray(inputs["L1_attn_Wk"]).T)
    w["l1_wvT"] = _f32(np.asarray(inputs["L1_attn_Wv"]).T)
    w["l1_bq"] = _f32(np.asarray(inputs["L1_attn_bq"]).reshape(NCD, 128).T)
    w["l1_bk"] = _f32(np.asarray(inputs["L1_attn_bk"]).reshape(NCD, 128).T)
    w["l1_bv"] = _f32(np.asarray(inputs["L1_attn_bv"]).reshape(NCD, 128).T)
    w["l1_gwT"] = _f32(np.asarray(inputs["L1_ffn_gate_w"]).T)
    w["l1_ebias"] = _f32(np.asarray(inputs["L1_ffn_expert_bias"]).reshape(1, 16))
    w["l1_gk"] = _f32(np.transpose(np.asarray(inputs["L1_ffn_w_gate"]), (0, 2, 1)))
    w["l1_uk"] = _f32(np.transpose(np.asarray(inputs["L1_ffn_w_up"]), (0, 2, 1)))
    w["l1_dk"] = _f32(np.transpose(np.asarray(inputs["L1_ffn_w_down"]), (0, 2, 1)))
    w["l1_sg"] = _f32(np.asarray(inputs["L1_ffn_shared_gate_w"]).T)
    w["l1_su"] = _f32(np.asarray(inputs["L1_ffn_shared_up_w"]).T)
    w["l1_sd"] = _f32(np.asarray(inputs["L1_ffn_shared_down_w"]).T)
    for ci in range(NCORE):
        xs = x[ci * NSEQ:(ci + 1) * NSEQ]
        x_fm = np.ascontiguousarray(xs.transpose(0, 2, 1))
        m = {"x": _f32(x_fm)}
        m.update(consts)
        m.update(w)
        cores.append(m)
    return cores


def unshard_output(results):
    outs = []
    for r in results:
        y = r["y"]
        outs.append(np.ascontiguousarray(np.transpose(y, (0, 2, 1))))
    return np.concatenate(outs, axis=0).astype(np.float32)



def declare_inputs(nc, names, host_map):
    aps = {}
    for nm in names:
        a = host_map[nm]
        dt = BF16 if a.dtype == ml_dtypes.bfloat16 else F32
        aps[nm] = nc.dram_tensor(nm, list(a.shape), dt, kind="ExternalInput").ap()
    return aps


def build_ffn_prog(layer, host_map):
    nc = bass.Bass("TRN2", target_bir_lowering=False)
    if layer == "l0":
        wnames = ["rtr", "l0_gk", "l0_uk", "l0_dk"]
    else:
        wnames = ["l1_gwT", "l1_ebias", "l1_gk", "l1_uk", "l1_dk",
                  "l1_sg", "l1_su", "l1_sd"]
    names = ["x", "ident_bf", "ident_f", "selc8", "selc16"] + wnames
    aps = declare_inputs(nc, names, host_map)
    y = nc.dram_tensor("y", [NSEQ, D, T], F32, kind="ExternalOutput").ap()

    ctx = ExitStack()
    with ctx:
        ch = Chains(nc, ctx)
        p = Prog(nc, ctx, ch)
        c = emit_consts(p)
        c["ident_bf"] = ctx.enter_context(nc.sbuf_tensor("identb", [128, 128], BF16))
        ctoks = [ch.dma("qs", c["ident_bf"][:], aps["ident_bf"][:])]
        c["ident_f"] = ctx.enter_context(nc.sbuf_tensor("identf", [128, 128], F32))
        ctoks.append(ch.dma("qs", c["ident_f"][:], aps["ident_f"][:]))
        if layer == "l0":
            wd = {"gk": aps["l0_gk"], "uk": aps["l0_uk"], "dk": aps["l0_dk"],
                  "selc": aps["selc8"], "rw": aps["rtr"]}
        else:
            wd = {"gk": aps["l1_gk"], "uk": aps["l1_uk"], "dk": aps["l1_dk"],
                  "sg": aps["l1_sg"], "su": aps["l1_su"], "sd": aps["l1_sd"],
                  "selc": aps["selc16"], "rw": aps["l1_gwT"],
                  "ebias": aps["l1_ebias"]}

        X = ctx.enter_context(nc.sbuf_tensor("X", [128, NCD * T], F32))
        c["consts_tok"] = ctoks
        st_tok = None
        for seq in range(NSEQ):
            ch.barrier()
            if st_tok is not None:
                ch.wait("qs", *st_tok)
            xt = ch.dma("qs", X[:],
                        aps["x"][seq].rearrange("(c p) t -> p c t", p=128))
            ch.wait("v", *xt)
            ch.wait("s", *xt)
            ch.wait("t", *xt)
            ch.wait("p", *xt)
            emit_ffn(p, c, X, wd, layer)
            ch.wait("qs", "p")
            ch.wait("qs", "v")
            st_tok = ch.dma("qs", y[seq].rearrange("(c p) t -> p c t", p=128),
                            X[:])
        nc.sync.wait_ge(ch.sems[st_tok[0]], st_tok[1])
    return nc



def build_full_prog(host_map):
    nc = bass.Bass("TRN2", target_bir_lowering=False)
    names = sorted(set(
        ["x", "ident_bf", "ident_f", "selc8", "selc16", "rtr", "l0_gk",
         "l0_uk", "l0_dk",
         "gln", "xlu", "xlv", "binmask", "sinembT", "wq", "wk", "wv", "wg",
         "wr", "wo", "l1_gwT", "l1_ebias", "l1_gk", "l1_uk", "l1_dk", "l1_sg",
         "l1_su", "l1_sd", "l1_bq", "l1_bk", "l1_bv", "l1_wqT", "l1_wkT",
         "l1_wvT", "dft_c", "dft_s", "idft_c", "idft_s"]))
    aps = declare_inputs(nc, names, host_map)
    y = nc.dram_tensor("y", [NSEQ, D, T], F32, kind="ExternalOutput").ap()

    ctx = ExitStack()
    with ctx:
        ch = Chains(nc, ctx)
        p = Prog(nc, ctx, ch)
        c = emit_consts(p)
        c["ident_bf"] = ctx.enter_context(nc.sbuf_tensor("identb", [128, 128], BF16))
        ctoks = [ch.dma("qs", c["ident_bf"][:], aps["ident_bf"][:])]
        c["ident_f"] = ctx.enter_context(nc.sbuf_tensor("identf", [128, 128], F32))
        ctoks.append(ch.dma("qs", c["ident_f"][:], aps["ident_f"][:]))
        c["consts_tok"] = ctoks
        wd0 = {"gk": aps["l0_gk"], "uk": aps["l0_uk"], "dk": aps["l0_dk"],
               "selc": aps["selc8"], "rw": aps["rtr"]}
        wd1 = {"gk": aps["l1_gk"], "uk": aps["l1_uk"], "dk": aps["l1_dk"],
               "sg": aps["l1_sg"], "su": aps["l1_su"], "sd": aps["l1_sd"],
               "selc": aps["selc16"], "rw": aps["l1_gwT"],
               "ebias": aps["l1_ebias"]}

        X = ctx.enter_context(nc.sbuf_tensor("X", [128, NCD * T], F32))
        st_tok = None
        for seq in range(NSEQ):
            ch.barrier()
            if st_tok is not None:
                ch.wait("qs", *st_tok)
            xt = ch.dma("qs", X[:],
                        aps["x"][seq].rearrange("(c p) t -> p c t", p=128))
            for e in ("v", "s", "t", "p"):
                ch.wait(e, *xt)
            emit_ffn(p, c, X, wd0, "l0")
            emit_l0attn(p, c, X, aps)
            emit_ffn(p, c, X, wd0, "l0")
            emit_ffn(p, c, X, wd1, "l1")
            emit_l1attn(p, c, X, aps)
            emit_ffn(p, c, X, wd1, "l1")
            ch.wait("qs", "v")
            ch.wait("qs", "p")
            st_tok = ch.dma("qs", y[seq].rearrange("(c p) t -> p c t", p=128),
                            X[:])
        nc.sync.wait_ge(ch.sems[st_tok[0]], st_tok[1])
    return nc



FFN_L0_NAMES = ["x", "ident_bf", "ident_f", "selc8", "selc16", "rtr",
                "l0_gk", "l0_uk", "l0_dk"]
FFN_L1_NAMES = ["x", "ident_bf", "ident_f", "selc8", "selc16", "l1_gwT",
                "l1_ebias", "l1_gk", "l1_uk", "l1_dk", "l1_sg", "l1_su",
                "l1_sd"]
ATTN_L0_NAMES = ["x", "ident_bf", "gln", "xlu", "xlv", "binmask", "sinembT",
                 "wq", "wk", "wv", "wg", "wr", "wo"]
ATTN_L1_NAMES = ["x", "l1_bq", "l1_bk", "l1_bv", "l1_wqT", "l1_wkT", "l1_wvT",
                 "dft_c", "dft_s", "idft_c", "idft_s"]

_PROGS = {}


def _get_prog(kind, hm):
    if kind in _PROGS:
        return _PROGS[kind]
    if kind == "ffn_l0":
        nc, names = build_ffn_prog("l0", hm), FFN_L0_NAMES
    elif kind == "ffn_l1":
        nc, names = build_ffn_prog("l1", hm), FFN_L1_NAMES
    elif kind == "attn_l0":
        nc, names = build_l0attn_prog(hm), ATTN_L0_NAMES
    else:
        nc, names = build_l1attn_prog(hm), ATTN_L1_NAMES
    _PROGS[kind] = (nc, names)
    return _PROGS[kind]


FULL_NAMES = sorted(set(
    ["x", "ident_bf", "ident_f", "selc8", "selc16", "rtr", "l0_gk",
     "l0_uk", "l0_dk",
     "gln", "xlu", "xlv", "binmask", "sinembT", "wq", "wk", "wv", "wg",
     "wr", "wo", "l1_gwT", "l1_ebias", "l1_gk", "l1_uk", "l1_dk", "l1_sg",
     "l1_su", "l1_sd", "l1_bq", "l1_bk", "l1_bv", "l1_wqT", "l1_wkT",
     "l1_wvT", "dft_c", "dft_s", "idft_c", "idft_s"]))

FUSED = True


def kernel(**inputs):
    cores = shard_inputs(inputs)
    if FUSED:
        if "full" not in _PROGS:
            _PROGS["full"] = (build_full_prog(cores[0]), FULL_NAMES)
        nc, names = _PROGS["full"]
        in_maps = [{nm: cores[ci][nm] for nm in names} for ci in range(NCORE)]
        res = run_bass_kernel_spmd(nc, in_maps, core_ids=list(range(NCORE)))
        return unshard_output([{"y": np.asarray(r["y"])} for r in res.results])
    xs = [m["x"] for m in cores]
    seqplan = ["ffn_l0", "attn_l0", "ffn_l0", "ffn_l1", "attn_l1", "ffn_l1"]
    for kind in seqplan:
        nc, names = _get_prog(kind, cores[0])
        in_maps = []
        for ci in range(NCORE):
            m = {nm: cores[ci][nm] for nm in names if nm != "x"}
            m["x"] = xs[ci]
            in_maps.append(m)
        res = run_bass_kernel_spmd(nc, in_maps, core_ids=list(range(NCORE)))
        xs = [np.asarray(r["y"]) for r in res.results]
    return unshard_output([{"y": x} for x in xs])


# version 07
# speedup vs baseline: 1.0410x; 1.0410x over previous
import numpy as np
import ml_dtypes
from contextlib import ExitStack
import dataclasses

import concourse.bass as bass
import concourse.mybir as mybir
from concourse.bass_utils import run_bass_kernel_spmd

F32 = mybir.dt.float32
BF16 = mybir.dt.bfloat16
U32 = mybir.dt.uint32
AF = mybir.ActivationFunctionType
ALU = mybir.AluOpType
AX = mybir.AxisListType

D = 1024
NCD = 8
T = 2048
NSEQ = 2
TT = 512
NTT = T // TT
NCORE = 8

LN_EPS = 1e-5
RMS_EPS = 1e-6
L0_E, L0_F, L0_FC = 8, 2048, 16
L1_E, L1_F, L1_FC = 16, 1024, 8
NFREQ = 640
NKC = 5



NQ = 8


class Chains:

    def __init__(self, nc, ctx):
        self.nc = nc
        self.sems = {}
        self.counts = {}
        self.waited = {}
        self.dma_rr = 0
        names = ["v", "s", "t", "p"] + [f"q{i}" for i in range(NQ)]
        for name in names:
            self.sems[name] = ctx.enter_context(nc.semaphore("ch_" + name))
            self.counts[name] = 0

    def eng(self, chn):
        return {"v": self.nc.vector, "s": self.nc.scalar, "t": self.nc.tensor,
                "p": self.nc.gpsimd, "qs": self.nc.sync, "qc": self.nc.scalar}[chn]

    def mark(self, chn, inst, n=1):
        inst.then_inc(self.sems[chn], n)
        self.counts[chn] += n
        return self.counts[chn]

    def wait(self, waiter, producer, tok=None):
        if tok is None:
            tok = self.counts[producer]
        if tok <= 0:
            return
        key = (waiter, producer)
        if self.waited.get(key, -1) >= tok:
            return
        self.eng(waiter).wait_ge(self.sems[producer], tok)
        self.waited[key] = tok

    def dma(self, issuer, out, in_, **kw):
        name = f"q{self.dma_rr}"
        self.dma_rr = (self.dma_rr + 1) % NQ
        self.wait(issuer, name)
        inst = self.eng(issuer).dma_start(out=out, in_=in_, **kw)
        inst.then_inc(self.sems[name], 16)
        self.counts[name] += 16
        return (name, self.counts[name])

    def barrier(self):
        chans = ["v", "s", "t", "p"]
        qs = [f"q{i}" for i in range(NQ)]
        for w in chans + ["qs"]:
            for pr in chans + qs:
                self.wait(w, pr)



class Prog:

    def __init__(self, nc, ctx, ch):
        self.nc = nc
        self.ctx = ctx
        self.ch = ch
        self.sb = lambda name, shape, dt: ctx.enter_context(
            nc.sbuf_tensor(name, shape, dt))
        self.uid = 0

    def name(self, base):
        self.uid += 1
        return f"{base}_{self.uid}"


def diag_ap(base_ap, tensor_cols, offset, nrows, ncols):
    return dataclasses.replace(
        base_ap, offset=offset, ap=[[tensor_cols - 1, nrows], [1, ncols]])



def emit_consts(p):
    nc, ch = p.nc, p.ch
    c = {}
    c["ones_col_bf"] = p.sb("ones_col_bf", [128, 1], BF16)
    c["ones_row_f"] = p.sb("ones_row_f", [1, 128], F32)
    c["eps_ln"] = p.sb("eps_ln", [1, 1], F32)
    c["eps_rms"] = p.sb("eps_rms", [1, 1], F32)
    nc.vector.memset(c["ones_col_bf"][:], 1.0)
    nc.vector.memset(c["ones_row_f"][:], 1.0)
    nc.vector.memset(c["eps_ln"][:], LN_EPS)
    i = nc.vector.memset(c["eps_rms"][:], RMS_EPS)
    ch.mark("v", i)
    return c



def xview3(X, t):
    return dataclasses.replace(
        X[:], offset=t * TT, ap=[[NCD * T, 128], [T, NCD], [1, TT]])


def _half(v, h):
    return dataclasses.replace(
        v, offset=v.offset + h * (NCD // 2) * v.ap[1][0],
        ap=[v.ap[0], [v.ap[1][0], NCD // 2], v.ap[2]])


def cview(tile):
    return dataclasses.replace(
        tile[:], ap=[[NCD * 512, 128], [512, NCD], [1, 512]])


def bview(tile, n):
    return dataclasses.replace(tile[:], ap=[tile[:].ap[0], [0, n], [1, TT]])


def emit_ln_tile(p, c, X, t, xn_view, scr, rms_fold=False):
    nc, ch = p.nc, p.ch
    V, S, TE, P = nc.vector, nc.scalar, nc.tensor, nc.gpsimd
    xb, xsq, ps_stat, ps_b, ab_sb, bb_sb, t1 = (
        scr["xb"], scr["xsq"], scr["ps_stat"], scr["ps_b"],
        scr["ab_sb"], scr["bb_sb"], scr["t1"])
    rb = scr["rowbuf"]
    s1, s2 = rb[0:1, 0:TT], rb[0:1, TT:2 * TT]
    aa, bb = rb[0:1, 2 * TT:3 * TT], rb[0:1, 3 * TT:4 * TT]

    if scr.get("reuse_tok_t") is not None:
        ch.wait("p", "t", scr["reuse_tok_t"])
        ch.wait("s", "t", scr["reuse_tok_t"])
    cast_tok = ch.mark("p", P.tensor_copy(out=xb[:], in_=xview3(X, t)))
    sq_tok = ch.mark("s", S.activation(out=xsq[:], in_=xview3(X, t),
                                       func=AF.Square))

    ch.wait("t", "p", cast_tok)
    ch.wait("t", "s", sq_tok)
    if scr.get("stat_read_tok") is not None:
        ch.wait("t", "v", scr["stat_read_tok"])
    for cc in range(NCD):
        TE.matmul(out=ps_stat[0:1, 0:TT], lhsT=c["ones_col_bf"][:],
                  rhs=xb[:, cc * TT:(cc + 1) * TT],
                  start=(cc == 0), stop=(cc == NCD - 1))
    for cc in range(NCD):
        i = TE.matmul(out=ps_stat[64:65, 0:TT], lhsT=c["ones_col_bf"][:],
                      rhs=xsq[:, cc * TT:(cc + 1) * TT],
                      start=(cc == 0), stop=(cc == NCD - 1),
                      skip_group_check=True)
    stat_tok = ch.mark("t", i)

    ch.wait("v", "t", stat_tok)
    ch.mark("v", V.tensor_scalar(out=s1, in0=ps_stat[0:1, 0:TT],
                                 scalar1=1.0 / D, scalar2=None, op0=ALU.mult))
    srd = ch.mark("v", V.tensor_scalar(out=s2, in0=ps_stat[64:65, 0:TT],
                                       scalar1=1.0 / D, scalar2=None,
                                       op0=ALU.mult))
    scr["stat_read_tok"] = srd
    ch.wait("v", "v")
    ch.mark("v", V.tensor_tensor(out=bb, in0=s1, in1=s1, op=ALU.mult))
    ch.wait("v", "v")
    var_tok = ch.mark("v", V.tensor_tensor(out=s2, in0=s2, in1=bb, op=ALU.subtract))
    ch.wait("s", "v", var_tok)
    lt = ch.mark("s", S.activation(out=aa, in_=s2, func=AF.Ln, bias=c["eps_ln"][:]))
    ch.wait("s", "s", lt)
    sq2_tok = ch.mark("s", S.activation(out=aa, in_=aa, func=AF.Exp, scale=-0.5))
    ch.wait("v", "s", sq2_tok)
    if rms_fold:
        ch.mark("v", V.tensor_tensor(out=s2, in0=s2, in1=aa, op=ALU.mult))
        ch.wait("v", "v")
        rt = ch.mark("v", V.tensor_tensor(out=s2, in0=s2, in1=aa, op=ALU.mult))
        ch.wait("s", "v", rt)
        lt = ch.mark("s", S.activation(out=s2, in_=s2, func=AF.Ln,
                                       bias=c["eps_rms"][:]))
        ch.wait("s", "s", lt)
        st = ch.mark("s", S.activation(out=s2, in_=s2, func=AF.Exp, scale=-0.5))
        ch.wait("v", "s", st)
        ch.mark("v", V.tensor_tensor(out=aa, in0=aa, in1=s2, op=ALU.mult))
        ch.wait("v", "v")
    ch.mark("v", V.tensor_tensor(out=bb, in0=s1, in1=aa, op=ALU.mult))
    ch.wait("v", "v")
    row_tok = ch.mark("v", V.tensor_scalar(out=bb, in0=bb, scalar1=-1.0,
                                           scalar2=None, op0=ALU.mult))

    ch.wait("t", "v", row_tok)
    if scr.get("bcast_read_tok") is not None:
        ch.wait("t", "v", scr["bcast_read_tok"])
    TE.matmul(out=ps_b[:, 0:TT], lhsT=c["ones_row_f"][:], rhs=aa, start=True,
              stop=True)
    i = TE.matmul(out=ps_b[:, TT:2 * TT], lhsT=c["ones_row_f"][:], rhs=bb,
                  start=True, stop=True)
    bc_tok = ch.mark("t", i)

    ch.wait("v", "t", bc_tok)
    if scr.get("pass2_tok") is not None:
        ch.wait("v", "p", scr["pass2_tok"])
    ch.mark("v", V.tensor_copy(out=ab_sb[:], in_=ps_b[:, 0:TT]))
    evb_tok = ch.mark("v", V.tensor_copy(out=bb_sb[:], in_=ps_b[:, TT:2 * TT]))
    scr["bcast_read_tok"] = evb_tok

    ch.wait("v", "v", evb_tok)
    p2_tok = None
    for h in range(2):
        if h == 1 or scr.get("pass2_tok") is not None:
            ch.wait("v", "p", p2_tok if h == 1 else scr["pass2_tok"])
        i = V.tensor_tensor(out=t1[:], in0=_half(xview3(X, t), h),
                            in1=bview(ab_sb, NCD // 2), op=ALU.mult)
        p1 = ch.mark("v", i)
        ch.wait("p", "v", p1)
        i = P.tensor_tensor(out=_half(xn_view, h), in0=t1[:],
                            in1=bview(bb_sb, NCD // 2), op=ALU.add)
        p2_tok = ch.mark("p", i)
    scr["reuse_tok_t"] = stat_tok
    scr["pass2_tok"] = p2_tok
    return p2_tok


def emit_ln_scratch(p, c, sctx, sep_stat=False):
    sb = lambda name, shape, dt: sctx.enter_context(
        p.nc.sbuf_tensor(p.name(name), shape, dt))
    ps = lambda name, shape, dt: sctx.enter_context(
        p.nc.psum_tensor(p.name(name), shape, dt))
    scr = {
        "rowbuf": sb("ln_rowbuf", [1, 4 * TT], F32),
        "xb": sb("ln_xb", [128, NCD * TT], BF16),
        "xsq": sb("ln_xsq", [128, NCD * TT], BF16),
        "t1": sb("ln_t1", [128, (NCD // 2) * TT], F32),
        "ab_sb": sb("ln_ab", [128, TT], F32),
        "bb_sb": sb("ln_bb", [128, TT], F32),
        "ps_b": ps("ln_ps_b", [128, 2 * TT], F32),
        "reuse_tok_t": None,
        "stat_read_tok": None,
        "bcast_read_tok": None,
        "pass2_tok": None,
    }
    if sep_stat:
        scr["ps_stat"] = ps("ln_ps_stat", [128, TT], F32)
    else:
        scr["ps_stat"] = scr["ps_b"]
    return scr



class WLoader:

    def __init__(self, p, sctx, tag, cols, nbuf=2, nstage=1):
        self.p = p
        self.cols = cols
        self.nbuf = nbuf
        self.nstage = nstage
        sb = lambda name, shape, dt: sctx.enter_context(
            p.nc.sbuf_tensor(p.name(name), shape, dt))
        self.stage = sb(f"{tag}_stg", [128, nstage * cols], F32)
        self.bf = sb(f"{tag}_bf", [128, nbuf * cols], BF16)
        self.slot_cast_tok = [None] * nbuf
        self.slot_read_tok = [None] * nbuf
        self.seq = 0

    def load(self, src_ap, read_done=None, cast="p", scale_aps=None,
             chunk_cols=128, ncols=None):
        p, ch, nc = self.p, self.p.ch, self.p.nc
        n = self.cols if ncols is None else ncols
        slot = self.seq % self.nbuf
        sslot = self.seq % self.nstage
        self.seq += 1
        st = self.stage[:, sslot * self.cols: sslot * self.cols + n]
        bf = self.bf[:, slot * self.cols: slot * self.cols + n]
        prev = self.seq - 1 - self.nstage
        if prev >= 0:
            cc, tt = self.slot_cast_tok[prev % self.nbuf]
            ch.wait("qs", cc, tt)
        dma_tok = ch.dma("qs", st, src_ap)
        ch.wait("s", *dma_tok)
        if self.slot_read_tok[slot] is not None:
            cc, tt = self.slot_read_tok[slot]
            ch.wait("s", cc, tt)
        if scale_aps is None:
            i = nc.scalar.copy(out=bf, in_=st)
        else:
            n = self.cols // chunk_cols
            for k in range(n):
                i = nc.scalar.activation(
                    out=bf[:, k * chunk_cols:(k + 1) * chunk_cols],
                    in_=st[:, k * chunk_cols:(k + 1) * chunk_cols],
                    func=AF.Copy, scale=scale_aps[k])
        tok = ch.mark("s", i)
        self.slot_cast_tok[slot] = ("s", tok)
        return slot, ("s", tok)

    def view(self, slot, ncols=None):
        n = self.cols if ncols is None else ncols
        return self.bf[:, slot * self.cols: slot * self.cols + n]

    def set_read(self, slot, chain, tok):
        self.slot_read_tok[slot] = (chain, tok)



def emit_ffn(p, c, X, w, layer):
    nc, ch = p.nc, p.ch
    V, S, TE, P = nc.vector, nc.scalar, nc.tensor, nc.gpsimd
    E = L0_E if layer == "l0" else L1_E
    FC = L0_FC if layer == "l0" else L1_FC

    sctx = ExitStack()
    with sctx:
        sb = lambda name, shape, dt: sctx.enter_context(
            nc.sbuf_tensor(p.name(name), shape, dt))
        ps = lambda name, shape, dt: sctx.enter_context(
            nc.psum_tensor(p.name(name), shape, dt))

        XN = sb("ffn_xn", [128, NCD * T], BF16)
        E16 = E if E >= 8 else 8
        selc = sb("selc", [E16, E * 128], BF16)
        sc_tok = ch.dma("qs", selc[:], w["selc"][:])
        rw = sb("rw", [128, NCD * E], BF16)
        rw_st = sb("rw_st", [128, NCD * E], F32)
        tk = ch.dma("qs", rw_st[:], w["rw"].rearrange("(c p) e -> p c e", p=128))
        ch.wait("p", *tk)
        ch.wait("p", "t")
        rw_tok = ch.mark("p", nc.gpsimd.tensor_copy(out=rw[:], in_=rw_st[:]))
        if layer == "l1":
            ebias_sb = sb("ebias_sb", [1, 16], F32)
            eb_tok = ch.dma("qs", ebias_sb[:], w["ebias"][:])

        xn_toks = []
        scr = emit_ln_scratch(p, c, sctx, sep_stat=True)
        for t in range(NTT):
            xn_toks.append(emit_ln_tile(p, c, X, t, xview3(XN, t), scr))

        nsl = T // 128
        c_fm = sb("c_fm", [E if E >= 8 else 8, nsl * 128], BF16)
        c_all = sb("c_all", [128, nsl * E], F32)
        r_sc = sb("r_sc", [128, 16 + 16 + 4], F32)
        r_sc2 = sb("r_sc2", [128, 16], F32)
        if layer == "l1":
            bias_b = sb("bias_b", [128, 16], F32)
        ps_cb = ps("ps_cb", [128, 512], F32)
        for tk_ in c["consts_tok"]:
            ch.wait("t", *tk_)
        ch.wait("t", "p", rw_tok)
        for sl_ in range(nsl):
            ch.wait("t", "p", xn_toks[sl_ // 4])
            for dc in range(NCD):
                i = TE.matmul(out=ps_cb[:, sl_ * 16: sl_ * 16 + E],
                              lhsT=XN[:, dc * T + sl_ * 128: dc * T + (sl_ + 1) * 128],
                              rhs=rw[:, dc * E:(dc + 1) * E],
                              start=(dc == 0), stop=(dc == NCD - 1))
        lg_tok = ch.mark("t", i)

        def emit_router_coeffs():
            if layer == "l1":
                ch.wait("t", *eb_tok)
                i = TE.matmul(out=ps_cb[:, 496:512], lhsT=c["ones_row_f"][:],
                              rhs=ebias_sb[0:1, :], start=True, stop=True,
                              skip_group_check=True)
                bt = ch.mark("t", i)
                ch.wait("v", "t", bt)
                bias_ev = ch.mark("v", V.tensor_copy(out=bias_b[:],
                                                     in_=ps_cb[:, 496:512]))
                ch.wait("v", "v")
                cb_guard = ("v", bias_ev)
            else:
                cb_guard = None
            ch.wait("v", "t")
            ch.wait("s", "t")
            for s_ in range(nsl):
                lgs = ps_cb[:, s_ * 16: s_ * 16 + E]
                logits = r_sc[:, 0:E]
                probs = r_sc[:, 16:16 + E]
                vals = r_sc2[:, 0:8]
                nmax = r_sc[:, 32:33]
                den = r_sc[:, 33:34]
                cdst = c_all[:, s_ * E: (s_ + 1) * E]
                ch.wait("v", "s")
                if layer == "l0":
                    ch.mark("v", V.reduce_max(out=nmax, in_=lgs, axis=AX.X,
                                              negate=True))
                    ch.wait("s", "v")
                    et = ch.mark("s", S.activation(out=logits, in_=lgs,
                                                   func=AF.Exp, bias=nmax))
                    ch.wait("v", "s", et)
                    ch.mark("v", V.reduce_sum(out=den, in_=logits, axis=AX.X))
                    ch.wait("v", "v")
                    ch.mark("v", V.reciprocal(out=den, in_=den))
                    ch.wait("v", "v")
                    ch.mark("v", V.tensor_scalar(out=probs, in0=logits,
                                                 scalar1=den, scalar2=None,
                                                 op0=ALU.mult))
                    ch.wait("v", "v")
                    ch.mark("v", V.max(out=vals, in_=probs))
                    ch.wait("v", "v")
                    ch.mark("v", V.tensor_scalar(out=logits, in0=probs,
                                                 scalar1=vals[:, 1:2],
                                                 scalar2=None, op0=ALU.is_ge))
                    ch.wait("s", "v")
                    et = ch.mark("s", S.activation(out=r_sc2[:, 8:8 + E],
                                                   in_=probs, func=AF.Exp))
                    ch.wait("v", "s", et)
                    ch.mark("v", V.tensor_tensor(out=probs,
                                                 in0=r_sc2[:, 8:8 + E],
                                                 in1=logits, op=ALU.mult))
                    ch.wait("v", "v")
                    ch.mark("v", V.reduce_sum(out=den, in_=probs, axis=AX.X))
                    ch.wait("v", "v")
                    ch.mark("v", V.reciprocal(out=den, in_=den))
                    ch.wait("v", "v")
                    ch.mark("v", V.tensor_scalar(out=cdst, in0=probs,
                                                 scalar1=den, scalar2=0.5,
                                                 op0=ALU.mult, op1=ALU.mult))
                else:
                    ch.wait("s", "v")
                    st = ch.mark("s", S.activation(out=probs, in_=lgs,
                                                   func=AF.Sigmoid))
                    ch.wait("v", "s", st)
                    ch.mark("v", V.tensor_tensor(out=logits, in0=probs,
                                                 in1=bias_b[:], op=ALU.add))
                    ch.wait("v", "v")
                    ch.mark("v", V.max(out=vals, in_=logits))
                    ch.wait("v", "v")
                    ch.mark("v", V.tensor_scalar(out=logits, in0=logits,
                                                 scalar1=vals[:, 7:8],
                                                 scalar2=None, op0=ALU.is_ge))
                    ch.wait("v", "v")
                    ch.mark("v", V.tensor_tensor(out=probs, in0=probs,
                                                 in1=logits, op=ALU.mult))
                    ch.wait("v", "v")
                    ch.mark("v", V.reduce_sum(out=den, in_=probs, axis=AX.X))
                    ch.wait("v", "v")
                    ch.mark("v", V.reciprocal(out=den, in_=den))
                    ch.wait("v", "v")
                    ch.mark("v", V.tensor_scalar(out=cdst, in0=probs,
                                                 scalar1=den,
                                                 scalar2=2.826 * 0.5,
                                                 op0=ALU.mult, op1=ALU.mult))
            cw_tok = ch.counts["v"]
            ch.wait("t", "v", cw_tok)
            if cb_guard is not None:
                ch.wait("t", cb_guard[0], cb_guard[1])
            tr_rd = None
            for s_ in range(nsl):
                if tr_rd is not None:
                    ch.wait("t", "v", tr_rd)
                i = TE.transpose(out=ps_cb[0:E, 0:128],
                                 in_=c_all[:, s_ * E: (s_ + 1) * E],
                                 identity=c["ident_f"][:])
                tt = ch.mark("t", i)
                ch.wait("v", "t", tt)
                tr_rd = ch.mark("v", V.tensor_copy(
                    out=c_fm[:, s_ * 128:(s_ + 1) * 128],
                    in_=ps_cb[0:E, 0:128]))
            nonlocal cb_read
            cb_read = ("v", tr_rd)

        fuse = sb("fuse", [128, (FC // 2) * T], BF16)
        act = sb("act", [128, TT], BF16)
        csb = sb("csb", [128, NTT * TT], BF16)
        dtmp = sb("dtmp", [128, TT], F32)
        ps_g = ps("ps_g", [128, 2 * 512], F32)
        ps_u = ps("ps_u", [128, 2 * 512], F32)
        ps_d = scr["ps_b"]
        wl_g = WLoader(p, sctx, "wg", NCD * 128)
        wl_u = WLoader(p, sctx, "wu", NCD * 128)

        gu_read = {}
        d_read = {}
        act_read = {}
        fuse_rd_tok = None
        csb_read = None
        dtmp_read = {}
        x_acc = {}
        gk = w["gk"]
        uk = w["uk"]
        dk = w["dk"]

        n_exp = E + (1 if layer == "l1" else 0)
        NH = 2
        FCH = FC // NH
        cb_read = None
        tr_done = None

        def w_src(e, fc, kind):
            if e == E:
                m = {"g": w["sg"], "u": w["su"]}
                if kind == "d":
                    return None
                return m[kind][:, fc * 128:(fc + 1) * 128].rearrange(
                    "(c p) f -> p c f", p=128)
            m = {"g": gk, "u": uk}
            return m[kind][e, :, fc * 128:(fc + 1) * 128].rearrange(
                "(c p) f -> p c f", p=128)

        def d_src(e, half, dc):
            r0, r1 = half * FCH * 128, (half + 1) * FCH * 128
            if e == E:
                return w["sd"][r0:r1, dc * 128:(dc + 1) * 128].rearrange(
                    "(c p) f -> p c f", p=128)
            return dk[e, r0:r1, dc * 128:(dc + 1) * 128].rearrange(
                "(c p) f -> p c f", p=128)

        plan = []
        for e_ in range(n_exp):
            for half_ in range(NH):
                for fcl_ in range(FCH):
                    fc_ = half_ * FCH + fcl_
                    plan.append((wl_g, w_src(e_, fc_, "g"), None))
                    plan.append((wl_u, w_src(e_, fc_, "u"), None))
                for dc_ in range(NCD):
                    plan.append((wl_g, d_src(e_, half_, dc_), FCH * 128))
        _lp = {"issued": 0, "h": {}}

        def lp_use(i):
            while _lp["issued"] <= min(i + 1, len(plan) - 1):
                j = _lp["issued"]
                ld, src, nc_ = plan[j]
                _lp["h"][j] = (ld, ld.load(src, ncols=nc_))
                _lp["issued"] += 1
            return _lp["h"].pop(i)

        def emit_csb(e, shared):
            nonlocal cb_read, csb_read
            if shared:
                if csb_read is not None:
                    ch.wait("v", csb_read[0], csb_read[1])
                rdy = ch.mark("v", V.memset(csb[:], 0.5))
                csb_read = None
                return rdy
            for t in range(NTT):
                ch.wait("t", "v", tr_done)
                ch.wait("t", *sc_tok)
                if cb_read is not None:
                    ch.wait("t", cb_read[0], cb_read[1])
                sel = selc[:, e * 128:(e + 1) * 128]
                i = TE.matmul(out=ps_cb[:, 0:TT], lhsT=sel,
                              rhs=c_fm[:, t * TT:(t + 1) * TT],
                              start=True, stop=True)
                bt = ch.mark("t", i)
                ch.wait("v", "t", bt)
                if csb_read is not None:
                    ch.wait("v", csb_read[0], csb_read[1])
                cb_tok = ch.mark("v", V.tensor_copy(
                    out=csb[:, t * TT:(t + 1) * TT], in_=ps_cb[:, 0:TT]))
                cb_read = ("v", cb_tok)
            csb_read = None
            return ch.counts["v"]

        for e in range(n_exp):
            shared = (e == E)
            eplan0 = e * NH * (2 * FCH + NCD)
            for half in range(NH):
                step = 0
                hplan0 = eplan0 + half * (2 * FCH + NCD)
                for fcl in range(FCH):
                    if e == 0 and half == 0 and fcl == 1:
                        emit_router_coeffs()
                        tr_done = ch.counts["v"]
                    fc = half * FCH + fcl
                    _, (sg, gtok) = lp_use(hplan0 + 2 * fcl)
                    _, (su, utok) = lp_use(hplan0 + 2 * fcl + 1)
                    wgb, wub = wl_g.view(sg), wl_u.view(su)
                    for t in range(NTT):
                        bk = step % 2
                        gsl = ps_g[:, bk * 512: bk * 512 + TT]
                        usl = ps_u[:, bk * 512: bk * 512 + TT]
                        ch.wait("t", "p", xn_toks[t])
                        ch.wait("t", gtok[0], gtok[1])
                        ch.wait("t", utok[0], utok[1])
                        if bk in gu_read:
                            ch.wait("t", gu_read[bk][0], gu_read[bk][1])
                        for dc in range(NCD):
                            TE.matmul(out=gsl,
                                      lhsT=wgb[:, dc * 128:(dc + 1) * 128],
                                      rhs=XN[:, dc * T + t * TT: dc * T + t * TT + TT],
                                      start=(dc == 0), stop=(dc == NCD - 1))
                        for dc in range(NCD):
                            i = TE.matmul(out=usl,
                                          lhsT=wub[:, dc * 128:(dc + 1) * 128],
                                          rhs=XN[:, dc * T + t * TT: dc * T + t * TT + TT],
                                          start=(dc == 0), stop=(dc == NCD - 1))
                        mm_tok = ch.mark("t", i)
                        asl = act[:, 0:TT]
                        ch.wait("s", "t", mm_tok)
                        if 0 in act_read:
                            ch.wait("s", act_read[0][0], act_read[0][1])
                        sa = ch.mark("s", S.activation(out=asl, in_=gsl, func=AF.Silu))
                        ch.wait("v", "s", sa)
                        if fuse_rd_tok is not None:
                            ch.wait("v", "t", fuse_rd_tok)
                        fv = ch.mark("v", V.tensor_tensor(
                            out=fuse[:, fcl * T + t * TT: fcl * T + t * TT + TT],
                            in0=asl, in1=usl, op=ALU.mult))
                        act_read[0] = ("v", fv)
                        gu_read[bk] = ("v", fv)
                        step += 1
                    wl_g.set_read(sg, "t", ch.counts["t"])
                    wl_u.set_read(su, "t", ch.counts["t"])
                fuse_v_tok = ch.counts["v"]
                if half == 0:
                    csb_ready = emit_csb(e, shared)

                dbanks = [ps_d[:, 0:512], ps_d[:, 512:1024],
                          ps_g[:, 0:512], ps_g[:, 512:1024],
                          ps_u[:, 0:512], ps_u[:, 512:1024]]
                dstep = 0
                for dc in range(NCD):
                    _, (sd_, dtok) = lp_use(hplan0 + 2 * FCH + dc)
                    wdb = wl_g.view(sd_, ncols=FCH * 128)
                    for t in range(NTT):
                        bk = dstep % 6
                        dsl = dbanks[bk]
                        ch.wait("t", dtok[0], dtok[1])
                        ch.wait("t", "v", fuse_v_tok)
                        prev = d_read.get(bk)
                        if prev is None or prev[1] is None:
                            if bk < 2:
                                ch.wait("t", "v", scr["bcast_read_tok"])
                            elif bk < 4:
                                gr = gu_read.get(0)
                                if gr:
                                    ch.wait("t", gr[0], gr[1])
                            else:
                                gr = gu_read.get(1)
                                if gr:
                                    ch.wait("t", gr[0], gr[1])
                        else:
                            ch.wait("t", prev[0], prev[1])
                        for fcl in range(FCH):
                            i = TE.matmul(
                                out=dsl,
                                lhsT=wdb[:, fcl * 128:(fcl + 1) * 128],
                                rhs=fuse[:, fcl * T + t * TT: fcl * T + t * TT + TT],
                                start=(fcl == 0), stop=(fcl == FCH - 1))
                        dmm_tok = ch.mark("t", i)
                        tsl = dtmp[:, 0:TT]
                        ch.wait("v", "t", dmm_tok)
                        ch.wait("v", "v", csb_ready)
                        if 0 in dtmp_read:
                            ch.wait("v", dtmp_read[0][0], dtmp_read[0][1])
                        tv = ch.mark("v", V.tensor_tensor(
                            out=tsl, in0=dsl, in1=csb[:, t * TT:(t + 1) * TT],
                            op=ALU.mult))
                        d_read[bk] = ("v", tv)
                        csb_read = ("v", tv)
                        ch.wait("p", "v", tv)
                        if (dc, t) in x_acc:
                            ch.wait("p", "p", x_acc[(dc, t)])
                        xc = X[:, dc * T + t * TT: dc * T + t * TT + TT]
                        px = ch.mark("p", P.tensor_tensor(out=xc, in0=xc, in1=tsl,
                                                          op=ALU.add))
                        x_acc[(dc, t)] = px
                        dtmp_read[0] = ("p", px)
                        dstep += 1
                    wl_g.set_read(sd_, "t", ch.counts["t"])
                for gb, pair in ((0, (2, 4)), (1, (3, 5))):
                    toks = [d_read[i][1] for i in pair if i in d_read]
                    if toks:
                        cur = gu_read.get(gb, ("v", 0))[1]
                        gu_read[gb] = ("v", max([cur] + toks))
                fuse_rd_tok = ch.counts["t"]
        ch.barrier()



def emit_l0attn_setup(p, c, aps, sctx):
    nc, ch = p.nc, p.ch
    V, S, TE, P = nc.vector, nc.scalar, nc.tensor, nc.gpsimd
    sb = lambda name, shape, dt: sctx.enter_context(
        nc.sbuf_tensor(p.name(name), shape, dt))
    a = {}
    a["gln"] = sb("gln", [128, NCD], F32)
    a["glnq"] = sb("glnq", [128, NCD], F32)
    a["xlu"] = sb("xlu", [128, 1], F32)
    a["xlv"] = sb("xlv", [128, 1], F32)
    a["binmask"] = sb("binmask", [128, 4 * 512], BF16)
    a["wq_bf"] = sb("wq_bf", [128, NCD * 128], BF16)
    a["wk_bf"] = sb("wk_bf", [128, NCD * 128], BF16)
    a["r_sb"] = sb("r_sb", [128, T], BF16)

    t1 = ch.dma("qs", a["gln"][:], aps["gln"][:])
    ch.dma("qs", a["xlu"][:], aps["xlu"][:])
    ch.dma("qs", a["xlv"][:], aps["xlv"][:])
    ch.dma("qs", a["binmask"][:],
           aps["binmask"].rearrange("(r p) j -> p r j", p=128))
    ch.wait("v", *t1)
    glt = ch.mark("v", V.tensor_scalar(out=a["glnq"][:], in0=a["gln"][:],
                                       scalar1=float(1.0 / np.sqrt(128.0)),
                                       scalar2=None, op0=ALU.mult))

    setup_ctx = ExitStack()
    setup_ctx.__enter__()
    stg = setup_ctx.enter_context(
        nc.sbuf_tensor(p.name("wqk_stg"), [128, NCD * 128], F32))
    for nm, dst, svec, vtok in (("wq", a["wq_bf"], a["glnq"], glt),
                                ("wk", a["wk_bf"], a["gln"], t1)):
        tk = ch.dma("qs", stg[:], aps[nm].rearrange("(c p) k -> p c k", p=128))
        ch.wait("s", *tk)
        ch.wait("s", "v", glt)
        for dc in range(NCD):
            i = S.activation(out=dst[:, dc * 128:(dc + 1) * 128],
                             in_=stg[:, dc * 128:(dc + 1) * 128],
                             func=AF.Copy, scale=svec[:, dc:dc + 1])
        stok = ch.mark("s", i)
        ch.wait("qs", "s", stok)

    wr_bf = setup_ctx.enter_context(
        nc.sbuf_tensor(p.name("wr_bf"), [128, NCD * 128], BF16))
    tk = ch.dma("qs", stg[:], aps["wr"].rearrange("(c p) k -> p c k", p=128))
    ch.wait("p", *tk)
    wrt = ch.mark("p", P.tensor_copy(out=wr_bf[:], in_=stg[:]))

    sin_st = setup_ctx.enter_context(
        nc.sbuf_tensor(p.name("sin_st"), [128, NCD * 512], BF16))
    pctx = ExitStack()
    with pctx:
        ps_r = pctx.enter_context(nc.psum_tensor(p.name("ps_rsb"), [128, 512], F32))
        ev_tok = None
        for pg in range(4):
            tk = ch.dma("qs", sin_st[:],
                        aps["sinembT"].rearrange("(c p) t -> p c t", p=128)
                        [:, :, pg * 512:(pg + 1) * 512])
            ch.wait("t", *tk)
            ch.wait("t", "p", wrt)
            if ev_tok is not None:
                ch.wait("t", "v", ev_tok)
            for dc in range(NCD):
                i = TE.matmul(out=ps_r[:, 0:512],
                              lhsT=wr_bf[:, dc * 128:(dc + 1) * 128],
                              rhs=sin_st[:, dc * 512:(dc + 1) * 512],
                              start=(dc == 0), stop=(dc == NCD - 1))
            mt = ch.mark("t", i)
            ch.wait("v", "t", mt)
            ev_tok = ch.mark("v", V.tensor_copy(
                out=a["r_sb"][:, pg * 512:(pg + 1) * 512], in_=ps_r[:, 0:512]))
            ch.wait("qs", "v", ev_tok)
        ch.barrier()
    setup_ctx.close()
    return a


def emit_l0attn(p, c, X, aps):
    nc, ch = p.nc, p.ch
    V, S, TE, P = nc.vector, nc.scalar, nc.tensor, nc.gpsimd
    sctx = ExitStack()
    with sctx:
        sb = lambda name, shape, dt: sctx.enter_context(
            nc.sbuf_tensor(p.name(name), shape, dt))
        ps = lambda name, shape, dt: sctx.enter_context(
            nc.psum_tensor(p.name(name), shape, dt))

        a = emit_l0attn_setup(p, c, aps, sctx)
        A_fm = sb("A_fm", [128, NCD * T], BF16)
        s01ctx = ExitStack()
        s01ctx.__enter__()
        sb01 = lambda name, shape, dt: s01ctx.enter_context(
            nc.sbuf_tensor(p.name(name), shape, dt))
        qu = sb01("qu", [128, T], BF16)
        qv = sb01("qv", [128, T], BF16)
        k_sb = sb01("k_sb", [128, T], BF16)
        xn_tm = sb01("xn_tm", [128, 16 * 1024], BF16)

        l0ctx = ExitStack()
        with l0ctx:
            scr = emit_ln_scratch(p, c, l0ctx, sep_stat=True)
            xn_t = l0ctx.enter_context(
                nc.sbuf_tensor(p.name("xn_t"), [128, NCD * 512], BF16))
            ps_qk = l0ctx.enter_context(
                nc.psum_tensor(p.name("ps_qk"), [128, 1024], F32))
            ps_tp = l0ctx.enter_context(
                nc.psum_tensor(p.name("ps_tp"), [128, 1024], BF16))
            qk_read = {}
            tp_read = None
            xn_read = [None]
            for t in range(NTT):
                slot = 0
                base = 0
                if xn_read[slot] is not None:
                    ch.wait("p", "t", xn_read[slot])
                xn_tok = emit_ln_tile(p, c, X, t, cview(xn_t), scr,
                                      rms_fold=True)
                for which, (wbf, bank) in enumerate(
                        ((a["wq_bf"], 0), (a["wk_bf"], 1))):
                    sl = ps_qk[:, bank * 512: bank * 512 + 512]
                    ch.wait("t", "p", xn_tok)
                    if bank in qk_read:
                        ch.wait("t", qk_read[bank][0], qk_read[bank][1])
                    for dc in range(NCD):
                        i = TE.matmul(out=sl, lhsT=wbf[:, dc * 128:(dc + 1) * 128],
                                      rhs=xn_t[:, base + dc * 512: base + dc * 512 + 512],
                                      start=(dc == 0), stop=(dc == NCD - 1))
                    mt = ch.mark("t", i)
                    ch.wait("s", "t", mt)
                    if which == 0:
                        S.activation(out=qu[:, t * 512:(t + 1) * 512], in_=sl,
                                     func=AF.Identity, bias=a["xlu"][:])
                        i = S.activation(out=qv[:, t * 512:(t + 1) * 512], in_=sl,
                                         func=AF.Identity, bias=a["xlv"][:])
                    else:
                        i = S.activation(out=k_sb[:, t * 512:(t + 1) * 512],
                                         in_=sl, func=AF.Identity)
                    qk_read[bank] = ("s", ch.mark("s", i))
                for sl4 in range(4):
                    if tp_read is not None:
                        ch.wait("t", "v", tp_read)
                    ch.wait("t", "p", xn_tok)
                    for dc in range(NCD):
                        i = TE.transpose(
                            out=ps_tp[:, dc * 128:(dc + 1) * 128],
                            in_=xn_t[:, base + dc * 512 + sl4 * 128:
                                     base + dc * 512 + sl4 * 128 + 128],
                            identity=c["ident_bf"][:])
                    tt = ch.mark("t", i)
                    ch.wait("v", "t", tt)
                    tp_read = ch.mark("v", V.tensor_copy(
                        out=xn_tm[:, (t * 4 + sl4) * 1024:(t * 4 + sl4 + 1) * 1024],
                        in_=ps_tp[:]))
                xn_read[slot] = ch.counts["t"]
            ch.barrier()

        l1ctx = ExitStack()
        with l1ctx:
            sb1 = lambda name, shape, dt: l1ctx.enter_context(
                nc.sbuf_tensor(p.name(name), shape, dt))
            raw = sb1("raw", [128, 2560], BF16)
            shifted = sb1("shifted", [128, T], BF16)
            s_sb = sb1("s_sb", [128, 2 * T], F32)
            w_sb = sb1("w_sb", [128, 2 * T], BF16)
            wT_sb = sb1("wT_sb", [128, 16 * 128], BF16)
            a_tm = sb1("a_tm", [128, 1024], BF16)
            rows = sb1("rows", [128, 4], F32)
            ps_pool = l1ctx.enter_context(
                nc.psum_tensor(p.name("ps_pool"), [128, 2048], F32))
            ps_tp1 = l1ctx.enter_context(
                nc.psum_tensor(p.name("ps_tp1"), [128, 1024], BF16))
            ps_a = l1ctx.enter_context(
                nc.psum_tensor(p.name("ps_a"), [128, 1024], F32))

            zt = ch.mark("v", V.memset(raw[:, 2048:2560], 0.0))
            st1 = {
                "bank_read": {}, "shift_read": None, "wT_read": None,
                "wsb_read": {}, "tp1_read": None, "a_read": None,
                "atm_read": None, "raw_dma": None, "s_tok": {}, "exp": {},
            }

            def emit_scores(qi):
                I0 = qi * 128
                ng = qi // 4 + 1
                g0 = (2047 - I0 - 127) // 512
                buf = qi % 2
                br = st1["bank_read"]
                for g in range(g0, 4):
                    if g in br:
                        ch.wait("t", br[g][0], br[g][1])
                    i = TE.matmul(out=ps_pool[:, g * 512:(g + 1) * 512],
                                  lhsT=qv[:, I0:I0 + 128],
                                  rhs=a["r_sb"][:, g * 512:(g + 1) * 512],
                                  start=True, stop=True)
                    br.pop(g, None)
                bd_tok = ch.mark("t", i)
                ch.wait("v", "t", bd_tok)
                ch.wait("v", "v", zt)
                if st1["raw_dma"] is not None:
                    ch.wait("v", *st1["raw_dma"])
                for g in range(g0, 4):
                    i = V.tensor_copy(out=raw[:, g * 512:(g + 1) * 512],
                                      in_=ps_pool[:, g * 512:(g + 1) * 512])
                raw_tok = ch.mark("v", i)
                ch.wait("qs", "v", raw_tok)
                if st1["shift_read"] is not None:
                    ch.wait("qs", "v", st1["shift_read"])
                dtok = ch.dma("qs", shifted[:, 0:ng * 512],
                              diag_ap(raw[:], 2560, 2047 - I0, 128, ng * 512))
                st1["raw_dma"] = dtok
                for g in range(ng):
                    ch.wait("t", "v", raw_tok)
                    if g in br:
                        ch.wait("t", br[g][0], br[g][1])
                    i = TE.matmul(out=ps_pool[:, g * 512:(g + 1) * 512],
                                  lhsT=qu[:, I0:I0 + 128],
                                  rhs=k_sb[:, g * 512:(g + 1) * 512],
                                  start=True, stop=True)
                ac_tok = ch.mark("t", i)
                ch.wait("v", "t", ac_tok)
                ch.wait("v", *dtok)
                ch.wait("v", "s")
                for g in range(ng):
                    i = V.tensor_tensor(
                        out=s_sb[:, buf * T + g * 512: buf * T + (g + 1) * 512],
                        in0=ps_pool[:, g * 512:(g + 1) * 512],
                        in1=shifted[:, g * 512:(g + 1) * 512],
                        op=ALU.add)
                    br[g] = ("v", ch.mark("v", i))
                st1["s_tok"][qi] = ch.counts["v"]
                st1["shift_read"] = ch.counts["v"]

            def emit_out(qi):
                I0 = qi * 128
                ng = qi // 4 + 1
                nj = qi + 1
                buf = qi % 2
                ssl = s_sb[:, buf * T: buf * T + ng * 512]
                wsl = w_sb[:, buf * T: buf * T + ng * 512]
                nmax = rows[:, 2 * buf: 2 * buf + 1]
                rden = rows[:, 2 * buf + 1: 2 * buf + 2]
                ch.wait("v", "v", st1["s_tok"][qi])
                ch.wait("v", "s")
                nm_tok = ch.mark("v", V.reduce_max(out=nmax, in_=ssl,
                                                   axis=AX.X, negate=True))
                ch.wait("s", "v", nm_tok)
                wg = st1["wsb_read"].get(buf)
                if wg is not None:
                    ch.wait("s", "t", wg)
                e_tok = ch.mark("s", S.activation(out=wsl, in_=ssl,
                                                  func=AF.Exp, bias=nmax))
                st1["exp"][qi] = e_tok
                ch.wait("v", "s", e_tok)
                m_tok = ch.mark("v", V.tensor_tensor(
                    out=w_sb[:, buf * T + (ng - 1) * 512: buf * T + ng * 512],
                    in0=w_sb[:, buf * T + (ng - 1) * 512: buf * T + ng * 512],
                    in1=a["binmask"][:, (qi % 4) * 512:(qi % 4 + 1) * 512],
                    op=ALU.mult))
                ch.wait("v", "v", m_tok)
                ch.mark("v", V.reduce_sum(out=rden, in_=wsl, axis=AX.X))
                ch.wait("v", "v")
                rd_tok = ch.mark("v", V.reciprocal(out=rden, in_=rden))
                for j0 in range(0, nj, 8):
                    jn = min(8, nj - j0)
                    ch.wait("t", "v", m_tok)
                    ch.wait("t", "s", e_tok)
                    if st1["tp1_read"] is not None:
                        ch.wait("t", "v", st1["tp1_read"])
                    for jj in range(jn):
                        i = TE.transpose(
                            out=ps_tp1[:, jj * 128:(jj + 1) * 128],
                            in_=w_sb[:, buf * T + (j0 + jj) * 128:
                                     buf * T + (j0 + jj + 1) * 128],
                            identity=c["ident_bf"][:])
                    tt = ch.mark("t", i)
                    st1["wsb_read"][buf] = tt
                    ch.wait("v", "t", tt)
                    st1["wT_read"] = ch.mark("v", V.tensor_copy(
                        out=wT_sb[:, j0 * 128:(j0 + jn) * 128],
                        in_=ps_tp1[:, 0:jn * 128]))
                    st1["tp1_read"] = st1["wT_read"]
                ch.wait("t", "v", st1["wT_read"])
                if st1["atm_read"] is not None:
                    ch.wait("t", "s", st1["atm_read"])
                for dh in range(2):
                    for j in range(nj):
                        i = TE.matmul(
                            out=ps_a[:, dh * 512:(dh + 1) * 512],
                            lhsT=wT_sb[:, j * 128:(j + 1) * 128],
                            rhs=xn_tm[:, j * 1024 + dh * 512:
                                      j * 1024 + dh * 512 + 512],
                            start=(j == 0), stop=(j == nj - 1))
                a_tok = ch.mark("t", i)
                ch.wait("s", "t", a_tok)
                ch.wait("s", "v", rd_tok)
                if st1["a_read"] is not None:
                    ch.wait("s", "t", st1["a_read"])
                i = S.activation(out=a_tm[:, 0:512], in_=ps_a[:, 0:512],
                                 func=AF.Copy, scale=rden)
                i = S.activation(out=a_tm[:, 512:1024], in_=ps_a[:, 512:1024],
                                 func=AF.Copy, scale=rden)
                st1["atm_read"] = ch.mark("s", i)
                ch.wait("t", "s", st1["atm_read"])
                if st1["tp1_read"] is not None:
                    ch.wait("t", "v", st1["tp1_read"])
                for dc in range(NCD):
                    i = TE.transpose(out=ps_tp1[:, dc * 128:(dc + 1) * 128],
                                     in_=a_tm[:, dc * 128:(dc + 1) * 128],
                                     identity=c["ident_bf"][:])
                st1["a_read"] = ch.mark("t", i)
                ch.wait("v", "t", st1["a_read"])
                dst = dataclasses.replace(
                    A_fm[:], offset=qi * 128,
                    ap=[[NCD * T, 128], [T, NCD], [1, 128]])
                st1["tp1_read"] = ch.mark("v", V.tensor_copy(out=dst,
                                                             in_=ps_tp1[:]))
            emit_scores(0)
            for qi in range(1, 16):
                emit_scores(qi)
                emit_out(qi - 1)
            emit_out(15)
            ch.barrier()
        s01ctx.close()

        l2ctx = ExitStack()
        with l2ctx:
            scr = emit_ln_scratch(p, c, l2ctx)
            xn_t = l2ctx.enter_context(
                nc.sbuf_tensor(p.name("xn_t2"), [128, NCD * 512], BF16))
            gt_sb = l2ctx.enter_context(
                nc.sbuf_tensor(p.name("gt_sb"), [128, 2 * 512], BF16))
            wvg = l2ctx.enter_context(
                nc.sbuf_tensor(p.name("wvg"), [128, 16 * 512], BF16))
            ps_wv = l2ctx.enter_context(
                nc.psum_tensor(p.name("ps_wv"), [128, 1024], F32))
            ps_gt = l2ctx.enter_context(
                nc.psum_tensor(p.name("ps_gt"), [128, 1024], F32))
            wl_wv = WLoader(p, l2ctx, "wwv", NCD * 128)
            wl_wg = WLoader(p, l2ctx, "wwg", NCD * 128)
            wl_wo = WLoader(p, l2ctx, "wwo", 8 * 128)
            ps_o = scr["ps_b"]
            gscale = [a["gln"][:, dc:dc + 1] for dc in range(NCD)]

            def vg_src(vc):
                return (aps["wv"][:, vc * 128:(vc + 1) * 128].rearrange(
                            "(c p) f -> p c f", p=128),
                        aps["wg"][:, vc * 128:(vc + 1) * 128].rearrange(
                            "(c p) f -> p c f", p=128))

            def wo_src(idx):
                dc_, vh_ = idx // 2, idx % 2
                return aps["wo"][vh_ * 1024:(vh_ + 1) * 1024,
                                 dc_ * 128:(dc_ + 1) * 128].rearrange(
                    "(c p) f -> p c f", p=128)

            v0_, g0_ = vg_src(0)
            vg_nxt = [(wl_wv.load(v0_, cast="s", scale_aps=gscale),
                       wl_wg.load(g0_, cast="s", scale_aps=gscale))]
            wo_nxt = [wl_wo.load(wo_src(0))]
            wv_read = {}
            gt_read = {}
            gts_read = {}
            o_read = {}
            wvg_rd = None
            xn2_read = None
            for t in range(NTT):
                if xn2_read is not None:
                    ch.wait("p", "t", xn2_read)
                xn_tok = emit_ln_tile(p, c, X, t, cview(xn_t), scr,
                                      rms_fold=True)
                step = 0
                for vc in range(16):
                    (sv, vtok), (sg, gtok) = vg_nxt[0]
                    vn_, gn_ = vg_src((vc + 1) % 16)
                    vg_nxt[0] = (wl_wv.load(vn_, cast="s", scale_aps=gscale),
                                 wl_wg.load(gn_, cast="s", scale_aps=gscale))
                    bk = step % 2
                    wsl = ps_wv[:, bk * 512: bk * 512 + 512]
                    gsl = ps_gt[:, bk * 512: bk * 512 + 512]
                    ch.wait("t", vtok[0], vtok[1])
                    if bk in wv_read:
                        ch.wait("t", wv_read[bk][0], wv_read[bk][1])
                    for dc in range(NCD):
                        i = TE.matmul(
                            out=wsl,
                            lhsT=wl_wv.view(sv)[:, dc * 128:(dc + 1) * 128],
                            rhs=A_fm[:, dc * T + t * 512: dc * T + t * 512 + 512],
                            start=(dc == 0), stop=(dc == NCD - 1))
                    wv_tok = ch.mark("t", i)
                    ch.wait("t", gtok[0], gtok[1])
                    ch.wait("t", "p", xn_tok)
                    if bk in gt_read:
                        ch.wait("t", gt_read[bk][0], gt_read[bk][1])
                    for dc in range(NCD):
                        i = TE.matmul(
                            out=gsl,
                            lhsT=wl_wg.view(sg)[:, dc * 128:(dc + 1) * 128],
                            rhs=xn_t[:, dc * 512:(dc + 1) * 512],
                            start=(dc == 0), stop=(dc == NCD - 1))
                    gt_tok = ch.mark("t", i)
                    wl_wv.set_read(sv, "t", wv_tok)
                    wl_wg.set_read(sg, "t", gt_tok)
                    ch.wait("s", "t", gt_tok)
                    if bk in gts_read:
                        ch.wait("s", gts_read[bk][0], gts_read[bk][1])
                    st = ch.mark("s", S.activation(
                        out=gt_sb[:, bk * 512: bk * 512 + 512], in_=gsl,
                        func=AF.Silu))
                    ch.wait("v", "s", st)
                    ch.wait("v", "t", wv_tok)
                    if wvg_rd is not None:
                        ch.wait("v", "t", wvg_rd)
                    i = V.tensor_tensor(out=wvg[:, vc * 512:(vc + 1) * 512],
                                        in0=gt_sb[:, bk * 512: bk * 512 + 512],
                                        in1=wsl, op=ALU.mult)
                    fv = ch.mark("v", i)
                    wv_read[bk] = ("v", fv)
                    gt_read[bk] = ("v", fv)
                    gts_read[bk] = ("v", fv)
                    step += 1
                wvg_tok = ch.counts["v"]
                xn2_read = ch.counts["t"]
                for dc in range(NCD):
                    bk = dc % 2
                    osl = ps_o[:, bk * 512: bk * 512 + 512]
                    if dc < 2:
                        o_read.setdefault(bk, ("v", scr["bcast_read_tok"]))
                    for vh in range(2):
                        so, otok = wo_nxt[0]
                        wo_nxt[0] = wl_wo.load(wo_src((dc * 2 + vh + 1) % 16))
                        ch.wait("t", otok[0], otok[1])
                        ch.wait("t", "v", wvg_tok)
                        if vh == 0 and bk in o_read:
                            ch.wait("t", o_read[bk][0], o_read[bk][1])
                        for vcl in range(8):
                            vc = vh * 8 + vcl
                            i = TE.matmul(
                                out=osl,
                                lhsT=wl_wo.view(so)[:, vcl * 128:(vcl + 1) * 128],
                                rhs=wvg[:, vc * 512:(vc + 1) * 512],
                                start=(vc == 0), stop=(vc == 15))
                        o_tok = ch.mark("t", i)
                        wl_wo.set_read(so, "t", o_tok)
                    ch.wait("v", "t", o_tok)
                    xc = X[:, dc * T + t * 512: dc * T + t * 512 + 512]
                    i = V.tensor_tensor(out=xc, in0=xc, in1=osl, op=ALU.add)
                    o_read[bk] = ("v", ch.mark("v", i))
                wvg_rd = ch.counts["t"]
                scr["bcast_read_tok"] = ch.counts["v"]
                scr["stat_read_tok"] = ch.counts["v"]
                o_read = {}
            ch.barrier()


def build_l0attn_prog(host_map):
    nc = bass.Bass("TRN2", target_bir_lowering=False)
    names = ["x", "ident_bf", "gln", "xlu", "xlv", "binmask", "sinembT",
             "wq", "wk", "wv", "wg", "wr", "wo"]
    aps = declare_inputs(nc, names, host_map)
    y = nc.dram_tensor("y", [NSEQ, D, T], F32, kind="ExternalOutput").ap()
    ctx = ExitStack()
    with ctx:
        ch = Chains(nc, ctx)
        p = Prog(nc, ctx, ch)
        c = emit_consts(p)
        c["ident_bf"] = ctx.enter_context(nc.sbuf_tensor("identb", [128, 128], BF16))
        c["consts_tok"] = [ch.dma("qs", c["ident_bf"][:], aps["ident_bf"][:])]
        X = ctx.enter_context(nc.sbuf_tensor("X", [128, NCD * T], F32))
        st_tok = None
        for seq in range(NSEQ):
            ch.barrier()
            if st_tok is not None:
                ch.wait("qs", *st_tok)
            xt = ch.dma("qs", X[:],
                        aps["x"][seq].rearrange("(c p) t -> p c t", p=128))
            for e in ("v", "s", "t", "p"):
                ch.wait(e, *xt)
            emit_l0attn(p, c, X, aps)
            ch.wait("qs", "v")
            ch.wait("qs", "p")
            st_tok = ch.dma("qs", y[seq].rearrange("(c p) t -> p c t", p=128),
                            X[:])
        nc.sync.wait_ge(ch.sems[st_tok[0]], st_tok[1])
    return nc



def emit_l1attn_setup(p, c, aps, sctx):
    nc, ch = p.nc, p.ch
    sb = lambda name, shape, dt: sctx.enter_context(
        nc.sbuf_tensor(p.name(name), shape, dt))
    a = {}
    for nm in ("bq", "bk", "bv"):
        a[nm] = sb(nm, [128, NCD], F32)
        tk = ch.dma("qs", a[nm][:], aps["l1_" + nm][:])
        ch.wait("s", *tk)
    a["carry"] = sb("carry", [128, 16], F32)
    a["zeros"] = sb("zeros", [128, 512], BF16)
    a["eps8"] = sb("eps8", [128, 1], F32)
    nc.vector.memset(a["zeros"][:], 0.0)
    ch.mark("v", nc.vector.memset(a["eps8"][:], 1e-8))
    ch.wait("s", "v")
    return a


def emit_l1attn(p, c, X, aps):
    nc, ch = p.nc, p.ch
    V, S, TE, P = nc.vector, nc.scalar, nc.tensor, nc.gpsimd
    KSEG = NKC * 512

    sctx = ExitStack()
    with sctx:
        sb = lambda name, shape, dt: sctx.enter_context(
            nc.sbuf_tensor(p.name(name), shape, dt))
        a = emit_l1attn_setup(p, c, aps, sctx)
        tb = sb("dft_tb", [128, 10240], BF16)
        cf = tb[:, 0:NCD * NFREQ]
        sf = tb[:, NCD * NFREQ:2 * NCD * NFREQ]
        ci = tb[:, 0:NKC * 1024]
        si = tb[:, NKC * 1024:2 * NKC * 1024]
        qkv = sb("qkv", [128, 3 * NCD * 512], BF16)
        wl_p = WLoader(p, sctx, "wp", NCD * 128)
        scr = emit_ln_scratch(p, c, sctx, sep_stat=False)
        xn_t = sb("xn_t1a", [128, NCD * 512], BF16)
        G = [sb(f"G{i_}", [128, 2 * KSEG], BF16) for i_ in range(3)]
        m2 = sb("m2", [128, KSEG], F32)
        tt2 = sb("tt2", [128, 1024], F32)
        sq_scr = tt2[:, 512:1024]
        cs = sb("cs", [128, 1], F32)
        ps_p = sctx.enter_context(nc.psum_tensor(p.name("ps_p"), [128, 1024], F32))
        ps_f = sctx.enter_context(nc.psum_tensor(p.name("ps_f"), [128, 2048], F32))

        sigs = (("l1_wqT", a["bq"]), ("l1_wkT", a["bk"]), ("l1_wvT", a["bv"]))

        def pj_src(idx):
            return aps[sigs[idx // NCD][0]][
                :, (idx % NCD) * 128:(idx % NCD + 1) * 128].rearrange(
                "(c p) f -> p c f", p=128)

        st = {"p_read": {}, "f_read": {}, "xn_read": None, "qkv_read": None,
              "proj_tok": None, "inv_tok": None, "z_tok": None, "pnxt": None}

        def emit_lnproj(t):
            if st["xn_read"] is not None:
                ch.wait("p", "t", st["xn_read"])
            xn_tok = emit_ln_tile(p, c, X, t, cview(xn_t), scr)
            if st["pnxt"] is None:
                st["pnxt"] = wl_p.load(pj_src(0))
            step = 0
            for sig, (wnm, bias) in enumerate(sigs):
                for mc in range(NCD):
                    sw, wtok = st["pnxt"]
                    nidx = sig * NCD + mc + 1
                    st["pnxt"] = wl_p.load(pj_src(nidx % (3 * NCD)))
                    bk = step % 2
                    sl = ps_p[:, bk * 512: bk * 512 + 512]
                    ch.wait("t", wtok[0], wtok[1])
                    ch.wait("t", "p", xn_tok)
                    pr = st["p_read"].get(bk)
                    if pr is not None:
                        ch.wait("t", pr[0], pr[1])
                    for dc in range(NCD):
                        i = TE.matmul(
                            out=sl,
                            lhsT=wl_p.view(sw)[:, dc * 128:(dc + 1) * 128],
                            rhs=xn_t[:, dc * 512:(dc + 1) * 512],
                            start=(dc == 0), stop=(dc == NCD - 1))
                    mt = ch.mark("t", i)
                    wl_p.set_read(sw, "t", mt)
                    ch.wait("s", "t", mt)
                    if st["qkv_read"] is not None:
                        ch.wait("s", "t", st["qkv_read"])
                    i = S.activation(
                        out=qkv[:, sig * NCD * 512 + mc * 512:
                                sig * NCD * 512 + (mc + 1) * 512],
                        in_=sl, func=AF.Identity, bias=bias[:, mc:mc + 1])
                    st["p_read"][bk] = ("s", ch.mark("s", i))
                    step += 1
            st["proj_tok"] = ch.counts["s"]
            st["xn_read"] = ch.counts["t"]

        def emit_fwd(t):
            ch.wait("qs", "t")
            ttok1 = ch.dma("qs", cf, aps["dft_c"].rearrange(
                "(c p) k -> p c k", p=128))
            ttok2 = ch.dma("qs", sf, aps["dft_s"].rearrange(
                "(c p) k -> p c k", p=128))
            ch.wait("t", *ttok1)
            ch.wait("t", *ttok2)
            ch.wait("t", "s", st["proj_tok"])
            if st["inv_tok"] is not None:
                ch.wait("v", "t", st["inv_tok"])
            for sig in range(3):
                sbase = sig * NCD * 512
                for kc in range(NKC):
                    bk = kc % 2
                    pre = ps_f[:, bk * 1024: bk * 1024 + 512]
                    pim = ps_f[:, bk * 1024 + 512: bk * 1024 + 1024]
                    for bsl, tbl in ((pre, cf), (pim, sf)):
                        fr = st["f_read"].get(bk)
                        if fr is not None:
                            ch.wait("t", fr[0], fr[1])
                        for dc in range(NCD):
                            i = TE.matmul(
                                out=bsl,
                                lhsT=tbl[:, dc * NFREQ + kc * 128:
                                         dc * NFREQ + (kc + 1) * 128],
                                rhs=qkv[:, sbase + dc * 512:
                                        sbase + (dc + 1) * 512],
                                start=(dc == 0), stop=(dc == NCD - 1))
                    ft = ch.mark("t", i)
                    ch.wait("v", "t", ft)
                    V.tensor_copy(out=G[sig][:, kc * 512:(kc + 1) * 512],
                                  in_=pre)
                    ev = ch.mark("v", V.tensor_copy(
                        out=G[sig][:, KSEG + kc * 512: KSEG + (kc + 1) * 512],
                        in_=pim))
                    ch.wait("s", "t", ft)
                    ch.wait("s", "v")
                    S.activation(out=m2[:, kc * 512:(kc + 1) * 512],
                                 in_=pre, func=AF.Square)
                    stok = ch.mark("s", S.activation(out=sq_scr,
                                                     in_=pim, func=AF.Square))
                    ch.wait("v", "s", stok)
                    mv = ch.mark("v", V.tensor_tensor(
                        out=m2[:, kc * 512:(kc + 1) * 512],
                        in0=m2[:, kc * 512:(kc + 1) * 512],
                        in1=sq_scr, op=ALU.add))
                    st["f_read"][bk] = ("v", mv)
                ch.wait("s", "v")
                stok = ch.mark("s", S.activation(out=m2[:], in_=m2[:],
                                                 func=AF.Sqrt))
                ch.wait("s", "s", stok)
                stok = ch.mark("s", S.activation(out=m2[:], in_=m2[:],
                                                 func=AF.Ln, bias=a["eps8"][:]))
                ch.wait("s", "s", stok)
                stok = ch.mark("s", S.activation(out=m2[:], in_=m2[:],
                                                 func=AF.Exp, scale=-1.0))
                ch.wait("v", "s", stok)
                V.tensor_tensor(out=G[sig][:, 0:KSEG], in0=G[sig][:, 0:KSEG],
                                in1=m2[:], op=ALU.mult)
                gt = ch.mark("v", V.tensor_tensor(
                    out=G[sig][:, KSEG:2 * KSEG],
                    in0=G[sig][:, KSEG:2 * KSEG], in1=m2[:], op=ALU.mult))
                ch.wait("s", "v", gt)

        def emit_vphase(t):
            ch.wait("v", "v")
            for kc in range(NKC):
                ksl = slice(kc * 512, (kc + 1) * 512)
                isl = slice(KSEG + kc * 512, KSEG + (kc + 1) * 512)
                kr, ki_ = G[1][:, ksl], G[1][:, isl]
                vr, vi = G[2][:, ksl], G[2][:, isl]
                V.tensor_tensor(out=tt2[:, 0:512], in0=kr, in1=vr, op=ALU.mult)
                i = V.tensor_tensor(out=tt2[:, 512:1024], in0=kr, in1=vi,
                                    op=ALU.mult)
                ch.wait("v", "v", ch.mark("v", i))
                i = V.tensor_tensor(out=kr, in0=ki_, in1=vi, op=ALU.mult)
                ch.wait("v", "v", ch.mark("v", i))
                i = V.tensor_tensor(out=vr, in0=ki_, in1=vr, op=ALU.mult)
                ch.wait("v", "v", ch.mark("v", i))
                i = V.tensor_tensor(out=kr, in0=tt2[:, 0:512], in1=kr,
                                    op=ALU.subtract)
                ch.wait("v", "v", ch.mark("v", i))
                i = V.tensor_tensor(out=ki_, in0=tt2[:, 512:1024], in1=vr,
                                    op=ALU.add)
                ch.wait("v", "v", ch.mark("v", i))
            for idx in range(2 * NKC):
                seg = slice(idx * 512, (idx + 1) * 512)
                init = 0.0 if t == 0 else a["carry"][:, idx:idx + 1]
                i = V.reduce_sum(out=cs[:], in_=G[1][:, seg], axis=AX.X)
                ch.wait("v", "v", ch.mark("v", i))
                i = V.tensor_tensor_scan(
                    out=G[1][:, seg], data0=a["zeros"][:], data1=G[1][:, seg],
                    initial=init, op0=ALU.add, op1=ALU.add)
                ch.wait("v", "v", ch.mark("v", i))
                if t == 0:
                    i = V.tensor_copy(out=a["carry"][:, idx:idx + 1], in_=cs[:])
                else:
                    i = V.tensor_tensor(out=a["carry"][:, idx:idx + 1],
                                        in0=a["carry"][:, idx:idx + 1],
                                        in1=cs[:], op=ALU.add)
                ch.wait("v", "v", ch.mark("v", i))
            for kc in range(NKC):
                ksl = slice(kc * 512, (kc + 1) * 512)
                isl = slice(KSEG + kc * 512, KSEG + (kc + 1) * 512)
                qr, qi_ = G[0][:, ksl], G[0][:, isl]
                sre, sim = G[1][:, ksl], G[1][:, isl]
                V.tensor_tensor(out=tt2[:, 0:512], in0=sre, in1=qr, op=ALU.mult)
                i = V.tensor_tensor(out=tt2[:, 512:1024], in0=sim, in1=qr,
                                    op=ALU.mult)
                ch.wait("v", "v", ch.mark("v", i))
                i = V.tensor_tensor(out=qr, in0=sre, in1=qi_, op=ALU.mult)
                ch.wait("v", "v", ch.mark("v", i))
                i = V.tensor_tensor(out=qi_, in0=sim, in1=qi_, op=ALU.mult)
                ch.wait("v", "v", ch.mark("v", i))
                i = V.tensor_tensor(out=G[2][:, ksl], in0=tt2[:, 0:512],
                                    in1=qi_, op=ALU.add)
                ch.wait("v", "v", ch.mark("v", i))
                i = V.tensor_tensor(out=G[2][:, isl], in0=tt2[:, 512:1024],
                                    in1=qr, op=ALU.subtract)
                ch.wait("v", "v", ch.mark("v", i))
            st["z_tok"] = ch.counts["v"]

        def emit_inv(t):
            ch.wait("qs", "t")
            tti1 = ch.dma("qs", ci, aps["idft_c"].rearrange(
                "(c p) n -> p c n", p=128))
            tti2 = ch.dma("qs", si, aps["idft_s"].rearrange(
                "(c p) n -> p c n", p=128))
            ch.wait("t", *tti1)
            ch.wait("t", *tti2)
            ch.wait("t", "v", st["z_tok"])
            o_read = {}
            for nch in range(NCD):
                bk = nch % 2
                osl = ps_f[:, bk * 512: bk * 512 + 512]
                prev = o_read.get(bk) or st["f_read"].get(bk)
                if prev is not None:
                    ch.wait("t", prev[0], prev[1])
                first = True
                for kc in range(NKC):
                    ksl = slice(kc * 512, (kc + 1) * 512)
                    isl = slice(KSEG + kc * 512, KSEG + (kc + 1) * 512)
                    TE.matmul(out=osl,
                              lhsT=ci[:, kc * 1024 + nch * 128:
                                      kc * 1024 + (nch + 1) * 128],
                              rhs=G[2][:, ksl], start=first, stop=False)
                    first = False
                    i = TE.matmul(out=osl,
                                  lhsT=si[:, kc * 1024 + nch * 128:
                                          kc * 1024 + (nch + 1) * 128],
                                  rhs=G[2][:, isl], start=False,
                                  stop=(kc == NKC - 1))
                ot = ch.mark("t", i)
                ch.wait("v", "t", ot)
                xc = X[:, nch * T + t * 512: nch * T + t * 512 + 512]
                i = V.tensor_tensor(out=xc, in0=xc, in1=osl, op=ALU.add)
                o_read[bk] = ("v", ch.mark("v", i))
            mx = ("v", max(o_read[0][1], o_read[1][1]))
            st["f_read"][0] = st["f_read"][1] = mx
            st["inv_tok"] = ch.counts["t"]

        emit_lnproj(0)
        for t in range(NTT):
            emit_fwd(t)
            if t + 1 < NTT:
                emit_lnproj(t + 1)
            emit_vphase(t)
            emit_inv(t)
        ch.barrier()


def build_l1attn_prog(host_map):
    nc = bass.Bass("TRN2", target_bir_lowering=False)
    names = ["x", "l1_bq", "l1_bk", "l1_bv", "l1_wqT", "l1_wkT", "l1_wvT",
             "dft_c", "dft_s", "idft_c", "idft_s"]
    aps = declare_inputs(nc, names, host_map)
    y = nc.dram_tensor("y", [NSEQ, D, T], F32, kind="ExternalOutput").ap()
    ctx = ExitStack()
    with ctx:
        ch = Chains(nc, ctx)
        p = Prog(nc, ctx, ch)
        c = emit_consts(p)
        X = ctx.enter_context(nc.sbuf_tensor("X", [128, NCD * T], F32))
        st_tok = None
        for seq in range(NSEQ):
            ch.barrier()
            if st_tok is not None:
                ch.wait("qs", *st_tok)
            xt = ch.dma("qs", X[:],
                        aps["x"][seq].rearrange("(c p) t -> p c t", p=128))
            for e in ("v", "s", "t", "p"):
                ch.wait(e, *xt)
            emit_l1attn(p, c, X, aps)
            ch.wait("qs", "v")
            ch.wait("qs", "p")
            st_tok = ch.dma("qs", y[seq].rearrange("(c p) t -> p c t", p=128),
                            X[:])
        nc.sync.wait_ge(ch.sems[st_tok[0]], st_tok[1])
    return nc



def _bf16(a):
    return np.ascontiguousarray(a.astype(ml_dtypes.bfloat16))


def _f32(a):
    return np.ascontiguousarray(a.astype(np.float32))


def host_consts():
    c = {}
    c["ident_bf"] = _bf16(np.eye(128))
    c["ident_f"] = _f32(np.eye(128))
    for nm, E in (("selc8", 8), ("selc16", 16)):
        sel = np.zeros((E, E * 128), np.float32)
        for e in range(E):
            sel[e, e * 128:(e + 1) * 128] = 1.0
        c[nm] = _bf16(sel)
    pos = np.arange(T, dtype=np.float64)
    inv = 1.0 / (10000.0 ** (np.arange(0, D, 2, dtype=np.float64) / D))
    pre = pos[:, None] * inv[None, :]
    cat = np.concatenate([np.sin(pre), np.cos(pre)], axis=-1)
    sinemb = cat[::-1]
    c["sinembT"] = _bf16(sinemb.T)
    m = np.zeros((4, 128, 512), np.float32)
    for r in range(4):
        for i in range(128):
            m[r, i, :r * 128 + i + 1] = 1.0
    c["binmask"] = _bf16(m.reshape(4 * 128, 512))
    N = D
    n = np.arange(N, dtype=np.float64)
    k = np.arange(NFREQ, dtype=np.float64)
    ang = 2.0 * np.pi * np.outer(n, k) / N
    c["dft_c"] = _bf16(np.cos(ang))
    c["dft_s"] = _bf16(-np.sin(ang))
    wk = np.zeros(NFREQ)
    wk[0] = 1.0
    wk[1:512] = 2.0
    wk[512] = 1.0
    angi = 2.0 * np.pi * np.outer(k, n) / N
    c["idft_c"] = _bf16((wk[:, None] / N) * np.cos(angi))
    c["idft_s"] = _bf16(-(wk[:, None] / N) * np.sin(angi))
    return c


def shard_inputs(inputs):
    x = np.asarray(inputs["L0_attn_x"])
    cores = []
    consts = host_consts()
    w = {}
    w["rtr"] = _f32(inputs["L0_ffn_router_kernel_DE"])
    w["l0_gk"] = _f32(inputs["L0_ffn_kernel_gating_EDF"])
    w["l0_uk"] = _f32(inputs["L0_ffn_kernel_up_EDF"])
    w["l0_dk"] = _f32(inputs["L0_ffn_kernel_down_EFD"])
    w["gln"] = _f32(np.asarray(inputs["L0_attn_g_ln"]).reshape(NCD, 128).T)
    w["wq"] = _f32(inputs["L0_attn_W_q"])
    w["wk"] = _f32(inputs["L0_attn_W_k"])
    w["wv"] = _f32(inputs["L0_attn_W_v"])
    w["wg"] = _f32(inputs["L0_attn_W_g"])
    w["wr"] = _f32(inputs["L0_attn_W_r"])
    w["xlu"] = _f32(np.asarray(inputs["L0_attn_xl_u"]).reshape(128, 1))
    w["xlv"] = _f32(np.asarray(inputs["L0_attn_xl_v"]).reshape(128, 1))
    w["wo"] = _f32(inputs["L0_attn_W_o"])
    w["l1_wqT"] = _f32(np.asarray(inputs["L1_attn_Wq"]).T)
    w["l1_wkT"] = _f32(np.asarray(inputs["L1_attn_Wk"]).T)
    w["l1_wvT"] = _f32(np.asarray(inputs["L1_attn_Wv"]).T)
    w["l1_bq"] = _f32(np.asarray(inputs["L1_attn_bq"]).reshape(NCD, 128).T)
    w["l1_bk"] = _f32(np.asarray(inputs["L1_attn_bk"]).reshape(NCD, 128).T)
    w["l1_bv"] = _f32(np.asarray(inputs["L1_attn_bv"]).reshape(NCD, 128).T)
    w["l1_gwT"] = _f32(np.asarray(inputs["L1_ffn_gate_w"]).T)
    w["l1_ebias"] = _f32(np.asarray(inputs["L1_ffn_expert_bias"]).reshape(1, 16))
    w["l1_gk"] = _f32(np.transpose(np.asarray(inputs["L1_ffn_w_gate"]), (0, 2, 1)))
    w["l1_uk"] = _f32(np.transpose(np.asarray(inputs["L1_ffn_w_up"]), (0, 2, 1)))
    w["l1_dk"] = _f32(np.transpose(np.asarray(inputs["L1_ffn_w_down"]), (0, 2, 1)))
    w["l1_sg"] = _f32(np.asarray(inputs["L1_ffn_shared_gate_w"]).T)
    w["l1_su"] = _f32(np.asarray(inputs["L1_ffn_shared_up_w"]).T)
    w["l1_sd"] = _f32(np.asarray(inputs["L1_ffn_shared_down_w"]).T)
    for ci in range(NCORE):
        xs = x[ci * NSEQ:(ci + 1) * NSEQ]
        x_fm = np.ascontiguousarray(xs.transpose(0, 2, 1))
        m = {"x": _f32(x_fm)}
        m.update(consts)
        m.update(w)
        cores.append(m)
    return cores


def unshard_output(results):
    outs = []
    for r in results:
        y = r["y"]
        outs.append(np.ascontiguousarray(np.transpose(y, (0, 2, 1))))
    return np.concatenate(outs, axis=0).astype(np.float32)



def declare_inputs(nc, names, host_map):
    aps = {}
    for nm in names:
        a = host_map[nm]
        dt = BF16 if a.dtype == ml_dtypes.bfloat16 else F32
        aps[nm] = nc.dram_tensor(nm, list(a.shape), dt, kind="ExternalInput").ap()
    return aps


def build_ffn_prog(layer, host_map):
    nc = bass.Bass("TRN2", target_bir_lowering=False)
    if layer == "l0":
        wnames = ["rtr", "l0_gk", "l0_uk", "l0_dk"]
    else:
        wnames = ["l1_gwT", "l1_ebias", "l1_gk", "l1_uk", "l1_dk",
                  "l1_sg", "l1_su", "l1_sd"]
    names = ["x", "ident_bf", "ident_f", "selc8", "selc16"] + wnames
    aps = declare_inputs(nc, names, host_map)
    y = nc.dram_tensor("y", [NSEQ, D, T], F32, kind="ExternalOutput").ap()

    ctx = ExitStack()
    with ctx:
        ch = Chains(nc, ctx)
        p = Prog(nc, ctx, ch)
        c = emit_consts(p)
        c["ident_bf"] = ctx.enter_context(nc.sbuf_tensor("identb", [128, 128], BF16))
        ctoks = [ch.dma("qs", c["ident_bf"][:], aps["ident_bf"][:])]
        c["ident_f"] = ctx.enter_context(nc.sbuf_tensor("identf", [128, 128], F32))
        ctoks.append(ch.dma("qs", c["ident_f"][:], aps["ident_f"][:]))
        if layer == "l0":
            wd = {"gk": aps["l0_gk"], "uk": aps["l0_uk"], "dk": aps["l0_dk"],
                  "selc": aps["selc8"], "rw": aps["rtr"]}
        else:
            wd = {"gk": aps["l1_gk"], "uk": aps["l1_uk"], "dk": aps["l1_dk"],
                  "sg": aps["l1_sg"], "su": aps["l1_su"], "sd": aps["l1_sd"],
                  "selc": aps["selc16"], "rw": aps["l1_gwT"],
                  "ebias": aps["l1_ebias"]}

        X = ctx.enter_context(nc.sbuf_tensor("X", [128, NCD * T], F32))
        c["consts_tok"] = ctoks
        st_tok = None
        for seq in range(NSEQ):
            ch.barrier()
            if st_tok is not None:
                ch.wait("qs", *st_tok)
            xt = ch.dma("qs", X[:],
                        aps["x"][seq].rearrange("(c p) t -> p c t", p=128))
            ch.wait("v", *xt)
            ch.wait("s", *xt)
            ch.wait("t", *xt)
            ch.wait("p", *xt)
            emit_ffn(p, c, X, wd, layer)
            ch.wait("qs", "p")
            ch.wait("qs", "v")
            st_tok = ch.dma("qs", y[seq].rearrange("(c p) t -> p c t", p=128),
                            X[:])
        nc.sync.wait_ge(ch.sems[st_tok[0]], st_tok[1])
    return nc



def build_full_prog(host_map):
    nc = bass.Bass("TRN2", target_bir_lowering=False)
    names = sorted(set(
        ["x", "ident_bf", "ident_f", "selc8", "selc16", "rtr", "l0_gk",
         "l0_uk", "l0_dk",
         "gln", "xlu", "xlv", "binmask", "sinembT", "wq", "wk", "wv", "wg",
         "wr", "wo", "l1_gwT", "l1_ebias", "l1_gk", "l1_uk", "l1_dk", "l1_sg",
         "l1_su", "l1_sd", "l1_bq", "l1_bk", "l1_bv", "l1_wqT", "l1_wkT",
         "l1_wvT", "dft_c", "dft_s", "idft_c", "idft_s"]))
    aps = declare_inputs(nc, names, host_map)
    y = nc.dram_tensor("y", [NSEQ, D, T], F32, kind="ExternalOutput").ap()

    ctx = ExitStack()
    with ctx:
        ch = Chains(nc, ctx)
        p = Prog(nc, ctx, ch)
        c = emit_consts(p)
        c["ident_bf"] = ctx.enter_context(nc.sbuf_tensor("identb", [128, 128], BF16))
        ctoks = [ch.dma("qs", c["ident_bf"][:], aps["ident_bf"][:])]
        c["ident_f"] = ctx.enter_context(nc.sbuf_tensor("identf", [128, 128], F32))
        ctoks.append(ch.dma("qs", c["ident_f"][:], aps["ident_f"][:]))
        c["consts_tok"] = ctoks
        wd0 = {"gk": aps["l0_gk"], "uk": aps["l0_uk"], "dk": aps["l0_dk"],
               "selc": aps["selc8"], "rw": aps["rtr"]}
        wd1 = {"gk": aps["l1_gk"], "uk": aps["l1_uk"], "dk": aps["l1_dk"],
               "sg": aps["l1_sg"], "su": aps["l1_su"], "sd": aps["l1_sd"],
               "selc": aps["selc16"], "rw": aps["l1_gwT"],
               "ebias": aps["l1_ebias"]}

        X = ctx.enter_context(nc.sbuf_tensor("X", [128, NCD * T], F32))
        st_tok = None
        for seq in range(NSEQ):
            ch.barrier()
            if st_tok is not None:
                ch.wait("qs", *st_tok)
            xt = ch.dma("qs", X[:],
                        aps["x"][seq].rearrange("(c p) t -> p c t", p=128))
            for e in ("v", "s", "t", "p"):
                ch.wait(e, *xt)
            emit_ffn(p, c, X, wd0, "l0")
            emit_l0attn(p, c, X, aps)
            emit_ffn(p, c, X, wd0, "l0")
            emit_ffn(p, c, X, wd1, "l1")
            emit_l1attn(p, c, X, aps)
            emit_ffn(p, c, X, wd1, "l1")
            ch.wait("qs", "v")
            ch.wait("qs", "p")
            st_tok = ch.dma("qs", y[seq].rearrange("(c p) t -> p c t", p=128),
                            X[:])
        nc.sync.wait_ge(ch.sems[st_tok[0]], st_tok[1])
    return nc



FFN_L0_NAMES = ["x", "ident_bf", "ident_f", "selc8", "selc16", "rtr",
                "l0_gk", "l0_uk", "l0_dk"]
FFN_L1_NAMES = ["x", "ident_bf", "ident_f", "selc8", "selc16", "l1_gwT",
                "l1_ebias", "l1_gk", "l1_uk", "l1_dk", "l1_sg", "l1_su",
                "l1_sd"]
ATTN_L0_NAMES = ["x", "ident_bf", "gln", "xlu", "xlv", "binmask", "sinembT",
                 "wq", "wk", "wv", "wg", "wr", "wo"]
ATTN_L1_NAMES = ["x", "l1_bq", "l1_bk", "l1_bv", "l1_wqT", "l1_wkT", "l1_wvT",
                 "dft_c", "dft_s", "idft_c", "idft_s"]

_PROGS = {}


def _get_prog(kind, hm):
    if kind in _PROGS:
        return _PROGS[kind]
    if kind == "ffn_l0":
        nc, names = build_ffn_prog("l0", hm), FFN_L0_NAMES
    elif kind == "ffn_l1":
        nc, names = build_ffn_prog("l1", hm), FFN_L1_NAMES
    elif kind == "attn_l0":
        nc, names = build_l0attn_prog(hm), ATTN_L0_NAMES
    else:
        nc, names = build_l1attn_prog(hm), ATTN_L1_NAMES
    _PROGS[kind] = (nc, names)
    return _PROGS[kind]


FULL_NAMES = sorted(set(
    ["x", "ident_bf", "ident_f", "selc8", "selc16", "rtr", "l0_gk",
     "l0_uk", "l0_dk",
     "gln", "xlu", "xlv", "binmask", "sinembT", "wq", "wk", "wv", "wg",
     "wr", "wo", "l1_gwT", "l1_ebias", "l1_gk", "l1_uk", "l1_dk", "l1_sg",
     "l1_su", "l1_sd", "l1_bq", "l1_bk", "l1_bv", "l1_wqT", "l1_wkT",
     "l1_wvT", "dft_c", "dft_s", "idft_c", "idft_s"]))

FUSED = True


def kernel(**inputs):
    cores = shard_inputs(inputs)
    if FUSED:
        if "full" not in _PROGS:
            _PROGS["full"] = (build_full_prog(cores[0]), FULL_NAMES)
        nc, names = _PROGS["full"]
        in_maps = [{nm: cores[ci][nm] for nm in names} for ci in range(NCORE)]
        res = run_bass_kernel_spmd(nc, in_maps, core_ids=list(range(NCORE)))
        return unshard_output([{"y": np.asarray(r["y"])} for r in res.results])
    xs = [m["x"] for m in cores]
    seqplan = ["ffn_l0", "attn_l0", "ffn_l0", "ffn_l1", "attn_l1", "ffn_l1"]
    for kind in seqplan:
        nc, names = _get_prog(kind, cores[0])
        in_maps = []
        for ci in range(NCORE):
            m = {nm: cores[ci][nm] for nm in names if nm != "x"}
            m["x"] = xs[ci]
            in_maps.append(m)
        res = run_bass_kernel_spmd(nc, in_maps, core_ids=list(range(NCORE)))
        xs = [np.asarray(r["y"]) for r in res.results]
    return unshard_output([{"y": x} for x in xs])


# version 08
# speedup vs baseline: 1.1398x; 1.0949x over previous
import numpy as np
import ml_dtypes
from contextlib import ExitStack
import dataclasses

import concourse.bass as bass
import concourse.mybir as mybir
from concourse.bass_utils import run_bass_kernel_spmd

F32 = mybir.dt.float32
BF16 = mybir.dt.bfloat16
U32 = mybir.dt.uint32
AF = mybir.ActivationFunctionType
ALU = mybir.AluOpType
AX = mybir.AxisListType

D = 1024
NCD = 8
T = 2048
NSEQ = 2
TT = 512
NTT = T // TT
NCORE = 8

LN_EPS = 1e-5
RMS_EPS = 1e-6
L0_E, L0_F, L0_FC = 8, 2048, 16
L1_E, L1_F, L1_FC = 16, 1024, 8
NFREQ = 640
NKC = 5



NQ = 8


class Chains:

    def __init__(self, nc, ctx):
        self.nc = nc
        self.sems = {}
        self.counts = {}
        self.waited = {}
        self.dma_rr = 0
        names = ["v", "s", "t", "p"] + [f"q{i}" for i in range(NQ)]
        for name in names:
            self.sems[name] = ctx.enter_context(nc.semaphore("ch_" + name))
            self.counts[name] = 0

    def eng(self, chn):
        return {"v": self.nc.vector, "s": self.nc.scalar, "t": self.nc.tensor,
                "p": self.nc.gpsimd, "qs": self.nc.sync, "qc": self.nc.scalar}[chn]

    def mark(self, chn, inst, n=1):
        inst.then_inc(self.sems[chn], n)
        self.counts[chn] += n
        return self.counts[chn]

    def wait(self, waiter, producer, tok=None):
        if tok is None:
            tok = self.counts[producer]
        if tok <= 0:
            return
        key = (waiter, producer)
        if self.waited.get(key, -1) >= tok:
            return
        self.eng(waiter).wait_ge(self.sems[producer], tok)
        self.waited[key] = tok

    def dma(self, issuer, out, in_, **kw):
        name = f"q{self.dma_rr}"
        self.dma_rr = (self.dma_rr + 1) % NQ
        self.wait(issuer, name)
        inst = self.eng(issuer).dma_start(out=out, in_=in_, **kw)
        inst.then_inc(self.sems[name], 16)
        self.counts[name] += 16
        return (name, self.counts[name])

    def barrier(self):
        chans = ["v", "s", "t", "p"]
        qs = [f"q{i}" for i in range(NQ)]
        for w in chans + ["qs"]:
            for pr in chans + qs:
                self.wait(w, pr)



class Prog:

    def __init__(self, nc, ctx, ch):
        self.nc = nc
        self.ctx = ctx
        self.ch = ch
        self.sb = lambda name, shape, dt: ctx.enter_context(
            nc.sbuf_tensor(name, shape, dt))
        self.uid = 0

    def name(self, base):
        self.uid += 1
        return f"{base}_{self.uid}"


def diag_ap(base_ap, tensor_cols, offset, nrows, ncols):
    return dataclasses.replace(
        base_ap, offset=offset, ap=[[tensor_cols - 1, nrows], [1, ncols]])



def emit_consts(p):
    nc, ch = p.nc, p.ch
    c = {}
    c["ones_col_bf"] = p.sb("ones_col_bf", [128, 1], BF16)
    c["ones_row_f"] = p.sb("ones_row_f", [1, 128], F32)
    c["eps_ln"] = p.sb("eps_ln", [1, 1], F32)
    c["eps_rms"] = p.sb("eps_rms", [1, 1], F32)
    nc.vector.memset(c["ones_col_bf"][:], 1.0)
    nc.vector.memset(c["ones_row_f"][:], 1.0)
    nc.vector.memset(c["eps_ln"][:], LN_EPS)
    i = nc.vector.memset(c["eps_rms"][:], RMS_EPS)
    ch.mark("v", i)
    return c



def xview3(X, t):
    return dataclasses.replace(
        X[:], offset=t * TT, ap=[[NCD * T, 128], [T, NCD], [1, TT]])


def _half(v, h):
    return dataclasses.replace(
        v, offset=v.offset + h * (NCD // 2) * v.ap[1][0],
        ap=[v.ap[0], [v.ap[1][0], NCD // 2], v.ap[2]])


def cview(tile):
    return dataclasses.replace(
        tile[:], ap=[[NCD * 512, 128], [512, NCD], [1, 512]])


def bview(tile, n):
    return dataclasses.replace(tile[:], ap=[tile[:].ap[0], [0, n], [1, TT]])


def emit_ln_tile(p, c, X, t, xn_view, scr, rms_fold=False):
    nc, ch = p.nc, p.ch
    V, S, TE, P = nc.vector, nc.scalar, nc.tensor, nc.gpsimd
    xb, xsq, ps_stat, ps_b, ab_sb, bb_sb, t1 = (
        scr["xb"], scr["xsq"], scr["ps_stat"], scr["ps_b"],
        scr["ab_sb"], scr["bb_sb"], scr["t1"])
    rb = scr["rowbuf"]
    s1, s2 = rb[0:1, 0:TT], rb[0:1, TT:2 * TT]
    aa, bb = rb[0:1, 2 * TT:3 * TT], rb[0:1, 3 * TT:4 * TT]

    if scr.get("reuse_tok_t") is not None:
        ch.wait("p", "t", scr["reuse_tok_t"])
        ch.wait("s", "t", scr["reuse_tok_t"])
    cast_tok = ch.mark("p", P.tensor_copy(out=xb[:], in_=xview3(X, t)))
    sq_tok = ch.mark("s", S.activation(out=xsq[:], in_=xview3(X, t),
                                       func=AF.Square))

    ch.wait("t", "p", cast_tok)
    ch.wait("t", "s", sq_tok)
    if scr.get("stat_read_tok") is not None:
        ch.wait("t", "v", scr["stat_read_tok"])
    for cc in range(NCD):
        TE.matmul(out=ps_stat[0:1, 0:TT], lhsT=c["ones_col_bf"][:],
                  rhs=xb[:, cc * TT:(cc + 1) * TT],
                  start=(cc == 0), stop=(cc == NCD - 1))
    for cc in range(NCD):
        i = TE.matmul(out=ps_stat[64:65, 0:TT], lhsT=c["ones_col_bf"][:],
                      rhs=xsq[:, cc * TT:(cc + 1) * TT],
                      start=(cc == 0), stop=(cc == NCD - 1),
                      skip_group_check=True)
    stat_tok = ch.mark("t", i)

    ch.wait("v", "t", stat_tok)
    ch.mark("v", V.tensor_scalar(out=s1, in0=ps_stat[0:1, 0:TT],
                                 scalar1=1.0 / D, scalar2=None, op0=ALU.mult))
    srd = ch.mark("v", V.tensor_scalar(out=s2, in0=ps_stat[64:65, 0:TT],
                                       scalar1=1.0 / D, scalar2=None,
                                       op0=ALU.mult))
    scr["stat_read_tok"] = srd
    ch.wait("v", "v")
    ch.mark("v", V.tensor_tensor(out=bb, in0=s1, in1=s1, op=ALU.mult))
    ch.wait("v", "v")
    var_tok = ch.mark("v", V.tensor_tensor(out=s2, in0=s2, in1=bb, op=ALU.subtract))
    ch.wait("s", "v", var_tok)
    lt = ch.mark("s", S.activation(out=aa, in_=s2, func=AF.Ln, bias=c["eps_ln"][:]))
    ch.wait("s", "s", lt)
    sq2_tok = ch.mark("s", S.activation(out=aa, in_=aa, func=AF.Exp, scale=-0.5))
    ch.wait("v", "s", sq2_tok)
    if rms_fold:
        ch.mark("v", V.tensor_tensor(out=s2, in0=s2, in1=aa, op=ALU.mult))
        ch.wait("v", "v")
        rt = ch.mark("v", V.tensor_tensor(out=s2, in0=s2, in1=aa, op=ALU.mult))
        ch.wait("s", "v", rt)
        lt = ch.mark("s", S.activation(out=s2, in_=s2, func=AF.Ln,
                                       bias=c["eps_rms"][:]))
        ch.wait("s", "s", lt)
        st = ch.mark("s", S.activation(out=s2, in_=s2, func=AF.Exp, scale=-0.5))
        ch.wait("v", "s", st)
        ch.mark("v", V.tensor_tensor(out=aa, in0=aa, in1=s2, op=ALU.mult))
        ch.wait("v", "v")
    ch.mark("v", V.tensor_tensor(out=bb, in0=s1, in1=aa, op=ALU.mult))
    ch.wait("v", "v")
    row_tok = ch.mark("v", V.tensor_scalar(out=bb, in0=bb, scalar1=-1.0,
                                           scalar2=None, op0=ALU.mult))

    ch.wait("t", "v", row_tok)
    if scr.get("bcast_read_tok") is not None:
        ch.wait("t", "v", scr["bcast_read_tok"])
    TE.matmul(out=ps_b[:, 0:TT], lhsT=c["ones_row_f"][:], rhs=aa, start=True,
              stop=True)
    i = TE.matmul(out=ps_b[:, TT:2 * TT], lhsT=c["ones_row_f"][:], rhs=bb,
                  start=True, stop=True)
    bc_tok = ch.mark("t", i)

    ch.wait("v", "t", bc_tok)
    if scr.get("pass2_tok") is not None:
        ch.wait("v", "p", scr["pass2_tok"])
    ch.mark("v", V.tensor_copy(out=ab_sb[:], in_=ps_b[:, 0:TT]))
    evb_tok = ch.mark("v", V.tensor_copy(out=bb_sb[:], in_=ps_b[:, TT:2 * TT]))
    scr["bcast_read_tok"] = evb_tok

    ch.wait("v", "v", evb_tok)
    p2_tok = None
    for h in range(2):
        if h == 1 or scr.get("pass2_tok") is not None:
            ch.wait("v", "p", p2_tok if h == 1 else scr["pass2_tok"])
        i = V.tensor_tensor(out=t1[:], in0=_half(xview3(X, t), h),
                            in1=bview(ab_sb, NCD // 2), op=ALU.mult)
        p1 = ch.mark("v", i)
        ch.wait("p", "v", p1)
        i = P.tensor_tensor(out=_half(xn_view, h), in0=t1[:],
                            in1=bview(bb_sb, NCD // 2), op=ALU.add)
        p2_tok = ch.mark("p", i)
    scr["reuse_tok_t"] = stat_tok
    scr["pass2_tok"] = p2_tok
    return p2_tok


def emit_ln_scratch(p, c, sctx, sep_stat=False):
    sb = lambda name, shape, dt: sctx.enter_context(
        p.nc.sbuf_tensor(p.name(name), shape, dt))
    ps = lambda name, shape, dt: sctx.enter_context(
        p.nc.psum_tensor(p.name(name), shape, dt))
    scr = {
        "rowbuf": sb("ln_rowbuf", [1, 4 * TT], F32),
        "xb": sb("ln_xb", [128, NCD * TT], BF16),
        "xsq": sb("ln_xsq", [128, NCD * TT], BF16),
        "t1": sb("ln_t1", [128, (NCD // 2) * TT], F32),
        "ab_sb": sb("ln_ab", [128, TT], F32),
        "bb_sb": sb("ln_bb", [128, TT], F32),
        "ps_b": ps("ln_ps_b", [128, 2 * TT], F32),
        "reuse_tok_t": None,
        "stat_read_tok": None,
        "bcast_read_tok": None,
        "pass2_tok": None,
    }
    if sep_stat:
        scr["ps_stat"] = ps("ln_ps_stat", [128, TT], F32)
    else:
        scr["ps_stat"] = scr["ps_b"]
    return scr



class WLoader:

    def __init__(self, p, sctx, tag, cols, nbuf=2, nstage=1):
        self.p = p
        self.cols = cols
        self.nbuf = nbuf
        self.nstage = nstage
        sb = lambda name, shape, dt: sctx.enter_context(
            p.nc.sbuf_tensor(p.name(name), shape, dt))
        self.stage = sb(f"{tag}_stg", [128, nstage * cols], F32)
        self.bf = sb(f"{tag}_bf", [128, nbuf * cols], BF16)
        self.slot_cast_tok = [None] * nbuf
        self.slot_read_tok = [None] * nbuf
        self.seq = 0

    def load(self, src_ap, read_done=None, cast="p", scale_aps=None,
             chunk_cols=128, ncols=None):
        p, ch, nc = self.p, self.p.ch, self.p.nc
        n = self.cols if ncols is None else ncols
        slot = self.seq % self.nbuf
        sslot = self.seq % self.nstage
        self.seq += 1
        st = self.stage[:, sslot * self.cols: sslot * self.cols + n]
        bf = self.bf[:, slot * self.cols: slot * self.cols + n]
        prev = self.seq - 1 - self.nstage
        if prev >= 0:
            cc, tt = self.slot_cast_tok[prev % self.nbuf]
            ch.wait("qs", cc, tt)
        dma_tok = ch.dma("qs", st, src_ap)
        ch.wait("s", *dma_tok)
        if self.slot_read_tok[slot] is not None:
            cc, tt = self.slot_read_tok[slot]
            ch.wait("s", cc, tt)
        if scale_aps is None:
            i = nc.scalar.copy(out=bf, in_=st)
        else:
            n = self.cols // chunk_cols
            for k in range(n):
                i = nc.scalar.activation(
                    out=bf[:, k * chunk_cols:(k + 1) * chunk_cols],
                    in_=st[:, k * chunk_cols:(k + 1) * chunk_cols],
                    func=AF.Copy, scale=scale_aps[k])
        tok = ch.mark("s", i)
        self.slot_cast_tok[slot] = ("s", tok)
        return slot, ("s", tok)

    def view(self, slot, ncols=None):
        n = self.cols if ncols is None else ncols
        return self.bf[:, slot * self.cols: slot * self.cols + n]

    def set_read(self, slot, chain, tok):
        self.slot_read_tok[slot] = (chain, tok)



def emit_ffn(p, c, X, w, layer):
    nc, ch = p.nc, p.ch
    V, S, TE, P = nc.vector, nc.scalar, nc.tensor, nc.gpsimd
    E = L0_E if layer == "l0" else L1_E
    FC = L0_FC if layer == "l0" else L1_FC

    sctx = ExitStack()
    with sctx:
        sb = lambda name, shape, dt: sctx.enter_context(
            nc.sbuf_tensor(p.name(name), shape, dt))
        ps = lambda name, shape, dt: sctx.enter_context(
            nc.psum_tensor(p.name(name), shape, dt))

        XN = sb("ffn_xn", [128, NCD * T], BF16)
        E16 = E if E >= 8 else 8
        selc = sb("selc", [E16, E * 128], BF16)
        sc_tok = ch.dma("qs", selc[:], w["selc"][:])
        rw = sb("rw", [128, NCD * E], BF16)
        rw_st = sb("rw_st", [128, NCD * E], F32)
        tk = ch.dma("qs", rw_st[:], w["rw"].rearrange("(c p) e -> p c e", p=128))
        ch.wait("p", *tk)
        ch.wait("p", "t")
        rw_tok = ch.mark("p", nc.gpsimd.tensor_copy(out=rw[:], in_=rw_st[:]))
        if layer == "l1":
            ebias_sb = sb("ebias_sb", [1, 16], F32)
            eb_tok = ch.dma("qs", ebias_sb[:], w["ebias"][:])

        xn_toks = []
        scr = emit_ln_scratch(p, c, sctx, sep_stat=True)
        for t in range(NTT):
            xn_toks.append(emit_ln_tile(p, c, X, t, xview3(XN, t), scr))

        nsl = T // 128
        c_fm = sb("c_fm", [E if E >= 8 else 8, nsl * 128], BF16)
        c_all = sb("c_all", [128, nsl * E], F32)
        r_sc = sb("r_sc", [128, 16 + 16 + 4], F32)
        r_sc2 = sb("r_sc2", [128, 16], F32)
        if layer == "l1":
            bias_b = sb("bias_b", [128, 16], F32)
        ps_cb = ps("ps_cb", [128, 512], F32)
        for tk_ in c["consts_tok"]:
            ch.wait("t", *tk_)
        ch.wait("t", "p", rw_tok)
        for sl_ in range(nsl):
            ch.wait("t", "p", xn_toks[sl_ // 4])
            for dc in range(NCD):
                i = TE.matmul(out=ps_cb[:, sl_ * 16: sl_ * 16 + E],
                              lhsT=XN[:, dc * T + sl_ * 128: dc * T + (sl_ + 1) * 128],
                              rhs=rw[:, dc * E:(dc + 1) * E],
                              start=(dc == 0), stop=(dc == NCD - 1))
        lg_tok = ch.mark("t", i)

        def emit_router_coeffs():
            if layer == "l1":
                ch.wait("t", *eb_tok)
                i = TE.matmul(out=ps_cb[:, 496:512], lhsT=c["ones_row_f"][:],
                              rhs=ebias_sb[0:1, :], start=True, stop=True,
                              skip_group_check=True)
                bt = ch.mark("t", i)
                ch.wait("v", "t", bt)
                bias_ev = ch.mark("v", V.tensor_copy(out=bias_b[:],
                                                     in_=ps_cb[:, 496:512]))
                ch.wait("v", "v")
                cb_guard = ("v", bias_ev)
            else:
                cb_guard = None
            ch.wait("v", "t")
            ch.wait("s", "t")
            for s_ in range(nsl):
                lgs = ps_cb[:, s_ * 16: s_ * 16 + E]
                logits = r_sc[:, 0:E]
                probs = r_sc[:, 16:16 + E]
                vals = r_sc2[:, 0:8]
                nmax = r_sc[:, 32:33]
                den = r_sc[:, 33:34]
                cdst = c_all[:, s_ * E: (s_ + 1) * E]
                ch.wait("v", "s")
                if layer == "l0":
                    ch.mark("v", V.reduce_max(out=nmax, in_=lgs, axis=AX.X,
                                              negate=True))
                    ch.wait("s", "v")
                    et = ch.mark("s", S.activation(out=logits, in_=lgs,
                                                   func=AF.Exp, bias=nmax))
                    ch.wait("v", "s", et)
                    ch.mark("v", V.reduce_sum(out=den, in_=logits, axis=AX.X))
                    ch.wait("v", "v")
                    ch.mark("v", V.reciprocal(out=den, in_=den))
                    ch.wait("v", "v")
                    ch.mark("v", V.tensor_scalar(out=probs, in0=logits,
                                                 scalar1=den, scalar2=None,
                                                 op0=ALU.mult))
                    ch.wait("v", "v")
                    ch.mark("v", V.max(out=vals, in_=probs))
                    ch.wait("v", "v")
                    ch.mark("v", V.tensor_scalar(out=logits, in0=probs,
                                                 scalar1=vals[:, 1:2],
                                                 scalar2=None, op0=ALU.is_ge))
                    ch.wait("s", "v")
                    et = ch.mark("s", S.activation(out=r_sc2[:, 8:8 + E],
                                                   in_=probs, func=AF.Exp))
                    ch.wait("v", "s", et)
                    ch.mark("v", V.tensor_tensor(out=probs,
                                                 in0=r_sc2[:, 8:8 + E],
                                                 in1=logits, op=ALU.mult))
                    ch.wait("v", "v")
                    ch.mark("v", V.reduce_sum(out=den, in_=probs, axis=AX.X))
                    ch.wait("v", "v")
                    ch.mark("v", V.reciprocal(out=den, in_=den))
                    ch.wait("v", "v")
                    ch.mark("v", V.tensor_scalar(out=cdst, in0=probs,
                                                 scalar1=den, scalar2=0.5,
                                                 op0=ALU.mult, op1=ALU.mult))
                else:
                    ch.wait("s", "v")
                    st = ch.mark("s", S.activation(out=probs, in_=lgs,
                                                   func=AF.Sigmoid))
                    ch.wait("v", "s", st)
                    ch.mark("v", V.tensor_tensor(out=logits, in0=probs,
                                                 in1=bias_b[:], op=ALU.add))
                    ch.wait("v", "v")
                    ch.mark("v", V.max(out=vals, in_=logits))
                    ch.wait("v", "v")
                    ch.mark("v", V.tensor_scalar(out=logits, in0=logits,
                                                 scalar1=vals[:, 7:8],
                                                 scalar2=None, op0=ALU.is_ge))
                    ch.wait("v", "v")
                    ch.mark("v", V.tensor_tensor(out=probs, in0=probs,
                                                 in1=logits, op=ALU.mult))
                    ch.wait("v", "v")
                    ch.mark("v", V.reduce_sum(out=den, in_=probs, axis=AX.X))
                    ch.wait("v", "v")
                    ch.mark("v", V.reciprocal(out=den, in_=den))
                    ch.wait("v", "v")
                    ch.mark("v", V.tensor_scalar(out=cdst, in0=probs,
                                                 scalar1=den,
                                                 scalar2=2.826 * 0.5,
                                                 op0=ALU.mult, op1=ALU.mult))
            cw_tok = ch.counts["v"]
            ch.wait("t", "v", cw_tok)
            if cb_guard is not None:
                ch.wait("t", cb_guard[0], cb_guard[1])
            tr_rd = None
            for s_ in range(nsl):
                if tr_rd is not None:
                    ch.wait("t", "v", tr_rd)
                i = TE.transpose(out=ps_cb[0:E, 0:128],
                                 in_=c_all[:, s_ * E: (s_ + 1) * E],
                                 identity=c["ident_f"][:])
                tt = ch.mark("t", i)
                ch.wait("v", "t", tt)
                tr_rd = ch.mark("v", V.tensor_copy(
                    out=c_fm[:, s_ * 128:(s_ + 1) * 128],
                    in_=ps_cb[0:E, 0:128]))
            nonlocal cb_read
            cb_read = ("v", tr_rd)

        fuse = sb("fuse", [128, (FC // 2) * T], BF16)
        act = sb("act", [128, TT], BF16)
        csb = sb("csb", [128, NTT * TT], BF16)
        dtmp = sb("dtmp", [128, TT], F32)
        ps_g = ps("ps_g", [128, 2 * 512], F32)
        ps_u = ps("ps_u", [128, 2 * 512], F32)
        ps_d = scr["ps_b"]
        wl_g = WLoader(p, sctx, "wg", NCD * 128, nbuf=3)
        wl_u = WLoader(p, sctx, "wu", NCD * 128)

        gu_read = {}
        d_read = {}
        act_read = {}
        fuse_rd_tok = None
        csb_read = None
        dtmp_read = {}
        x_acc = {}
        gk = w["gk"]
        uk = w["uk"]
        dk = w["dk"]

        n_exp = E + (1 if layer == "l1" else 0)
        NH = 2
        FCH = FC // NH
        cb_read = None
        tr_done = None

        def w_src(e, fc, kind):
            if e == E:
                m = {"g": w["sg"], "u": w["su"]}
                if kind == "d":
                    return None
                return m[kind][:, fc * 128:(fc + 1) * 128].rearrange(
                    "(c p) f -> p c f", p=128)
            m = {"g": gk, "u": uk}
            return m[kind][e, :, fc * 128:(fc + 1) * 128].rearrange(
                "(c p) f -> p c f", p=128)

        def d_src(e, half, dc):
            r0, r1 = half * FCH * 128, (half + 1) * FCH * 128
            if e == E:
                return w["sd"][r0:r1, dc * 128:(dc + 1) * 128].rearrange(
                    "(c p) f -> p c f", p=128)
            return dk[e, r0:r1, dc * 128:(dc + 1) * 128].rearrange(
                "(c p) f -> p c f", p=128)

        plan = []
        for e_ in range(n_exp):
            for half_ in range(NH):
                for fcl_ in range(FCH):
                    fc_ = half_ * FCH + fcl_
                    plan.append((wl_g, w_src(e_, fc_, "g"), None))
                    plan.append((wl_u, w_src(e_, fc_, "u"), None))
                for dc_ in range(NCD):
                    plan.append((wl_g, d_src(e_, half_, dc_), FCH * 128))
        _lp = {"issued": 0, "h": {}}

        def lp_use(i):
            while _lp["issued"] <= min(i + 2, len(plan) - 1):
                j = _lp["issued"]
                ld, src, nc_ = plan[j]
                _lp["h"][j] = (ld, ld.load(src, ncols=nc_))
                _lp["issued"] += 1
            return _lp["h"].pop(i)

        def emit_csb(e, shared):
            nonlocal cb_read, csb_read
            if shared:
                if csb_read is not None:
                    ch.wait("v", csb_read[0], csb_read[1])
                rdy = ch.mark("v", V.memset(csb[:], 0.5))
                csb_read = None
                return rdy
            for t in range(NTT):
                ch.wait("t", "v", tr_done)
                ch.wait("t", *sc_tok)
                if cb_read is not None:
                    ch.wait("t", cb_read[0], cb_read[1])
                sel = selc[:, e * 128:(e + 1) * 128]
                i = TE.matmul(out=ps_cb[:, 0:TT], lhsT=sel,
                              rhs=c_fm[:, t * TT:(t + 1) * TT],
                              start=True, stop=True)
                bt = ch.mark("t", i)
                ch.wait("v", "t", bt)
                if csb_read is not None:
                    ch.wait("v", csb_read[0], csb_read[1])
                cb_tok = ch.mark("v", V.tensor_copy(
                    out=csb[:, t * TT:(t + 1) * TT], in_=ps_cb[:, 0:TT]))
                cb_read = ("v", cb_tok)
            csb_read = None
            return ch.counts["v"]

        for e in range(n_exp):
            shared = (e == E)
            eplan0 = e * NH * (2 * FCH + NCD)
            for half in range(NH):
                step = 0
                hplan0 = eplan0 + half * (2 * FCH + NCD)
                for fcl in range(FCH):
                    if e == 0 and half == 0 and fcl == 1:
                        emit_router_coeffs()
                        tr_done = ch.counts["v"]
                    fc = half * FCH + fcl
                    _, (sg, gtok) = lp_use(hplan0 + 2 * fcl)
                    _, (su, utok) = lp_use(hplan0 + 2 * fcl + 1)
                    wgb, wub = wl_g.view(sg), wl_u.view(su)
                    for t in range(NTT):
                        bk = step % 2
                        gsl = ps_g[:, bk * 512: bk * 512 + TT]
                        usl = ps_u[:, bk * 512: bk * 512 + TT]
                        ch.wait("t", "p", xn_toks[t])
                        ch.wait("t", gtok[0], gtok[1])
                        ch.wait("t", utok[0], utok[1])
                        if bk in gu_read:
                            ch.wait("t", gu_read[bk][0], gu_read[bk][1])
                        for dc in range(NCD):
                            TE.matmul(out=gsl,
                                      lhsT=wgb[:, dc * 128:(dc + 1) * 128],
                                      rhs=XN[:, dc * T + t * TT: dc * T + t * TT + TT],
                                      start=(dc == 0), stop=(dc == NCD - 1))
                        for dc in range(NCD):
                            i = TE.matmul(out=usl,
                                          lhsT=wub[:, dc * 128:(dc + 1) * 128],
                                          rhs=XN[:, dc * T + t * TT: dc * T + t * TT + TT],
                                          start=(dc == 0), stop=(dc == NCD - 1))
                        mm_tok = ch.mark("t", i)
                        asl = act[:, 0:TT]
                        ch.wait("s", "t", mm_tok)
                        if 0 in act_read:
                            ch.wait("s", act_read[0][0], act_read[0][1])
                        sa = ch.mark("s", S.activation(out=asl, in_=gsl, func=AF.Silu))
                        ch.wait("v", "s", sa)
                        if fuse_rd_tok is not None:
                            ch.wait("v", "t", fuse_rd_tok)
                        fv = ch.mark("v", V.tensor_tensor(
                            out=fuse[:, fcl * T + t * TT: fcl * T + t * TT + TT],
                            in0=asl, in1=usl, op=ALU.mult))
                        act_read[0] = ("v", fv)
                        gu_read[bk] = ("v", fv)
                        step += 1
                    wl_g.set_read(sg, "t", ch.counts["t"])
                    wl_u.set_read(su, "t", ch.counts["t"])
                fuse_v_tok = ch.counts["v"]
                if half == 0:
                    csb_ready = emit_csb(e, shared)

                dbanks = [ps_d[:, 0:512], ps_d[:, 512:1024],
                          ps_g[:, 0:512], ps_g[:, 512:1024],
                          ps_u[:, 0:512], ps_u[:, 512:1024]]
                dstep = 0
                for dc in range(NCD):
                    _, (sd_, dtok) = lp_use(hplan0 + 2 * FCH + dc)
                    wdb = wl_g.view(sd_, ncols=FCH * 128)
                    for t in range(NTT):
                        bk = dstep % 6
                        dsl = dbanks[bk]
                        ch.wait("t", dtok[0], dtok[1])
                        ch.wait("t", "v", fuse_v_tok)
                        prev = d_read.get(bk)
                        if prev is None or prev[1] is None:
                            if bk < 2:
                                ch.wait("t", "v", scr["bcast_read_tok"])
                            elif bk < 4:
                                gr = gu_read.get(0)
                                if gr:
                                    ch.wait("t", gr[0], gr[1])
                            else:
                                gr = gu_read.get(1)
                                if gr:
                                    ch.wait("t", gr[0], gr[1])
                        else:
                            ch.wait("t", prev[0], prev[1])
                        for fcl in range(FCH):
                            i = TE.matmul(
                                out=dsl,
                                lhsT=wdb[:, fcl * 128:(fcl + 1) * 128],
                                rhs=fuse[:, fcl * T + t * TT: fcl * T + t * TT + TT],
                                start=(fcl == 0), stop=(fcl == FCH - 1))
                        dmm_tok = ch.mark("t", i)
                        tsl = dtmp[:, 0:TT]
                        ch.wait("v", "t", dmm_tok)
                        ch.wait("v", "v", csb_ready)
                        if 0 in dtmp_read:
                            ch.wait("v", dtmp_read[0][0], dtmp_read[0][1])
                        tv = ch.mark("v", V.tensor_tensor(
                            out=tsl, in0=dsl, in1=csb[:, t * TT:(t + 1) * TT],
                            op=ALU.mult))
                        d_read[bk] = ("v", tv)
                        csb_read = ("v", tv)
                        ch.wait("p", "v", tv)
                        if (dc, t) in x_acc:
                            ch.wait("p", "p", x_acc[(dc, t)])
                        xc = X[:, dc * T + t * TT: dc * T + t * TT + TT]
                        px = ch.mark("p", P.tensor_tensor(out=xc, in0=xc, in1=tsl,
                                                          op=ALU.add))
                        x_acc[(dc, t)] = px
                        dtmp_read[0] = ("p", px)
                        dstep += 1
                    wl_g.set_read(sd_, "t", ch.counts["t"])
                for gb, pair in ((0, (2, 4)), (1, (3, 5))):
                    toks = [d_read[i][1] for i in pair if i in d_read]
                    if toks:
                        cur = gu_read.get(gb, ("v", 0))[1]
                        gu_read[gb] = ("v", max([cur] + toks))
                fuse_rd_tok = ch.counts["t"]
        ch.barrier()



def emit_l0attn_setup(p, c, aps, sctx):
    nc, ch = p.nc, p.ch
    V, S, TE, P = nc.vector, nc.scalar, nc.tensor, nc.gpsimd
    sb = lambda name, shape, dt: sctx.enter_context(
        nc.sbuf_tensor(p.name(name), shape, dt))
    a = {}
    a["gln"] = sb("gln", [128, NCD], F32)
    a["glnq"] = sb("glnq", [128, NCD], F32)
    a["xlu"] = sb("xlu", [128, 1], F32)
    a["xlv"] = sb("xlv", [128, 1], F32)
    a["binmask"] = sb("binmask", [128, 4 * 512], BF16)
    a["wq_bf"] = sb("wq_bf", [128, NCD * 128], BF16)
    a["wk_bf"] = sb("wk_bf", [128, NCD * 128], BF16)
    a["r_sb"] = sb("r_sb", [128, T], BF16)

    t1 = ch.dma("qs", a["gln"][:], aps["gln"][:])
    ch.dma("qs", a["xlu"][:], aps["xlu"][:])
    ch.dma("qs", a["xlv"][:], aps["xlv"][:])
    ch.dma("qs", a["binmask"][:],
           aps["binmask"].rearrange("(r p) j -> p r j", p=128))
    ch.wait("v", *t1)
    glt = ch.mark("v", V.tensor_scalar(out=a["glnq"][:], in0=a["gln"][:],
                                       scalar1=float(1.0 / np.sqrt(128.0)),
                                       scalar2=None, op0=ALU.mult))

    setup_ctx = ExitStack()
    setup_ctx.__enter__()
    stg = setup_ctx.enter_context(
        nc.sbuf_tensor(p.name("wqk_stg"), [128, NCD * 128], F32))
    for nm, dst, svec, vtok in (("wq", a["wq_bf"], a["glnq"], glt),
                                ("wk", a["wk_bf"], a["gln"], t1)):
        tk = ch.dma("qs", stg[:], aps[nm].rearrange("(c p) k -> p c k", p=128))
        ch.wait("s", *tk)
        ch.wait("s", "v", glt)
        for dc in range(NCD):
            i = S.activation(out=dst[:, dc * 128:(dc + 1) * 128],
                             in_=stg[:, dc * 128:(dc + 1) * 128],
                             func=AF.Copy, scale=svec[:, dc:dc + 1])
        stok = ch.mark("s", i)
        ch.wait("qs", "s", stok)

    wr_bf = setup_ctx.enter_context(
        nc.sbuf_tensor(p.name("wr_bf"), [128, NCD * 128], BF16))
    tk = ch.dma("qs", stg[:], aps["wr"].rearrange("(c p) k -> p c k", p=128))
    ch.wait("p", *tk)
    wrt = ch.mark("p", P.tensor_copy(out=wr_bf[:], in_=stg[:]))

    sin_st = setup_ctx.enter_context(
        nc.sbuf_tensor(p.name("sin_st"), [128, NCD * 512], BF16))
    pctx = ExitStack()
    with pctx:
        ps_r = pctx.enter_context(nc.psum_tensor(p.name("ps_rsb"), [128, 512], F32))
        ev_tok = None
        for pg in range(4):
            tk = ch.dma("qs", sin_st[:],
                        aps["sinembT"].rearrange("(c p) t -> p c t", p=128)
                        [:, :, pg * 512:(pg + 1) * 512])
            ch.wait("t", *tk)
            ch.wait("t", "p", wrt)
            if ev_tok is not None:
                ch.wait("t", "v", ev_tok)
            for dc in range(NCD):
                i = TE.matmul(out=ps_r[:, 0:512],
                              lhsT=wr_bf[:, dc * 128:(dc + 1) * 128],
                              rhs=sin_st[:, dc * 512:(dc + 1) * 512],
                              start=(dc == 0), stop=(dc == NCD - 1))
            mt = ch.mark("t", i)
            ch.wait("v", "t", mt)
            ev_tok = ch.mark("v", V.tensor_copy(
                out=a["r_sb"][:, pg * 512:(pg + 1) * 512], in_=ps_r[:, 0:512]))
            ch.wait("qs", "v", ev_tok)
        ch.barrier()
    setup_ctx.close()
    return a


def emit_l0attn(p, c, X, aps):
    nc, ch = p.nc, p.ch
    V, S, TE, P = nc.vector, nc.scalar, nc.tensor, nc.gpsimd
    sctx = ExitStack()
    with sctx:
        sb = lambda name, shape, dt: sctx.enter_context(
            nc.sbuf_tensor(p.name(name), shape, dt))
        ps = lambda name, shape, dt: sctx.enter_context(
            nc.psum_tensor(p.name(name), shape, dt))

        a = emit_l0attn_setup(p, c, aps, sctx)
        A_fm = sb("A_fm", [128, NCD * T], BF16)
        s01ctx = ExitStack()
        s01ctx.__enter__()
        sb01 = lambda name, shape, dt: s01ctx.enter_context(
            nc.sbuf_tensor(p.name(name), shape, dt))
        qu = sb01("qu", [128, T], BF16)
        qv = sb01("qv", [128, T], BF16)
        k_sb = sb01("k_sb", [128, T], BF16)
        xn_tm = sb01("xn_tm", [128, 16 * 1024], BF16)

        l0ctx = ExitStack()
        with l0ctx:
            scr = emit_ln_scratch(p, c, l0ctx, sep_stat=True)
            xn_t = l0ctx.enter_context(
                nc.sbuf_tensor(p.name("xn_t"), [128, NCD * 512], BF16))
            ps_qk = l0ctx.enter_context(
                nc.psum_tensor(p.name("ps_qk"), [128, 1024], F32))
            ps_tp = l0ctx.enter_context(
                nc.psum_tensor(p.name("ps_tp"), [128, 1024], BF16))
            qk_read = {}
            tp_read = None
            xn_read = [None]
            for t in range(NTT):
                slot = 0
                base = 0
                if xn_read[slot] is not None:
                    ch.wait("p", "t", xn_read[slot])
                xn_tok = emit_ln_tile(p, c, X, t, cview(xn_t), scr,
                                      rms_fold=True)
                for which, (wbf, bank) in enumerate(
                        ((a["wq_bf"], 0), (a["wk_bf"], 1))):
                    sl = ps_qk[:, bank * 512: bank * 512 + 512]
                    ch.wait("t", "p", xn_tok)
                    if bank in qk_read:
                        ch.wait("t", qk_read[bank][0], qk_read[bank][1])
                    for dc in range(NCD):
                        i = TE.matmul(out=sl, lhsT=wbf[:, dc * 128:(dc + 1) * 128],
                                      rhs=xn_t[:, base + dc * 512: base + dc * 512 + 512],
                                      start=(dc == 0), stop=(dc == NCD - 1))
                    mt = ch.mark("t", i)
                    ch.wait("s", "t", mt)
                    if which == 0:
                        S.activation(out=qu[:, t * 512:(t + 1) * 512], in_=sl,
                                     func=AF.Identity, bias=a["xlu"][:])
                        i = S.activation(out=qv[:, t * 512:(t + 1) * 512], in_=sl,
                                         func=AF.Identity, bias=a["xlv"][:])
                    else:
                        i = S.activation(out=k_sb[:, t * 512:(t + 1) * 512],
                                         in_=sl, func=AF.Identity)
                    qk_read[bank] = ("s", ch.mark("s", i))
                for sl4 in range(4):
                    if tp_read is not None:
                        ch.wait("t", "v", tp_read)
                    ch.wait("t", "p", xn_tok)
                    for dc in range(NCD):
                        i = TE.transpose(
                            out=ps_tp[:, dc * 128:(dc + 1) * 128],
                            in_=xn_t[:, base + dc * 512 + sl4 * 128:
                                     base + dc * 512 + sl4 * 128 + 128],
                            identity=c["ident_bf"][:])
                    tt = ch.mark("t", i)
                    ch.wait("v", "t", tt)
                    tp_read = ch.mark("v", V.tensor_copy(
                        out=xn_tm[:, (t * 4 + sl4) * 1024:(t * 4 + sl4 + 1) * 1024],
                        in_=ps_tp[:]))
                xn_read[slot] = ch.counts["t"]
            ch.barrier()

        l1ctx = ExitStack()
        with l1ctx:
            sb1 = lambda name, shape, dt: l1ctx.enter_context(
                nc.sbuf_tensor(p.name(name), shape, dt))
            raw = sb1("raw", [128, 2560], BF16)
            shifted = sb1("shifted", [128, T], BF16)
            s_sb = sb1("s_sb", [128, 2 * T], F32)
            w_sb = sb1("w_sb", [128, 2 * T], BF16)
            wT_sb = sb1("wT_sb", [128, 16 * 128], BF16)
            a_tm = sb1("a_tm", [128, 1024], BF16)
            rows = sb1("rows", [128, 4], F32)
            ps_pool = l1ctx.enter_context(
                nc.psum_tensor(p.name("ps_pool"), [128, 2048], F32))
            ps_tp1 = l1ctx.enter_context(
                nc.psum_tensor(p.name("ps_tp1"), [128, 1024], BF16))
            ps_a = l1ctx.enter_context(
                nc.psum_tensor(p.name("ps_a"), [128, 1024], F32))

            zt = ch.mark("v", V.memset(raw[:, 2048:2560], 0.0))
            st1 = {
                "bank_read": {}, "shift_read": None, "wT_read": None,
                "wsb_read": {}, "tp1_read": None, "a_read": None,
                "atm_read": None, "raw_dma": None, "s_tok": {}, "exp": {},
            }

            def emit_scores(qi):
                I0 = qi * 128
                ng = qi // 4 + 1
                g0 = (2047 - I0 - 127) // 512
                buf = qi % 2
                br = st1["bank_read"]
                for g in range(g0, 4):
                    if g in br:
                        ch.wait("t", br[g][0], br[g][1])
                    i = TE.matmul(out=ps_pool[:, g * 512:(g + 1) * 512],
                                  lhsT=qv[:, I0:I0 + 128],
                                  rhs=a["r_sb"][:, g * 512:(g + 1) * 512],
                                  start=True, stop=True)
                    br.pop(g, None)
                bd_tok = ch.mark("t", i)
                ch.wait("v", "t", bd_tok)
                ch.wait("v", "v", zt)
                if st1["raw_dma"] is not None:
                    ch.wait("v", *st1["raw_dma"])
                for g in range(g0, 4):
                    i = V.tensor_copy(out=raw[:, g * 512:(g + 1) * 512],
                                      in_=ps_pool[:, g * 512:(g + 1) * 512])
                raw_tok = ch.mark("v", i)
                ch.wait("qs", "v", raw_tok)
                if st1["shift_read"] is not None:
                    ch.wait("qs", "v", st1["shift_read"])
                dtok = ch.dma("qs", shifted[:, 0:ng * 512],
                              diag_ap(raw[:], 2560, 2047 - I0, 128, ng * 512))
                st1["raw_dma"] = dtok
                for g in range(ng):
                    ch.wait("t", "v", raw_tok)
                    if g in br:
                        ch.wait("t", br[g][0], br[g][1])
                    i = TE.matmul(out=ps_pool[:, g * 512:(g + 1) * 512],
                                  lhsT=qu[:, I0:I0 + 128],
                                  rhs=k_sb[:, g * 512:(g + 1) * 512],
                                  start=True, stop=True)
                ac_tok = ch.mark("t", i)
                ch.wait("v", "t", ac_tok)
                ch.wait("v", *dtok)
                ch.wait("v", "s")
                for g in range(ng):
                    i = V.tensor_tensor(
                        out=s_sb[:, buf * T + g * 512: buf * T + (g + 1) * 512],
                        in0=ps_pool[:, g * 512:(g + 1) * 512],
                        in1=shifted[:, g * 512:(g + 1) * 512],
                        op=ALU.add)
                    br[g] = ("v", ch.mark("v", i))
                st1["s_tok"][qi] = ch.counts["v"]
                st1["shift_read"] = ch.counts["v"]

            def emit_out(qi):
                I0 = qi * 128
                ng = qi // 4 + 1
                nj = qi + 1
                buf = qi % 2
                ssl = s_sb[:, buf * T: buf * T + ng * 512]
                wsl = w_sb[:, buf * T: buf * T + ng * 512]
                nmax = rows[:, 2 * buf: 2 * buf + 1]
                rden = rows[:, 2 * buf + 1: 2 * buf + 2]
                ch.wait("v", "v", st1["s_tok"][qi])
                ch.wait("v", "s")
                nm_tok = ch.mark("v", V.reduce_max(out=nmax, in_=ssl,
                                                   axis=AX.X, negate=True))
                ch.wait("s", "v", nm_tok)
                wg = st1["wsb_read"].get(buf)
                if wg is not None:
                    ch.wait("s", "t", wg)
                e_tok = ch.mark("s", S.activation(out=wsl, in_=ssl,
                                                  func=AF.Exp, bias=nmax))
                st1["exp"][qi] = e_tok
                ch.wait("v", "s", e_tok)
                m_tok = ch.mark("v", V.tensor_tensor(
                    out=w_sb[:, buf * T + (ng - 1) * 512: buf * T + ng * 512],
                    in0=w_sb[:, buf * T + (ng - 1) * 512: buf * T + ng * 512],
                    in1=a["binmask"][:, (qi % 4) * 512:(qi % 4 + 1) * 512],
                    op=ALU.mult))
                ch.wait("v", "v", m_tok)
                ch.mark("v", V.reduce_sum(out=rden, in_=wsl, axis=AX.X))
                ch.wait("v", "v")
                rd_tok = ch.mark("v", V.reciprocal(out=rden, in_=rden))
                for j0 in range(0, nj, 8):
                    jn = min(8, nj - j0)
                    ch.wait("t", "v", m_tok)
                    ch.wait("t", "s", e_tok)
                    if st1["tp1_read"] is not None:
                        ch.wait("t", "v", st1["tp1_read"])
                    for jj in range(jn):
                        i = TE.transpose(
                            out=ps_tp1[:, jj * 128:(jj + 1) * 128],
                            in_=w_sb[:, buf * T + (j0 + jj) * 128:
                                     buf * T + (j0 + jj + 1) * 128],
                            identity=c["ident_bf"][:])
                    tt = ch.mark("t", i)
                    st1["wsb_read"][buf] = tt
                    ch.wait("v", "t", tt)
                    st1["wT_read"] = ch.mark("v", V.tensor_copy(
                        out=wT_sb[:, j0 * 128:(j0 + jn) * 128],
                        in_=ps_tp1[:, 0:jn * 128]))
                    st1["tp1_read"] = st1["wT_read"]
                ch.wait("t", "v", st1["wT_read"])
                if st1["atm_read"] is not None:
                    ch.wait("t", "s", st1["atm_read"])
                for dh in range(2):
                    for j in range(nj):
                        i = TE.matmul(
                            out=ps_a[:, dh * 512:(dh + 1) * 512],
                            lhsT=wT_sb[:, j * 128:(j + 1) * 128],
                            rhs=xn_tm[:, j * 1024 + dh * 512:
                                      j * 1024 + dh * 512 + 512],
                            start=(j == 0), stop=(j == nj - 1))
                a_tok = ch.mark("t", i)
                ch.wait("s", "t", a_tok)
                ch.wait("s", "v", rd_tok)
                if st1["a_read"] is not None:
                    ch.wait("s", "t", st1["a_read"])
                i = S.activation(out=a_tm[:, 0:512], in_=ps_a[:, 0:512],
                                 func=AF.Copy, scale=rden)
                i = S.activation(out=a_tm[:, 512:1024], in_=ps_a[:, 512:1024],
                                 func=AF.Copy, scale=rden)
                st1["atm_read"] = ch.mark("s", i)
                ch.wait("t", "s", st1["atm_read"])
                if st1["tp1_read"] is not None:
                    ch.wait("t", "v", st1["tp1_read"])
                for dc in range(NCD):
                    i = TE.transpose(out=ps_tp1[:, dc * 128:(dc + 1) * 128],
                                     in_=a_tm[:, dc * 128:(dc + 1) * 128],
                                     identity=c["ident_bf"][:])
                st1["a_read"] = ch.mark("t", i)
                ch.wait("v", "t", st1["a_read"])
                dst = dataclasses.replace(
                    A_fm[:], offset=qi * 128,
                    ap=[[NCD * T, 128], [T, NCD], [1, 128]])
                st1["tp1_read"] = ch.mark("v", V.tensor_copy(out=dst,
                                                             in_=ps_tp1[:]))
            emit_scores(0)
            for qi in range(1, 16):
                emit_scores(qi)
                emit_out(qi - 1)
            emit_out(15)
            ch.barrier()
        s01ctx.close()

        l2ctx = ExitStack()
        with l2ctx:
            scr = emit_ln_scratch(p, c, l2ctx)
            xn_t = l2ctx.enter_context(
                nc.sbuf_tensor(p.name("xn_t2"), [128, NCD * 512], BF16))
            gt_sb = l2ctx.enter_context(
                nc.sbuf_tensor(p.name("gt_sb"), [128, 2 * 512], BF16))
            wvg = l2ctx.enter_context(
                nc.sbuf_tensor(p.name("wvg"), [128, 16 * 512], BF16))
            ps_wv = l2ctx.enter_context(
                nc.psum_tensor(p.name("ps_wv"), [128, 1024], F32))
            ps_gt = l2ctx.enter_context(
                nc.psum_tensor(p.name("ps_gt"), [128, 1024], F32))
            wl_wv = WLoader(p, l2ctx, "wwv", NCD * 128)
            wl_wg = WLoader(p, l2ctx, "wwg", NCD * 128)
            wl_wo = WLoader(p, l2ctx, "wwo", 8 * 128)
            ps_o = scr["ps_b"]
            gscale = [a["gln"][:, dc:dc + 1] for dc in range(NCD)]

            def vg_src(vc):
                return (aps["wv"][:, vc * 128:(vc + 1) * 128].rearrange(
                            "(c p) f -> p c f", p=128),
                        aps["wg"][:, vc * 128:(vc + 1) * 128].rearrange(
                            "(c p) f -> p c f", p=128))

            def wo_src(idx):
                dc_, vh_ = idx // 2, idx % 2
                return aps["wo"][vh_ * 1024:(vh_ + 1) * 1024,
                                 dc_ * 128:(dc_ + 1) * 128].rearrange(
                    "(c p) f -> p c f", p=128)

            v0_, g0_ = vg_src(0)
            vg_nxt = [(wl_wv.load(v0_, cast="s", scale_aps=gscale),
                       wl_wg.load(g0_, cast="s", scale_aps=gscale))]
            wo_nxt = [wl_wo.load(wo_src(0))]
            wv_read = {}
            gt_read = {}
            gts_read = {}
            o_read = {}
            wvg_rd = None
            xn2_read = None
            for t in range(NTT):
                if xn2_read is not None:
                    ch.wait("p", "t", xn2_read)
                xn_tok = emit_ln_tile(p, c, X, t, cview(xn_t), scr,
                                      rms_fold=True)
                step = 0
                for vc in range(16):
                    (sv, vtok), (sg, gtok) = vg_nxt[0]
                    vn_, gn_ = vg_src((vc + 1) % 16)
                    vg_nxt[0] = (wl_wv.load(vn_, cast="s", scale_aps=gscale),
                                 wl_wg.load(gn_, cast="s", scale_aps=gscale))
                    bk = step % 2
                    wsl = ps_wv[:, bk * 512: bk * 512 + 512]
                    gsl = ps_gt[:, bk * 512: bk * 512 + 512]
                    ch.wait("t", vtok[0], vtok[1])
                    if bk in wv_read:
                        ch.wait("t", wv_read[bk][0], wv_read[bk][1])
                    for dc in range(NCD):
                        i = TE.matmul(
                            out=wsl,
                            lhsT=wl_wv.view(sv)[:, dc * 128:(dc + 1) * 128],
                            rhs=A_fm[:, dc * T + t * 512: dc * T + t * 512 + 512],
                            start=(dc == 0), stop=(dc == NCD - 1))
                    wv_tok = ch.mark("t", i)
                    ch.wait("t", gtok[0], gtok[1])
                    ch.wait("t", "p", xn_tok)
                    if bk in gt_read:
                        ch.wait("t", gt_read[bk][0], gt_read[bk][1])
                    for dc in range(NCD):
                        i = TE.matmul(
                            out=gsl,
                            lhsT=wl_wg.view(sg)[:, dc * 128:(dc + 1) * 128],
                            rhs=xn_t[:, dc * 512:(dc + 1) * 512],
                            start=(dc == 0), stop=(dc == NCD - 1))
                    gt_tok = ch.mark("t", i)
                    wl_wv.set_read(sv, "t", wv_tok)
                    wl_wg.set_read(sg, "t", gt_tok)
                    ch.wait("s", "t", gt_tok)
                    if bk in gts_read:
                        ch.wait("s", gts_read[bk][0], gts_read[bk][1])
                    st = ch.mark("s", S.activation(
                        out=gt_sb[:, bk * 512: bk * 512 + 512], in_=gsl,
                        func=AF.Silu))
                    ch.wait("v", "s", st)
                    ch.wait("v", "t", wv_tok)
                    if wvg_rd is not None:
                        ch.wait("v", "t", wvg_rd)
                    i = V.tensor_tensor(out=wvg[:, vc * 512:(vc + 1) * 512],
                                        in0=gt_sb[:, bk * 512: bk * 512 + 512],
                                        in1=wsl, op=ALU.mult)
                    fv = ch.mark("v", i)
                    wv_read[bk] = ("v", fv)
                    gt_read[bk] = ("v", fv)
                    gts_read[bk] = ("v", fv)
                    step += 1
                wvg_tok = ch.counts["v"]
                xn2_read = ch.counts["t"]
                for dc in range(NCD):
                    bk = dc % 2
                    osl = ps_o[:, bk * 512: bk * 512 + 512]
                    if dc < 2:
                        o_read.setdefault(bk, ("v", scr["bcast_read_tok"]))
                    for vh in range(2):
                        so, otok = wo_nxt[0]
                        wo_nxt[0] = wl_wo.load(wo_src((dc * 2 + vh + 1) % 16))
                        ch.wait("t", otok[0], otok[1])
                        ch.wait("t", "v", wvg_tok)
                        if vh == 0 and bk in o_read:
                            ch.wait("t", o_read[bk][0], o_read[bk][1])
                        for vcl in range(8):
                            vc = vh * 8 + vcl
                            i = TE.matmul(
                                out=osl,
                                lhsT=wl_wo.view(so)[:, vcl * 128:(vcl + 1) * 128],
                                rhs=wvg[:, vc * 512:(vc + 1) * 512],
                                start=(vc == 0), stop=(vc == 15))
                        o_tok = ch.mark("t", i)
                        wl_wo.set_read(so, "t", o_tok)
                    ch.wait("v", "t", o_tok)
                    xc = X[:, dc * T + t * 512: dc * T + t * 512 + 512]
                    i = V.tensor_tensor(out=xc, in0=xc, in1=osl, op=ALU.add)
                    o_read[bk] = ("v", ch.mark("v", i))
                wvg_rd = ch.counts["t"]
                scr["bcast_read_tok"] = ch.counts["v"]
                scr["stat_read_tok"] = ch.counts["v"]
                o_read = {}
            ch.barrier()


def build_l0attn_prog(host_map):
    nc = bass.Bass("TRN2", target_bir_lowering=False)
    names = ["x", "ident_bf", "gln", "xlu", "xlv", "binmask", "sinembT",
             "wq", "wk", "wv", "wg", "wr", "wo"]
    aps = declare_inputs(nc, names, host_map)
    y = nc.dram_tensor("y", [NSEQ, D, T], F32, kind="ExternalOutput").ap()
    ctx = ExitStack()
    with ctx:
        ch = Chains(nc, ctx)
        p = Prog(nc, ctx, ch)
        c = emit_consts(p)
        c["ident_bf"] = ctx.enter_context(nc.sbuf_tensor("identb", [128, 128], BF16))
        c["consts_tok"] = [ch.dma("qs", c["ident_bf"][:], aps["ident_bf"][:])]
        X = ctx.enter_context(nc.sbuf_tensor("X", [128, NCD * T], F32))
        st_tok = None
        for seq in range(NSEQ):
            ch.barrier()
            if st_tok is not None:
                ch.wait("qs", *st_tok)
            xt = ch.dma("qs", X[:],
                        aps["x"][seq].rearrange("(c p) t -> p c t", p=128))
            for e in ("v", "s", "t", "p"):
                ch.wait(e, *xt)
            emit_l0attn(p, c, X, aps)
            ch.wait("qs", "v")
            ch.wait("qs", "p")
            st_tok = ch.dma("qs", y[seq].rearrange("(c p) t -> p c t", p=128),
                            X[:])
        nc.sync.wait_ge(ch.sems[st_tok[0]], st_tok[1])
    return nc



def emit_l1attn_setup(p, c, aps, sctx):
    nc, ch = p.nc, p.ch
    sb = lambda name, shape, dt: sctx.enter_context(
        nc.sbuf_tensor(p.name(name), shape, dt))
    a = {}
    for nm in ("bq", "bk", "bv"):
        a[nm] = sb(nm, [128, NCD], F32)
        tk = ch.dma("qs", a[nm][:], aps["l1_" + nm][:])
        ch.wait("s", *tk)
    a["carry"] = sb("carry", [128, 16], F32)
    a["zeros"] = sb("zeros", [128, 512], BF16)
    a["eps8"] = sb("eps8", [128, 1], F32)
    nc.vector.memset(a["zeros"][:], 0.0)
    ch.mark("v", nc.vector.memset(a["eps8"][:], 1e-8))
    ch.wait("s", "v")
    return a


def emit_l1attn(p, c, X, aps):
    nc, ch = p.nc, p.ch
    V, S, TE, P = nc.vector, nc.scalar, nc.tensor, nc.gpsimd
    KSEG = NKC * 512

    sctx = ExitStack()
    with sctx:
        sb = lambda name, shape, dt: sctx.enter_context(
            nc.sbuf_tensor(p.name(name), shape, dt))
        a = emit_l1attn_setup(p, c, aps, sctx)
        tb = sb("dft_tb", [128, 10240], BF16)
        cf = tb[:, 0:NCD * NFREQ]
        sf = tb[:, NCD * NFREQ:2 * NCD * NFREQ]
        ci = tb[:, 0:NKC * 1024]
        si = tb[:, NKC * 1024:2 * NKC * 1024]
        qkv = sb("qkv", [128, 3 * NCD * 512], BF16)
        wl_p = WLoader(p, sctx, "wp", NCD * 128)
        scr = emit_ln_scratch(p, c, sctx, sep_stat=False)
        xn_t = sb("xn_t1a", [128, NCD * 512], BF16)
        G = [sb(f"G{i_}", [128, 2 * KSEG], BF16) for i_ in range(3)]
        m2 = sb("m2", [128, KSEG], F32)
        tt2 = sb("tt2", [128, 1024], F32)
        sq_scr = tt2[:, 512:1024]
        cs = sb("cs", [128, 1], F32)
        ps_p = sctx.enter_context(nc.psum_tensor(p.name("ps_p"), [128, 1024], F32))
        ps_f = sctx.enter_context(nc.psum_tensor(p.name("ps_f"), [128, 2048], F32))

        sigs = (("l1_wqT", a["bq"]), ("l1_wkT", a["bk"]), ("l1_wvT", a["bv"]))

        def pj_src(idx):
            return aps[sigs[idx // NCD][0]][
                :, (idx % NCD) * 128:(idx % NCD + 1) * 128].rearrange(
                "(c p) f -> p c f", p=128)

        st = {"p_read": {}, "f_read": {}, "xn_read": None, "qkv_read": None,
              "proj_tok": None, "inv_tok": None, "z_tok": None, "pnxt": None}

        def emit_lnproj(t):
            if st["xn_read"] is not None:
                ch.wait("p", "t", st["xn_read"])
            xn_tok = emit_ln_tile(p, c, X, t, cview(xn_t), scr)
            if st["pnxt"] is None:
                st["pnxt"] = wl_p.load(pj_src(0))
            step = 0
            for sig, (wnm, bias) in enumerate(sigs):
                for mc in range(NCD):
                    sw, wtok = st["pnxt"]
                    nidx = sig * NCD + mc + 1
                    st["pnxt"] = wl_p.load(pj_src(nidx % (3 * NCD)))
                    bk = step % 2
                    sl = ps_p[:, bk * 512: bk * 512 + 512]
                    ch.wait("t", wtok[0], wtok[1])
                    ch.wait("t", "p", xn_tok)
                    pr = st["p_read"].get(bk)
                    if pr is not None:
                        ch.wait("t", pr[0], pr[1])
                    for dc in range(NCD):
                        i = TE.matmul(
                            out=sl,
                            lhsT=wl_p.view(sw)[:, dc * 128:(dc + 1) * 128],
                            rhs=xn_t[:, dc * 512:(dc + 1) * 512],
                            start=(dc == 0), stop=(dc == NCD - 1))
                    mt = ch.mark("t", i)
                    wl_p.set_read(sw, "t", mt)
                    ch.wait("s", "t", mt)
                    if st["qkv_read"] is not None:
                        ch.wait("s", "t", st["qkv_read"])
                    i = S.activation(
                        out=qkv[:, sig * NCD * 512 + mc * 512:
                                sig * NCD * 512 + (mc + 1) * 512],
                        in_=sl, func=AF.Identity, bias=bias[:, mc:mc + 1])
                    st["p_read"][bk] = ("s", ch.mark("s", i))
                    step += 1
            st["proj_tok"] = ch.counts["s"]
            st["xn_read"] = ch.counts["t"]

        def emit_fwd(t):
            ch.wait("qs", "t")
            ttok1 = ch.dma("qs", cf, aps["dft_c"].rearrange(
                "(c p) k -> p c k", p=128))
            ttok2 = ch.dma("qs", sf, aps["dft_s"].rearrange(
                "(c p) k -> p c k", p=128))
            ch.wait("t", *ttok1)
            ch.wait("t", *ttok2)
            ch.wait("t", "s", st["proj_tok"])
            if st["inv_tok"] is not None:
                ch.wait("v", "t", st["inv_tok"])
            for sig in range(3):
                sbase = sig * NCD * 512
                for kc in range(NKC):
                    bk = kc % 2
                    pre = ps_f[:, bk * 1024: bk * 1024 + 512]
                    pim = ps_f[:, bk * 1024 + 512: bk * 1024 + 1024]
                    for bsl, tbl in ((pre, cf), (pim, sf)):
                        fr = st["f_read"].get(bk)
                        if fr is not None:
                            ch.wait("t", fr[0], fr[1])
                        for dc in range(NCD):
                            i = TE.matmul(
                                out=bsl,
                                lhsT=tbl[:, dc * NFREQ + kc * 128:
                                         dc * NFREQ + (kc + 1) * 128],
                                rhs=qkv[:, sbase + dc * 512:
                                        sbase + (dc + 1) * 512],
                                start=(dc == 0), stop=(dc == NCD - 1))
                    ft = ch.mark("t", i)
                    ch.wait("v", "t", ft)
                    V.tensor_copy(out=G[sig][:, kc * 512:(kc + 1) * 512],
                                  in_=pre)
                    ev = ch.mark("v", V.tensor_copy(
                        out=G[sig][:, KSEG + kc * 512: KSEG + (kc + 1) * 512],
                        in_=pim))
                    ch.wait("s", "t", ft)
                    ch.wait("s", "v")
                    S.activation(out=m2[:, kc * 512:(kc + 1) * 512],
                                 in_=pre, func=AF.Square)
                    stok = ch.mark("s", S.activation(out=sq_scr,
                                                     in_=pim, func=AF.Square))
                    ch.wait("v", "s", stok)
                    mv = ch.mark("v", V.tensor_tensor(
                        out=m2[:, kc * 512:(kc + 1) * 512],
                        in0=m2[:, kc * 512:(kc + 1) * 512],
                        in1=sq_scr, op=ALU.add))
                    st["f_read"][bk] = ("v", mv)
                ch.wait("s", "v")
                stok = ch.mark("s", S.activation(out=m2[:], in_=m2[:],
                                                 func=AF.Sqrt))
                ch.wait("s", "s", stok)
                stok = ch.mark("s", S.activation(out=m2[:], in_=m2[:],
                                                 func=AF.Ln, bias=a["eps8"][:]))
                ch.wait("s", "s", stok)
                stok = ch.mark("s", S.activation(out=m2[:], in_=m2[:],
                                                 func=AF.Exp, scale=-1.0))
                ch.wait("v", "s", stok)
                V.tensor_tensor(out=G[sig][:, 0:KSEG], in0=G[sig][:, 0:KSEG],
                                in1=m2[:], op=ALU.mult)
                gt = ch.mark("v", V.tensor_tensor(
                    out=G[sig][:, KSEG:2 * KSEG],
                    in0=G[sig][:, KSEG:2 * KSEG], in1=m2[:], op=ALU.mult))
                ch.wait("s", "v", gt)

        def emit_vphase(t):
            ch.wait("v", "v")
            for kc in range(NKC):
                ksl = slice(kc * 512, (kc + 1) * 512)
                isl = slice(KSEG + kc * 512, KSEG + (kc + 1) * 512)
                kr, ki_ = G[1][:, ksl], G[1][:, isl]
                vr, vi = G[2][:, ksl], G[2][:, isl]
                V.tensor_tensor(out=tt2[:, 0:512], in0=kr, in1=vr, op=ALU.mult)
                i = V.tensor_tensor(out=tt2[:, 512:1024], in0=kr, in1=vi,
                                    op=ALU.mult)
                ch.wait("v", "v", ch.mark("v", i))
                i = V.tensor_tensor(out=kr, in0=ki_, in1=vi, op=ALU.mult)
                ch.wait("v", "v", ch.mark("v", i))
                i = V.tensor_tensor(out=vr, in0=ki_, in1=vr, op=ALU.mult)
                ch.wait("v", "v", ch.mark("v", i))
                i = V.tensor_tensor(out=kr, in0=tt2[:, 0:512], in1=kr,
                                    op=ALU.subtract)
                ch.wait("v", "v", ch.mark("v", i))
                i = V.tensor_tensor(out=ki_, in0=tt2[:, 512:1024], in1=vr,
                                    op=ALU.add)
                ch.wait("v", "v", ch.mark("v", i))
            for idx in range(2 * NKC):
                seg = slice(idx * 512, (idx + 1) * 512)
                init = 0.0 if t == 0 else a["carry"][:, idx:idx + 1]
                i = V.reduce_sum(out=cs[:], in_=G[1][:, seg], axis=AX.X)
                ch.wait("v", "v", ch.mark("v", i))
                i = V.tensor_tensor_scan(
                    out=G[1][:, seg], data0=a["zeros"][:], data1=G[1][:, seg],
                    initial=init, op0=ALU.add, op1=ALU.add)
                ch.wait("v", "v", ch.mark("v", i))
                if t == 0:
                    i = V.tensor_copy(out=a["carry"][:, idx:idx + 1], in_=cs[:])
                else:
                    i = V.tensor_tensor(out=a["carry"][:, idx:idx + 1],
                                        in0=a["carry"][:, idx:idx + 1],
                                        in1=cs[:], op=ALU.add)
                ch.wait("v", "v", ch.mark("v", i))
            for kc in range(NKC):
                ksl = slice(kc * 512, (kc + 1) * 512)
                isl = slice(KSEG + kc * 512, KSEG + (kc + 1) * 512)
                qr, qi_ = G[0][:, ksl], G[0][:, isl]
                sre, sim = G[1][:, ksl], G[1][:, isl]
                V.tensor_tensor(out=tt2[:, 0:512], in0=sre, in1=qr, op=ALU.mult)
                i = V.tensor_tensor(out=tt2[:, 512:1024], in0=sim, in1=qr,
                                    op=ALU.mult)
                ch.wait("v", "v", ch.mark("v", i))
                i = V.tensor_tensor(out=qr, in0=sre, in1=qi_, op=ALU.mult)
                ch.wait("v", "v", ch.mark("v", i))
                i = V.tensor_tensor(out=qi_, in0=sim, in1=qi_, op=ALU.mult)
                ch.wait("v", "v", ch.mark("v", i))
                i = V.tensor_tensor(out=G[2][:, ksl], in0=tt2[:, 0:512],
                                    in1=qi_, op=ALU.add)
                ch.wait("v", "v", ch.mark("v", i))
                i = V.tensor_tensor(out=G[2][:, isl], in0=tt2[:, 512:1024],
                                    in1=qr, op=ALU.subtract)
                ch.wait("v", "v", ch.mark("v", i))
            st["z_tok"] = ch.counts["v"]

        def emit_inv(t):
            ch.wait("qs", "t")
            tti1 = ch.dma("qs", ci, aps["idft_c"].rearrange(
                "(c p) n -> p c n", p=128))
            tti2 = ch.dma("qs", si, aps["idft_s"].rearrange(
                "(c p) n -> p c n", p=128))
            ch.wait("t", *tti1)
            ch.wait("t", *tti2)
            ch.wait("t", "v", st["z_tok"])
            o_read = {}
            for nch in range(NCD):
                bk = nch % 2
                osl = ps_f[:, bk * 512: bk * 512 + 512]
                prev = o_read.get(bk) or st["f_read"].get(bk)
                if prev is not None:
                    ch.wait("t", prev[0], prev[1])
                first = True
                for kc in range(NKC):
                    ksl = slice(kc * 512, (kc + 1) * 512)
                    isl = slice(KSEG + kc * 512, KSEG + (kc + 1) * 512)
                    TE.matmul(out=osl,
                              lhsT=ci[:, kc * 1024 + nch * 128:
                                      kc * 1024 + (nch + 1) * 128],
                              rhs=G[2][:, ksl], start=first, stop=False)
                    first = False
                    i = TE.matmul(out=osl,
                                  lhsT=si[:, kc * 1024 + nch * 128:
                                          kc * 1024 + (nch + 1) * 128],
                                  rhs=G[2][:, isl], start=False,
                                  stop=(kc == NKC - 1))
                ot = ch.mark("t", i)
                ch.wait("v", "t", ot)
                xc = X[:, nch * T + t * 512: nch * T + t * 512 + 512]
                i = V.tensor_tensor(out=xc, in0=xc, in1=osl, op=ALU.add)
                o_read[bk] = ("v", ch.mark("v", i))
            mx = ("v", max(o_read[0][1], o_read[1][1]))
            st["f_read"][0] = st["f_read"][1] = mx
            st["inv_tok"] = ch.counts["t"]

        emit_lnproj(0)
        for t in range(NTT):
            emit_fwd(t)
            if t + 1 < NTT:
                emit_lnproj(t + 1)
            emit_vphase(t)
            emit_inv(t)
        ch.barrier()


def build_l1attn_prog(host_map):
    nc = bass.Bass("TRN2", target_bir_lowering=False)
    names = ["x", "l1_bq", "l1_bk", "l1_bv", "l1_wqT", "l1_wkT", "l1_wvT",
             "dft_c", "dft_s", "idft_c", "idft_s"]
    aps = declare_inputs(nc, names, host_map)
    y = nc.dram_tensor("y", [NSEQ, D, T], F32, kind="ExternalOutput").ap()
    ctx = ExitStack()
    with ctx:
        ch = Chains(nc, ctx)
        p = Prog(nc, ctx, ch)
        c = emit_consts(p)
        X = ctx.enter_context(nc.sbuf_tensor("X", [128, NCD * T], F32))
        st_tok = None
        for seq in range(NSEQ):
            ch.barrier()
            if st_tok is not None:
                ch.wait("qs", *st_tok)
            xt = ch.dma("qs", X[:],
                        aps["x"][seq].rearrange("(c p) t -> p c t", p=128))
            for e in ("v", "s", "t", "p"):
                ch.wait(e, *xt)
            emit_l1attn(p, c, X, aps)
            ch.wait("qs", "v")
            ch.wait("qs", "p")
            st_tok = ch.dma("qs", y[seq].rearrange("(c p) t -> p c t", p=128),
                            X[:])
        nc.sync.wait_ge(ch.sems[st_tok[0]], st_tok[1])
    return nc



def _bf16(a):
    return np.ascontiguousarray(a.astype(ml_dtypes.bfloat16))


def _f32(a):
    return np.ascontiguousarray(a.astype(np.float32))


def host_consts():
    c = {}
    c["ident_bf"] = _bf16(np.eye(128))
    c["ident_f"] = _f32(np.eye(128))
    for nm, E in (("selc8", 8), ("selc16", 16)):
        sel = np.zeros((E, E * 128), np.float32)
        for e in range(E):
            sel[e, e * 128:(e + 1) * 128] = 1.0
        c[nm] = _bf16(sel)
    pos = np.arange(T, dtype=np.float64)
    inv = 1.0 / (10000.0 ** (np.arange(0, D, 2, dtype=np.float64) / D))
    pre = pos[:, None] * inv[None, :]
    cat = np.concatenate([np.sin(pre), np.cos(pre)], axis=-1)
    sinemb = cat[::-1]
    c["sinembT"] = _bf16(sinemb.T)
    m = np.zeros((4, 128, 512), np.float32)
    for r in range(4):
        for i in range(128):
            m[r, i, :r * 128 + i + 1] = 1.0
    c["binmask"] = _bf16(m.reshape(4 * 128, 512))
    N = D
    n = np.arange(N, dtype=np.float64)
    k = np.arange(NFREQ, dtype=np.float64)
    ang = 2.0 * np.pi * np.outer(n, k) / N
    c["dft_c"] = _bf16(np.cos(ang))
    c["dft_s"] = _bf16(-np.sin(ang))
    wk = np.zeros(NFREQ)
    wk[0] = 1.0
    wk[1:512] = 2.0
    wk[512] = 1.0
    angi = 2.0 * np.pi * np.outer(k, n) / N
    c["idft_c"] = _bf16((wk[:, None] / N) * np.cos(angi))
    c["idft_s"] = _bf16(-(wk[:, None] / N) * np.sin(angi))
    return c


def shard_inputs(inputs):
    x = np.asarray(inputs["L0_attn_x"])
    cores = []
    consts = host_consts()
    w = {}
    w["rtr"] = _f32(inputs["L0_ffn_router_kernel_DE"])
    w["l0_gk"] = _f32(inputs["L0_ffn_kernel_gating_EDF"])
    w["l0_uk"] = _f32(inputs["L0_ffn_kernel_up_EDF"])
    w["l0_dk"] = _f32(inputs["L0_ffn_kernel_down_EFD"])
    w["gln"] = _f32(np.asarray(inputs["L0_attn_g_ln"]).reshape(NCD, 128).T)
    w["wq"] = _f32(inputs["L0_attn_W_q"])
    w["wk"] = _f32(inputs["L0_attn_W_k"])
    w["wv"] = _f32(inputs["L0_attn_W_v"])
    w["wg"] = _f32(inputs["L0_attn_W_g"])
    w["wr"] = _f32(inputs["L0_attn_W_r"])
    w["xlu"] = _f32(np.asarray(inputs["L0_attn_xl_u"]).reshape(128, 1))
    w["xlv"] = _f32(np.asarray(inputs["L0_attn_xl_v"]).reshape(128, 1))
    w["wo"] = _f32(inputs["L0_attn_W_o"])
    w["l1_wqT"] = _f32(np.asarray(inputs["L1_attn_Wq"]).T)
    w["l1_wkT"] = _f32(np.asarray(inputs["L1_attn_Wk"]).T)
    w["l1_wvT"] = _f32(np.asarray(inputs["L1_attn_Wv"]).T)
    w["l1_bq"] = _f32(np.asarray(inputs["L1_attn_bq"]).reshape(NCD, 128).T)
    w["l1_bk"] = _f32(np.asarray(inputs["L1_attn_bk"]).reshape(NCD, 128).T)
    w["l1_bv"] = _f32(np.asarray(inputs["L1_attn_bv"]).reshape(NCD, 128).T)
    w["l1_gwT"] = _f32(np.asarray(inputs["L1_ffn_gate_w"]).T)
    w["l1_ebias"] = _f32(np.asarray(inputs["L1_ffn_expert_bias"]).reshape(1, 16))
    w["l1_gk"] = _f32(np.transpose(np.asarray(inputs["L1_ffn_w_gate"]), (0, 2, 1)))
    w["l1_uk"] = _f32(np.transpose(np.asarray(inputs["L1_ffn_w_up"]), (0, 2, 1)))
    w["l1_dk"] = _f32(np.transpose(np.asarray(inputs["L1_ffn_w_down"]), (0, 2, 1)))
    w["l1_sg"] = _f32(np.asarray(inputs["L1_ffn_shared_gate_w"]).T)
    w["l1_su"] = _f32(np.asarray(inputs["L1_ffn_shared_up_w"]).T)
    w["l1_sd"] = _f32(np.asarray(inputs["L1_ffn_shared_down_w"]).T)
    for ci in range(NCORE):
        xs = x[ci * NSEQ:(ci + 1) * NSEQ]
        x_fm = np.ascontiguousarray(xs.transpose(0, 2, 1))
        m = {"x": _f32(x_fm)}
        m.update(consts)
        m.update(w)
        cores.append(m)
    return cores


def unshard_output(results):
    outs = []
    for r in results:
        y = r["y"]
        outs.append(np.ascontiguousarray(np.transpose(y, (0, 2, 1))))
    return np.concatenate(outs, axis=0).astype(np.float32)



def declare_inputs(nc, names, host_map):
    aps = {}
    for nm in names:
        a = host_map[nm]
        dt = BF16 if a.dtype == ml_dtypes.bfloat16 else F32
        aps[nm] = nc.dram_tensor(nm, list(a.shape), dt, kind="ExternalInput").ap()
    return aps


def build_ffn_prog(layer, host_map):
    nc = bass.Bass("TRN2", target_bir_lowering=False)
    if layer == "l0":
        wnames = ["rtr", "l0_gk", "l0_uk", "l0_dk"]
    else:
        wnames = ["l1_gwT", "l1_ebias", "l1_gk", "l1_uk", "l1_dk",
                  "l1_sg", "l1_su", "l1_sd"]
    names = ["x", "ident_bf", "ident_f", "selc8", "selc16"] + wnames
    aps = declare_inputs(nc, names, host_map)
    y = nc.dram_tensor("y", [NSEQ, D, T], F32, kind="ExternalOutput").ap()

    ctx = ExitStack()
    with ctx:
        ch = Chains(nc, ctx)
        p = Prog(nc, ctx, ch)
        c = emit_consts(p)
        c["ident_bf"] = ctx.enter_context(nc.sbuf_tensor("identb", [128, 128], BF16))
        ctoks = [ch.dma("qs", c["ident_bf"][:], aps["ident_bf"][:])]
        c["ident_f"] = ctx.enter_context(nc.sbuf_tensor("identf", [128, 128], F32))
        ctoks.append(ch.dma("qs", c["ident_f"][:], aps["ident_f"][:]))
        if layer == "l0":
            wd = {"gk": aps["l0_gk"], "uk": aps["l0_uk"], "dk": aps["l0_dk"],
                  "selc": aps["selc8"], "rw": aps["rtr"]}
        else:
            wd = {"gk": aps["l1_gk"], "uk": aps["l1_uk"], "dk": aps["l1_dk"],
                  "sg": aps["l1_sg"], "su": aps["l1_su"], "sd": aps["l1_sd"],
                  "selc": aps["selc16"], "rw": aps["l1_gwT"],
                  "ebias": aps["l1_ebias"]}

        X = ctx.enter_context(nc.sbuf_tensor("X", [128, NCD * T], F32))
        c["consts_tok"] = ctoks
        st_tok = None
        for seq in range(NSEQ):
            ch.barrier()
            if st_tok is not None:
                ch.wait("qs", *st_tok)
            xt = ch.dma("qs", X[:],
                        aps["x"][seq].rearrange("(c p) t -> p c t", p=128))
            ch.wait("v", *xt)
            ch.wait("s", *xt)
            ch.wait("t", *xt)
            ch.wait("p", *xt)
            emit_ffn(p, c, X, wd, layer)
            ch.wait("qs", "p")
            ch.wait("qs", "v")
            st_tok = ch.dma("qs", y[seq].rearrange("(c p) t -> p c t", p=128),
                            X[:])
        nc.sync.wait_ge(ch.sems[st_tok[0]], st_tok[1])
    return nc



def build_full_prog(host_map):
    nc = bass.Bass("TRN2", target_bir_lowering=False)
    names = sorted(set(
        ["x", "ident_bf", "ident_f", "selc8", "selc16", "rtr", "l0_gk",
         "l0_uk", "l0_dk",
         "gln", "xlu", "xlv", "binmask", "sinembT", "wq", "wk", "wv", "wg",
         "wr", "wo", "l1_gwT", "l1_ebias", "l1_gk", "l1_uk", "l1_dk", "l1_sg",
         "l1_su", "l1_sd", "l1_bq", "l1_bk", "l1_bv", "l1_wqT", "l1_wkT",
         "l1_wvT", "dft_c", "dft_s", "idft_c", "idft_s"]))
    aps = declare_inputs(nc, names, host_map)
    y = nc.dram_tensor("y", [NSEQ, D, T], F32, kind="ExternalOutput").ap()

    ctx = ExitStack()
    with ctx:
        ch = Chains(nc, ctx)
        p = Prog(nc, ctx, ch)
        c = emit_consts(p)
        c["ident_bf"] = ctx.enter_context(nc.sbuf_tensor("identb", [128, 128], BF16))
        ctoks = [ch.dma("qs", c["ident_bf"][:], aps["ident_bf"][:])]
        c["ident_f"] = ctx.enter_context(nc.sbuf_tensor("identf", [128, 128], F32))
        ctoks.append(ch.dma("qs", c["ident_f"][:], aps["ident_f"][:]))
        c["consts_tok"] = ctoks
        wd0 = {"gk": aps["l0_gk"], "uk": aps["l0_uk"], "dk": aps["l0_dk"],
               "selc": aps["selc8"], "rw": aps["rtr"]}
        wd1 = {"gk": aps["l1_gk"], "uk": aps["l1_uk"], "dk": aps["l1_dk"],
               "sg": aps["l1_sg"], "su": aps["l1_su"], "sd": aps["l1_sd"],
               "selc": aps["selc16"], "rw": aps["l1_gwT"],
               "ebias": aps["l1_ebias"]}

        X = ctx.enter_context(nc.sbuf_tensor("X", [128, NCD * T], F32))
        st_tok = None
        for seq in range(NSEQ):
            ch.barrier()
            if st_tok is not None:
                ch.wait("qs", *st_tok)
            xt = ch.dma("qs", X[:],
                        aps["x"][seq].rearrange("(c p) t -> p c t", p=128))
            for e in ("v", "s", "t", "p"):
                ch.wait(e, *xt)
            emit_ffn(p, c, X, wd0, "l0")
            emit_l0attn(p, c, X, aps)
            emit_ffn(p, c, X, wd0, "l0")
            emit_ffn(p, c, X, wd1, "l1")
            emit_l1attn(p, c, X, aps)
            emit_ffn(p, c, X, wd1, "l1")
            ch.wait("qs", "v")
            ch.wait("qs", "p")
            st_tok = ch.dma("qs", y[seq].rearrange("(c p) t -> p c t", p=128),
                            X[:])
        nc.sync.wait_ge(ch.sems[st_tok[0]], st_tok[1])
    return nc



FFN_L0_NAMES = ["x", "ident_bf", "ident_f", "selc8", "selc16", "rtr",
                "l0_gk", "l0_uk", "l0_dk"]
FFN_L1_NAMES = ["x", "ident_bf", "ident_f", "selc8", "selc16", "l1_gwT",
                "l1_ebias", "l1_gk", "l1_uk", "l1_dk", "l1_sg", "l1_su",
                "l1_sd"]
ATTN_L0_NAMES = ["x", "ident_bf", "gln", "xlu", "xlv", "binmask", "sinembT",
                 "wq", "wk", "wv", "wg", "wr", "wo"]
ATTN_L1_NAMES = ["x", "l1_bq", "l1_bk", "l1_bv", "l1_wqT", "l1_wkT", "l1_wvT",
                 "dft_c", "dft_s", "idft_c", "idft_s"]

_PROGS = {}


def _get_prog(kind, hm):
    if kind in _PROGS:
        return _PROGS[kind]
    if kind == "ffn_l0":
        nc, names = build_ffn_prog("l0", hm), FFN_L0_NAMES
    elif kind == "ffn_l1":
        nc, names = build_ffn_prog("l1", hm), FFN_L1_NAMES
    elif kind == "attn_l0":
        nc, names = build_l0attn_prog(hm), ATTN_L0_NAMES
    else:
        nc, names = build_l1attn_prog(hm), ATTN_L1_NAMES
    _PROGS[kind] = (nc, names)
    return _PROGS[kind]


FULL_NAMES = sorted(set(
    ["x", "ident_bf", "ident_f", "selc8", "selc16", "rtr", "l0_gk",
     "l0_uk", "l0_dk",
     "gln", "xlu", "xlv", "binmask", "sinembT", "wq", "wk", "wv", "wg",
     "wr", "wo", "l1_gwT", "l1_ebias", "l1_gk", "l1_uk", "l1_dk", "l1_sg",
     "l1_su", "l1_sd", "l1_bq", "l1_bk", "l1_bv", "l1_wqT", "l1_wkT",
     "l1_wvT", "dft_c", "dft_s", "idft_c", "idft_s"]))

FUSED = True


def kernel(**inputs):
    cores = shard_inputs(inputs)
    if FUSED:
        if "full" not in _PROGS:
            _PROGS["full"] = (build_full_prog(cores[0]), FULL_NAMES)
        nc, names = _PROGS["full"]
        in_maps = [{nm: cores[ci][nm] for nm in names} for ci in range(NCORE)]
        res = run_bass_kernel_spmd(nc, in_maps, core_ids=list(range(NCORE)))
        return unshard_output([{"y": np.asarray(r["y"])} for r in res.results])
    xs = [m["x"] for m in cores]
    seqplan = ["ffn_l0", "attn_l0", "ffn_l0", "ffn_l1", "attn_l1", "ffn_l1"]
    for kind in seqplan:
        nc, names = _get_prog(kind, cores[0])
        in_maps = []
        for ci in range(NCORE):
            m = {nm: cores[ci][nm] for nm in names if nm != "x"}
            m["x"] = xs[ci]
            in_maps.append(m)
        res = run_bass_kernel_spmd(nc, in_maps, core_ids=list(range(NCORE)))
        xs = [np.asarray(r["y"]) for r in res.results]
    return unshard_output([{"y": x} for x in xs])


# version 09 (final)
# speedup vs baseline: 1.3016x; 1.1420x over previous
import numpy as np
import ml_dtypes
from contextlib import ExitStack
import dataclasses

import concourse.bass as bass
import concourse.mybir as mybir
from concourse.bass_utils import run_bass_kernel_spmd

F32 = mybir.dt.float32
BF16 = mybir.dt.bfloat16
U32 = mybir.dt.uint32
AF = mybir.ActivationFunctionType
ALU = mybir.AluOpType
AX = mybir.AxisListType

D = 1024
NCD = 8
T = 2048
NSEQ = 2
TT = 512
NTT = T // TT
NCORE = 8

LN_EPS = 1e-5
RMS_EPS = 1e-6
L0_E, L0_F, L0_FC = 8, 2048, 16
L1_E, L1_F, L1_FC = 16, 1024, 8
NFREQ = 640
NKC = 5



NQ = 8


class Chains:

    def __init__(self, nc, ctx):
        self.nc = nc
        self.sems = {}
        self.counts = {}
        self.waited = {}
        self.dma_rr = 0
        names = ["v", "s", "t", "p"] + [f"q{i}" for i in range(NQ)]
        for name in names:
            self.sems[name] = ctx.enter_context(nc.semaphore("ch_" + name))
            self.counts[name] = 0

    def eng(self, chn):
        return {"v": self.nc.vector, "s": self.nc.scalar, "t": self.nc.tensor,
                "p": self.nc.gpsimd, "qs": self.nc.sync, "qc": self.nc.scalar}[chn]

    def mark(self, chn, inst, n=1):
        inst.then_inc(self.sems[chn], n)
        self.counts[chn] += n
        return self.counts[chn]

    def wait(self, waiter, producer, tok=None):
        if tok is None:
            tok = self.counts[producer]
        if tok <= 0:
            return
        key = (waiter, producer)
        if self.waited.get(key, -1) >= tok:
            return
        self.eng(waiter).wait_ge(self.sems[producer], tok)
        self.waited[key] = tok

    def dma(self, issuer, out, in_, **kw):
        name = f"q{self.dma_rr}"
        self.dma_rr = (self.dma_rr + 1) % NQ
        self.wait(issuer, name)
        inst = self.eng(issuer).dma_start(out=out, in_=in_, **kw)
        inst.then_inc(self.sems[name], 16)
        self.counts[name] += 16
        return (name, self.counts[name])

    def barrier(self):
        chans = ["v", "s", "t", "p"]
        qs = [f"q{i}" for i in range(NQ)]
        for w in chans + ["qs"]:
            for pr in chans + qs:
                self.wait(w, pr)



class Prog:

    def __init__(self, nc, ctx, ch):
        self.nc = nc
        self.ctx = ctx
        self.ch = ch
        self.sb = lambda name, shape, dt: ctx.enter_context(
            nc.sbuf_tensor(name, shape, dt))
        self.uid = 0

    def name(self, base):
        self.uid += 1
        return f"{base}_{self.uid}"


def diag_ap(base_ap, tensor_cols, offset, nrows, ncols):
    return dataclasses.replace(
        base_ap, offset=offset, ap=[[tensor_cols - 1, nrows], [1, ncols]])



def emit_consts(p):
    nc, ch = p.nc, p.ch
    c = {}
    c["ones_col_bf"] = p.sb("ones_col_bf", [128, 1], BF16)
    c["ones_row_f"] = p.sb("ones_row_f", [1, 128], F32)
    c["eps_ln"] = p.sb("eps_ln", [1, 1], F32)
    c["eps_rms"] = p.sb("eps_rms", [1, 1], F32)
    nc.vector.memset(c["ones_col_bf"][:], 1.0)
    nc.vector.memset(c["ones_row_f"][:], 1.0)
    nc.vector.memset(c["eps_ln"][:], LN_EPS)
    i = nc.vector.memset(c["eps_rms"][:], RMS_EPS)
    ch.mark("v", i)
    return c



def xview3(X, t):
    return dataclasses.replace(
        X[:], offset=t * TT, ap=[[NCD * T, 128], [T, NCD], [1, TT]])


def _half(v, h):
    return dataclasses.replace(
        v, offset=v.offset + h * (NCD // 2) * v.ap[1][0],
        ap=[v.ap[0], [v.ap[1][0], NCD // 2], v.ap[2]])


def cview(tile):
    return dataclasses.replace(
        tile[:], ap=[[NCD * 512, 128], [512, NCD], [1, 512]])


def bview(tile, n):
    return dataclasses.replace(tile[:], ap=[tile[:].ap[0], [0, n], [1, TT]])


def emit_ln_tile(p, c, X, t, xn_view, scr, rms_fold=False):
    nc, ch = p.nc, p.ch
    V, S, TE, P = nc.vector, nc.scalar, nc.tensor, nc.gpsimd
    xb, xsq, ps_stat, ps_b, ab_sb, bb_sb, t1 = (
        scr["xb"], scr["xsq"], scr["ps_stat"], scr["ps_b"],
        scr["ab_sb"], scr["bb_sb"], scr["t1"])
    rb = scr["rowbuf"]
    s1, s2 = rb[0:1, 0:TT], rb[0:1, TT:2 * TT]
    aa, bb = rb[0:1, 2 * TT:3 * TT], rb[0:1, 3 * TT:4 * TT]

    if scr.get("reuse_tok_t") is not None:
        ch.wait("p", "t", scr["reuse_tok_t"])
        ch.wait("s", "t", scr["reuse_tok_t"])
    cast_tok = ch.mark("p", P.tensor_copy(out=xb[:], in_=xview3(X, t)))
    sq_tok = ch.mark("s", S.activation(out=xsq[:], in_=xview3(X, t),
                                       func=AF.Square))

    ch.wait("t", "p", cast_tok)
    ch.wait("t", "s", sq_tok)
    if scr.get("stat_read_tok") is not None:
        ch.wait("t", "v", scr["stat_read_tok"])
    for cc in range(NCD):
        TE.matmul(out=ps_stat[0:1, 0:TT], lhsT=c["ones_col_bf"][:],
                  rhs=xb[:, cc * TT:(cc + 1) * TT],
                  start=(cc == 0), stop=(cc == NCD - 1))
    for cc in range(NCD):
        i = TE.matmul(out=ps_stat[64:65, 0:TT], lhsT=c["ones_col_bf"][:],
                      rhs=xsq[:, cc * TT:(cc + 1) * TT],
                      start=(cc == 0), stop=(cc == NCD - 1),
                      skip_group_check=True)
    stat_tok = ch.mark("t", i)

    ch.wait("v", "t", stat_tok)
    ch.mark("v", V.tensor_scalar(out=s1, in0=ps_stat[0:1, 0:TT],
                                 scalar1=1.0 / D, scalar2=None, op0=ALU.mult))
    srd = ch.mark("v", V.tensor_scalar(out=s2, in0=ps_stat[64:65, 0:TT],
                                       scalar1=1.0 / D, scalar2=None,
                                       op0=ALU.mult))
    scr["stat_read_tok"] = srd
    ch.wait("v", "v")
    ch.mark("v", V.tensor_tensor(out=bb, in0=s1, in1=s1, op=ALU.mult))
    ch.wait("v", "v")
    var_tok = ch.mark("v", V.tensor_tensor(out=s2, in0=s2, in1=bb, op=ALU.subtract))
    ch.wait("s", "v", var_tok)
    lt = ch.mark("s", S.activation(out=aa, in_=s2, func=AF.Ln, bias=c["eps_ln"][:]))
    ch.wait("s", "s", lt)
    sq2_tok = ch.mark("s", S.activation(out=aa, in_=aa, func=AF.Exp, scale=-0.5))
    ch.wait("v", "s", sq2_tok)
    if rms_fold:
        ch.mark("v", V.tensor_tensor(out=s2, in0=s2, in1=aa, op=ALU.mult))
        ch.wait("v", "v")
        rt = ch.mark("v", V.tensor_tensor(out=s2, in0=s2, in1=aa, op=ALU.mult))
        ch.wait("s", "v", rt)
        lt = ch.mark("s", S.activation(out=s2, in_=s2, func=AF.Ln,
                                       bias=c["eps_rms"][:]))
        ch.wait("s", "s", lt)
        st = ch.mark("s", S.activation(out=s2, in_=s2, func=AF.Exp, scale=-0.5))
        ch.wait("v", "s", st)
        ch.mark("v", V.tensor_tensor(out=aa, in0=aa, in1=s2, op=ALU.mult))
        ch.wait("v", "v")
    ch.mark("v", V.tensor_tensor(out=bb, in0=s1, in1=aa, op=ALU.mult))
    ch.wait("v", "v")
    row_tok = ch.mark("v", V.tensor_scalar(out=bb, in0=bb, scalar1=-1.0,
                                           scalar2=None, op0=ALU.mult))

    ch.wait("t", "v", row_tok)
    if scr.get("bcast_read_tok") is not None:
        ch.wait("t", "v", scr["bcast_read_tok"])
    TE.matmul(out=ps_b[:, 0:TT], lhsT=c["ones_row_f"][:], rhs=aa, start=True,
              stop=True)
    i = TE.matmul(out=ps_b[:, TT:2 * TT], lhsT=c["ones_row_f"][:], rhs=bb,
                  start=True, stop=True)
    bc_tok = ch.mark("t", i)

    ch.wait("v", "t", bc_tok)
    if scr.get("pass2_tok") is not None:
        ch.wait("v", "p", scr["pass2_tok"])
    ch.mark("v", V.tensor_copy(out=ab_sb[:], in_=ps_b[:, 0:TT]))
    evb_tok = ch.mark("v", V.tensor_copy(out=bb_sb[:], in_=ps_b[:, TT:2 * TT]))
    scr["bcast_read_tok"] = evb_tok

    ch.wait("v", "v", evb_tok)
    p2_tok = None
    for h in range(2):
        if h == 1 or scr.get("pass2_tok") is not None:
            ch.wait("v", "p", p2_tok if h == 1 else scr["pass2_tok"])
        i = V.tensor_tensor(out=t1[:], in0=_half(xview3(X, t), h),
                            in1=bview(ab_sb, NCD // 2), op=ALU.mult)
        p1 = ch.mark("v", i)
        ch.wait("p", "v", p1)
        i = P.tensor_tensor(out=_half(xn_view, h), in0=t1[:],
                            in1=bview(bb_sb, NCD // 2), op=ALU.add)
        p2_tok = ch.mark("p", i)
    scr["reuse_tok_t"] = stat_tok
    scr["pass2_tok"] = p2_tok
    return p2_tok


def emit_ln_scratch(p, c, sctx, sep_stat=False):
    sb = lambda name, shape, dt: sctx.enter_context(
        p.nc.sbuf_tensor(p.name(name), shape, dt))
    ps = lambda name, shape, dt: sctx.enter_context(
        p.nc.psum_tensor(p.name(name), shape, dt))
    scr = {
        "rowbuf": sb("ln_rowbuf", [1, 4 * TT], F32),
        "xb": sb("ln_xb", [128, NCD * TT], BF16),
        "xsq": sb("ln_xsq", [128, NCD * TT], BF16),
        "t1": sb("ln_t1", [128, (NCD // 2) * TT], F32),
        "ab_sb": sb("ln_ab", [128, TT], F32),
        "bb_sb": sb("ln_bb", [128, TT], F32),
        "ps_b": ps("ln_ps_b", [128, 2 * TT], F32),
        "reuse_tok_t": None,
        "stat_read_tok": None,
        "bcast_read_tok": None,
        "pass2_tok": None,
    }
    if sep_stat:
        scr["ps_stat"] = ps("ln_ps_stat", [128, TT], F32)
    else:
        scr["ps_stat"] = scr["ps_b"]
    return scr



class WLoader:

    def __init__(self, p, sctx, tag, cols, nbuf=2, nstage=1):
        self.p = p
        self.cols = cols
        self.nbuf = nbuf
        self.nstage = nstage
        sb = lambda name, shape, dt: sctx.enter_context(
            p.nc.sbuf_tensor(p.name(name), shape, dt))
        self.stage = sb(f"{tag}_stg", [128, nstage * cols], F32)
        self.bf = sb(f"{tag}_bf", [128, nbuf * cols], BF16)
        self.slot_cast_tok = [None] * nbuf
        self.slot_read_tok = [None] * nbuf
        self.seq = 0

    def load(self, src_ap, read_done=None, cast="p", scale_aps=None,
             chunk_cols=128, ncols=None):
        p, ch, nc = self.p, self.p.ch, self.p.nc
        n = self.cols if ncols is None else ncols
        slot = self.seq % self.nbuf
        sslot = self.seq % self.nstage
        self.seq += 1
        st = self.stage[:, sslot * self.cols: sslot * self.cols + n]
        bf = self.bf[:, slot * self.cols: slot * self.cols + n]
        prev = self.seq - 1 - self.nstage
        if prev >= 0:
            cc, tt = self.slot_cast_tok[prev % self.nbuf]
            ch.wait("qs", cc, tt)
        dma_tok = ch.dma("qs", st, src_ap)
        ch.wait("s", *dma_tok)
        if self.slot_read_tok[slot] is not None:
            cc, tt = self.slot_read_tok[slot]
            ch.wait("s", cc, tt)
        if scale_aps is None:
            i = nc.scalar.copy(out=bf, in_=st)
        else:
            n = self.cols // chunk_cols
            for k in range(n):
                i = nc.scalar.activation(
                    out=bf[:, k * chunk_cols:(k + 1) * chunk_cols],
                    in_=st[:, k * chunk_cols:(k + 1) * chunk_cols],
                    func=AF.Copy, scale=scale_aps[k])
        tok = ch.mark("s", i)
        self.slot_cast_tok[slot] = ("s", tok)
        return slot, ("s", tok)

    def view(self, slot, ncols=None):
        n = self.cols if ncols is None else ncols
        return self.bf[:, slot * self.cols: slot * self.cols + n]

    def set_read(self, slot, chain, tok):
        self.slot_read_tok[slot] = (chain, tok)



def emit_ffn(p, c, X, w, layer):
    nc, ch = p.nc, p.ch
    V, S, TE, P = nc.vector, nc.scalar, nc.tensor, nc.gpsimd
    E = L0_E if layer == "l0" else L1_E
    FC = L0_FC if layer == "l0" else L1_FC

    sctx = ExitStack()
    with sctx:
        sb = lambda name, shape, dt: sctx.enter_context(
            nc.sbuf_tensor(p.name(name), shape, dt))
        ps = lambda name, shape, dt: sctx.enter_context(
            nc.psum_tensor(p.name(name), shape, dt))

        XN = sb("ffn_xn", [128, NCD * T], BF16)
        E16 = E if E >= 8 else 8
        selc = sb("selc", [E16, E * 128], BF16)
        sc_tok = ch.dma("qs", selc[:], w["selc"][:])
        rw = sb("rw", [128, NCD * E], BF16)
        rw_st = sb("rw_st", [128, NCD * E], F32)
        tk = ch.dma("qs", rw_st[:], w["rw"].rearrange("(c p) e -> p c e", p=128))
        ch.wait("p", *tk)
        ch.wait("p", "t")
        rw_tok = ch.mark("p", nc.gpsimd.tensor_copy(out=rw[:], in_=rw_st[:]))
        if layer == "l1":
            ebias_sb = sb("ebias_sb", [1, 16], F32)
            eb_tok = ch.dma("qs", ebias_sb[:], w["ebias"][:])

        xn_toks = []
        scr = emit_ln_scratch(p, c, sctx, sep_stat=True)
        for t in range(NTT):
            xn_toks.append(emit_ln_tile(p, c, X, t, xview3(XN, t), scr))

        nsl = T // 128
        c_fm = sb("c_fm", [E if E >= 8 else 8, nsl * 128], BF16)
        c_all = sb("c_all", [128, nsl * E], F32)
        r_sc = sb("r_sc", [128, 16 + 16 + 4], F32)
        r_sc2 = sb("r_sc2", [128, 16], F32)
        if layer == "l1":
            bias_b = sb("bias_b", [128, 16], F32)
        ps_cb = ps("ps_cb", [128, 512], F32)
        for tk_ in c["consts_tok"]:
            ch.wait("t", *tk_)
        ch.wait("t", "p", rw_tok)
        for sl_ in range(nsl):
            ch.wait("t", "p", xn_toks[sl_ // 4])
            for dc in range(NCD):
                i = TE.matmul(out=ps_cb[:, sl_ * 16: sl_ * 16 + E],
                              lhsT=XN[:, dc * T + sl_ * 128: dc * T + (sl_ + 1) * 128],
                              rhs=rw[:, dc * E:(dc + 1) * E],
                              start=(dc == 0), stop=(dc == NCD - 1))
        lg_tok = ch.mark("t", i)

        def emit_router_coeffs():
            if layer == "l1":
                ch.wait("t", *eb_tok)
                i = TE.matmul(out=ps_cb[:, 496:512], lhsT=c["ones_row_f"][:],
                              rhs=ebias_sb[0:1, :], start=True, stop=True,
                              skip_group_check=True)
                bt = ch.mark("t", i)
                ch.wait("v", "t", bt)
                bias_ev = ch.mark("v", V.tensor_copy(out=bias_b[:],
                                                     in_=ps_cb[:, 496:512]))
                ch.wait("v", "v")
                cb_guard = ("v", bias_ev)
            else:
                cb_guard = None
            ch.wait("v", "t")
            ch.wait("s", "t")
            for s_ in range(nsl):
                lgs = ps_cb[:, s_ * 16: s_ * 16 + E]
                logits = r_sc[:, 0:E]
                probs = r_sc[:, 16:16 + E]
                vals = r_sc2[:, 0:8]
                nmax = r_sc[:, 32:33]
                den = r_sc[:, 33:34]
                cdst = c_all[:, s_ * E: (s_ + 1) * E]
                ch.wait("v", "s")
                if layer == "l0":
                    ch.mark("v", V.reduce_max(out=nmax, in_=lgs, axis=AX.X,
                                              negate=True))
                    ch.wait("s", "v")
                    et = ch.mark("s", S.activation(out=logits, in_=lgs,
                                                   func=AF.Exp, bias=nmax))
                    ch.wait("v", "s", et)
                    ch.mark("v", V.reduce_sum(out=den, in_=logits, axis=AX.X))
                    ch.wait("v", "v")
                    ch.mark("v", V.reciprocal(out=den, in_=den))
                    ch.wait("v", "v")
                    ch.mark("v", V.tensor_scalar(out=probs, in0=logits,
                                                 scalar1=den, scalar2=None,
                                                 op0=ALU.mult))
                    ch.wait("v", "v")
                    ch.mark("v", V.max(out=vals, in_=probs))
                    ch.wait("v", "v")
                    ch.mark("v", V.tensor_scalar(out=logits, in0=probs,
                                                 scalar1=vals[:, 1:2],
                                                 scalar2=None, op0=ALU.is_ge))
                    ch.wait("s", "v")
                    et = ch.mark("s", S.activation(out=r_sc2[:, 8:8 + E],
                                                   in_=probs, func=AF.Exp))
                    ch.wait("v", "s", et)
                    ch.mark("v", V.tensor_tensor(out=probs,
                                                 in0=r_sc2[:, 8:8 + E],
                                                 in1=logits, op=ALU.mult))
                    ch.wait("v", "v")
                    ch.mark("v", V.reduce_sum(out=den, in_=probs, axis=AX.X))
                    ch.wait("v", "v")
                    ch.mark("v", V.reciprocal(out=den, in_=den))
                    ch.wait("v", "v")
                    ch.mark("v", V.tensor_scalar(out=cdst, in0=probs,
                                                 scalar1=den, scalar2=0.5,
                                                 op0=ALU.mult, op1=ALU.mult))
                else:
                    ch.wait("s", "v")
                    st = ch.mark("s", S.activation(out=probs, in_=lgs,
                                                   func=AF.Sigmoid))
                    ch.wait("v", "s", st)
                    ch.mark("v", V.tensor_tensor(out=logits, in0=probs,
                                                 in1=bias_b[:], op=ALU.add))
                    ch.wait("v", "v")
                    ch.mark("v", V.max(out=vals, in_=logits))
                    ch.wait("v", "v")
                    ch.mark("v", V.tensor_scalar(out=logits, in0=logits,
                                                 scalar1=vals[:, 7:8],
                                                 scalar2=None, op0=ALU.is_ge))
                    ch.wait("v", "v")
                    ch.mark("v", V.tensor_tensor(out=probs, in0=probs,
                                                 in1=logits, op=ALU.mult))
                    ch.wait("v", "v")
                    ch.mark("v", V.reduce_sum(out=den, in_=probs, axis=AX.X))
                    ch.wait("v", "v")
                    ch.mark("v", V.reciprocal(out=den, in_=den))
                    ch.wait("v", "v")
                    ch.mark("v", V.tensor_scalar(out=cdst, in0=probs,
                                                 scalar1=den,
                                                 scalar2=2.826 * 0.5,
                                                 op0=ALU.mult, op1=ALU.mult))
            cw_tok = ch.counts["v"]
            ch.wait("t", "v", cw_tok)
            if cb_guard is not None:
                ch.wait("t", cb_guard[0], cb_guard[1])
            tr_rd = None
            for s_ in range(nsl):
                if tr_rd is not None:
                    ch.wait("t", "v", tr_rd)
                i = TE.transpose(out=ps_cb[0:E, 0:128],
                                 in_=c_all[:, s_ * E: (s_ + 1) * E],
                                 identity=c["ident_f"][:])
                tt = ch.mark("t", i)
                ch.wait("v", "t", tt)
                tr_rd = ch.mark("v", V.tensor_copy(
                    out=c_fm[:, s_ * 128:(s_ + 1) * 128],
                    in_=ps_cb[0:E, 0:128]))
            nonlocal cb_read
            cb_read = ("v", tr_rd)

        fuse = sb("fuse", [128, (FC // 2) * T], BF16)
        act = sb("act", [128, TT], BF16)
        csb = sb("csb", [128, NTT * TT], BF16)
        dtmp = sb("dtmp", [128, 4 * TT], F32)
        ps_g = ps("ps_g", [128, 2 * 512], F32)
        ps_u = ps("ps_u", [128, 2 * 512], F32)
        ps_d = scr["ps_b"]
        wl_g = WLoader(p, sctx, "wg", NCD * 128, nbuf=3)
        wl_u = WLoader(p, sctx, "wu", NCD * 128)

        gu_read = {}
        d_read = {}
        act_read = {}
        fuse_rd_tok = None
        csb_read = None
        dtmp_read = {}
        x_acc = {}
        gk = w["gk"]
        uk = w["uk"]
        dk = w["dk"]

        n_exp = E + (1 if layer == "l1" else 0)
        NH = 2
        FCH = FC // NH
        cb_read = None
        tr_done = None

        def w_src(e, fc, kind):
            if e == E:
                m = {"g": w["sg"], "u": w["su"]}
                if kind == "d":
                    return None
                return m[kind][:, fc * 128:(fc + 1) * 128].rearrange(
                    "(c p) f -> p c f", p=128)
            m = {"g": gk, "u": uk}
            return m[kind][e, :, fc * 128:(fc + 1) * 128].rearrange(
                "(c p) f -> p c f", p=128)

        def d_src(e, half, dc):
            r0, r1 = half * FCH * 128, (half + 1) * FCH * 128
            if e == E:
                return w["sd"][r0:r1, dc * 128:(dc + 1) * 128].rearrange(
                    "(c p) f -> p c f", p=128)
            return dk[e, r0:r1, dc * 128:(dc + 1) * 128].rearrange(
                "(c p) f -> p c f", p=128)

        plan = []
        for e_ in range(n_exp):
            for half_ in range(NH):
                for fcl_ in range(FCH):
                    fc_ = half_ * FCH + fcl_
                    plan.append((wl_g, w_src(e_, fc_, "g"), None))
                    plan.append((wl_u, w_src(e_, fc_, "u"), None))
                for dc_ in range(NCD):
                    plan.append((wl_g, d_src(e_, half_, dc_), FCH * 128))
        _lp = {"issued": 0, "h": {}}

        def lp_use(i):
            while _lp["issued"] <= min(i + 2, len(plan) - 1):
                j = _lp["issued"]
                ld, src, nc_ = plan[j]
                _lp["h"][j] = (ld, ld.load(src, ncols=nc_))
                _lp["issued"] += 1
            return _lp["h"].pop(i)

        def emit_csb(e, shared):
            nonlocal cb_read, csb_read
            if shared:
                if csb_read is not None:
                    ch.wait("v", csb_read[0], csb_read[1])
                rdy = ch.mark("v", V.memset(csb[:], 0.5))
                csb_read = None
                return rdy
            for t in range(NTT):
                ch.wait("t", "v", tr_done)
                ch.wait("t", *sc_tok)
                if cb_read is not None:
                    ch.wait("t", cb_read[0], cb_read[1])
                sel = selc[:, e * 128:(e + 1) * 128]
                i = TE.matmul(out=ps_cb[:, 0:TT], lhsT=sel,
                              rhs=c_fm[:, t * TT:(t + 1) * TT],
                              start=True, stop=True)
                bt = ch.mark("t", i)
                ch.wait("v", "t", bt)
                if csb_read is not None:
                    ch.wait("v", csb_read[0], csb_read[1])
                cb_tok = ch.mark("v", V.tensor_copy(
                    out=csb[:, t * TT:(t + 1) * TT], in_=ps_cb[:, 0:TT]))
                cb_read = ("v", cb_tok)
            csb_read = None
            return ch.counts["v"]

        for e in range(n_exp):
            shared = (e == E)
            eplan0 = e * NH * (2 * FCH + NCD)
            for half in range(NH):
                step = 0
                hplan0 = eplan0 + half * (2 * FCH + NCD)
                for fcl in range(FCH):
                    if e == 0 and half == 0 and fcl == 1:
                        emit_router_coeffs()
                        tr_done = ch.counts["v"]
                    fc = half * FCH + fcl
                    _, (sg, gtok) = lp_use(hplan0 + 2 * fcl)
                    _, (su, utok) = lp_use(hplan0 + 2 * fcl + 1)
                    wgb, wub = wl_g.view(sg), wl_u.view(su)
                    for t in range(NTT):
                        bk = step % 2
                        gsl = ps_g[:, bk * 512: bk * 512 + TT]
                        usl = ps_u[:, bk * 512: bk * 512 + TT]
                        ch.wait("t", "p", xn_toks[t])
                        ch.wait("t", gtok[0], gtok[1])
                        ch.wait("t", utok[0], utok[1])
                        if bk in gu_read:
                            ch.wait("t", gu_read[bk][0], gu_read[bk][1])
                        for dc in range(NCD):
                            TE.matmul(out=gsl,
                                      lhsT=wgb[:, dc * 128:(dc + 1) * 128],
                                      rhs=XN[:, dc * T + t * TT: dc * T + t * TT + TT],
                                      start=(dc == 0), stop=(dc == NCD - 1))
                        for dc in range(NCD):
                            i = TE.matmul(out=usl,
                                          lhsT=wub[:, dc * 128:(dc + 1) * 128],
                                          rhs=XN[:, dc * T + t * TT: dc * T + t * TT + TT],
                                          start=(dc == 0), stop=(dc == NCD - 1))
                        mm_tok = ch.mark("t", i)
                        asl = act[:, 0:TT]
                        ch.wait("s", "t", mm_tok)
                        if 0 in act_read:
                            ch.wait("s", act_read[0][0], act_read[0][1])
                        sa = ch.mark("s", S.activation(out=asl, in_=gsl, func=AF.Silu))
                        ch.wait("v", "s", sa)
                        if fuse_rd_tok is not None:
                            ch.wait("v", "t", fuse_rd_tok)
                        fv = ch.mark("v", V.tensor_tensor(
                            out=fuse[:, fcl * T + t * TT: fcl * T + t * TT + TT],
                            in0=asl, in1=usl, op=ALU.mult))
                        act_read[0] = ("v", fv)
                        gu_read[bk] = ("v", fv)
                        step += 1
                    wl_g.set_read(sg, "t", ch.counts["t"])
                    wl_u.set_read(su, "t", ch.counts["t"])
                fuse_v_tok = ch.counts["v"]
                if half == 0:
                    csb_ready = emit_csb(e, shared)

                dbanks = [ps_d[:, 0:512], ps_d[:, 512:1024],
                          ps_g[:, 0:512], ps_g[:, 512:1024],
                          ps_u[:, 0:512], ps_u[:, 512:1024]]
                dstep = 0
                for dc in range(NCD):
                    _, (sd_, dtok) = lp_use(hplan0 + 2 * FCH + dc)
                    wdb = wl_g.view(sd_, ncols=FCH * 128)
                    for t in range(NTT):
                        bk = dstep % 6
                        dsl = dbanks[bk]
                        ch.wait("t", dtok[0], dtok[1])
                        ch.wait("t", "v", fuse_v_tok)
                        prev = d_read.get(bk)
                        if prev is None or prev[1] is None:
                            if bk < 2:
                                ch.wait("t", "v", scr["bcast_read_tok"])
                            elif bk < 4:
                                gr = gu_read.get(0)
                                if gr:
                                    ch.wait("t", gr[0], gr[1])
                            else:
                                gr = gu_read.get(1)
                                if gr:
                                    ch.wait("t", gr[0], gr[1])
                        else:
                            ch.wait("t", prev[0], prev[1])
                        for fcl in range(FCH):
                            i = TE.matmul(
                                out=dsl,
                                lhsT=wdb[:, fcl * 128:(fcl + 1) * 128],
                                rhs=fuse[:, fcl * T + t * TT: fcl * T + t * TT + TT],
                                start=(fcl == 0), stop=(fcl == FCH - 1))
                        dmm_tok = ch.mark("t", i)
                        dslot = dstep % 4
                        tsl = dtmp[:, dslot * TT:(dslot + 1) * TT]
                        ch.wait("v", "t", dmm_tok)
                        ch.wait("v", "v", csb_ready)
                        if dslot in dtmp_read:
                            ch.wait("v", dtmp_read[dslot][0], dtmp_read[dslot][1])
                        tv = ch.mark("v", V.tensor_tensor(
                            out=tsl, in0=dsl, in1=csb[:, t * TT:(t + 1) * TT],
                            op=ALU.mult))
                        d_read[bk] = ("v", tv)
                        csb_read = ("v", tv)
                        xc = X[:, dc * T + t * TT: dc * T + t * TT + TT]
                        eng = "p" if dstep % 2 == 0 else "v"
                        ch.wait(eng, "v", tv)
                        prevx = x_acc.get((dc, t))
                        if prevx is not None:
                            ch.wait(eng, prevx[0], prevx[1])
                        if eng == "p":
                            px = ch.mark("p", P.tensor_tensor(out=xc, in0=xc,
                                                              in1=tsl, op=ALU.add))
                        else:
                            px = ch.mark("v", V.tensor_tensor(out=xc, in0=xc,
                                                              in1=tsl, op=ALU.add))
                        x_acc[(dc, t)] = (eng, px)
                        dtmp_read[dslot] = (eng, px)
                        dstep += 1
                    wl_g.set_read(sd_, "t", ch.counts["t"])
                for gb, pair in ((0, (2, 4)), (1, (3, 5))):
                    toks = [d_read[i][1] for i in pair if i in d_read]
                    if toks:
                        cur = gu_read.get(gb, ("v", 0))[1]
                        gu_read[gb] = ("v", max([cur] + toks))
                fuse_rd_tok = ch.counts["t"]
        ch.barrier()



def emit_l0attn_setup(p, c, aps, sctx):
    nc, ch = p.nc, p.ch
    V, S, TE, P = nc.vector, nc.scalar, nc.tensor, nc.gpsimd
    sb = lambda name, shape, dt: sctx.enter_context(
        nc.sbuf_tensor(p.name(name), shape, dt))
    a = {}
    a["gln"] = sb("gln", [128, NCD], F32)
    a["glnq"] = sb("glnq", [128, NCD], F32)
    a["xlu"] = sb("xlu", [128, 1], F32)
    a["xlv"] = sb("xlv", [128, 1], F32)
    a["binmask"] = sb("binmask", [128, 4 * 512], BF16)
    a["wq_bf"] = sb("wq_bf", [128, NCD * 128], BF16)
    a["wk_bf"] = sb("wk_bf", [128, NCD * 128], BF16)
    a["r_sb"] = sb("r_sb", [128, T], BF16)

    t1 = ch.dma("qs", a["gln"][:], aps["gln"][:])
    ch.dma("qs", a["xlu"][:], aps["xlu"][:])
    ch.dma("qs", a["xlv"][:], aps["xlv"][:])
    ch.dma("qs", a["binmask"][:],
           aps["binmask"].rearrange("(r p) j -> p r j", p=128))
    ch.wait("v", *t1)
    glt = ch.mark("v", V.tensor_scalar(out=a["glnq"][:], in0=a["gln"][:],
                                       scalar1=float(1.0 / np.sqrt(128.0)),
                                       scalar2=None, op0=ALU.mult))

    setup_ctx = ExitStack()
    setup_ctx.__enter__()
    stg = setup_ctx.enter_context(
        nc.sbuf_tensor(p.name("wqk_stg"), [128, NCD * 128], F32))
    for nm, dst, svec, vtok in (("wq", a["wq_bf"], a["glnq"], glt),
                                ("wk", a["wk_bf"], a["gln"], t1)):
        tk = ch.dma("qs", stg[:], aps[nm].rearrange("(c p) k -> p c k", p=128))
        ch.wait("s", *tk)
        ch.wait("s", "v", glt)
        for dc in range(NCD):
            i = S.activation(out=dst[:, dc * 128:(dc + 1) * 128],
                             in_=stg[:, dc * 128:(dc + 1) * 128],
                             func=AF.Copy, scale=svec[:, dc:dc + 1])
        stok = ch.mark("s", i)
        ch.wait("qs", "s", stok)

    wr_bf = setup_ctx.enter_context(
        nc.sbuf_tensor(p.name("wr_bf"), [128, NCD * 128], BF16))
    tk = ch.dma("qs", stg[:], aps["wr"].rearrange("(c p) k -> p c k", p=128))
    ch.wait("p", *tk)
    wrt = ch.mark("p", P.tensor_copy(out=wr_bf[:], in_=stg[:]))

    sin_st = setup_ctx.enter_context(
        nc.sbuf_tensor(p.name("sin_st"), [128, NCD * 512], BF16))
    pctx = ExitStack()
    with pctx:
        ps_r = pctx.enter_context(nc.psum_tensor(p.name("ps_rsb"), [128, 512], F32))
        ev_tok = None
        for pg in range(4):
            tk = ch.dma("qs", sin_st[:],
                        aps["sinembT"].rearrange("(c p) t -> p c t", p=128)
                        [:, :, pg * 512:(pg + 1) * 512])
            ch.wait("t", *tk)
            ch.wait("t", "p", wrt)
            if ev_tok is not None:
                ch.wait("t", "v", ev_tok)
            for dc in range(NCD):
                i = TE.matmul(out=ps_r[:, 0:512],
                              lhsT=wr_bf[:, dc * 128:(dc + 1) * 128],
                              rhs=sin_st[:, dc * 512:(dc + 1) * 512],
                              start=(dc == 0), stop=(dc == NCD - 1))
            mt = ch.mark("t", i)
            ch.wait("v", "t", mt)
            ev_tok = ch.mark("v", V.tensor_copy(
                out=a["r_sb"][:, pg * 512:(pg + 1) * 512], in_=ps_r[:, 0:512]))
            ch.wait("qs", "v", ev_tok)
        ch.barrier()
    setup_ctx.close()
    return a


def emit_l0attn(p, c, X, aps):
    nc, ch = p.nc, p.ch
    V, S, TE, P = nc.vector, nc.scalar, nc.tensor, nc.gpsimd
    sctx = ExitStack()
    with sctx:
        sb = lambda name, shape, dt: sctx.enter_context(
            nc.sbuf_tensor(p.name(name), shape, dt))
        ps = lambda name, shape, dt: sctx.enter_context(
            nc.psum_tensor(p.name(name), shape, dt))

        a = emit_l0attn_setup(p, c, aps, sctx)
        A_fm = sb("A_fm", [128, NCD * T], BF16)
        s01ctx = ExitStack()
        s01ctx.__enter__()
        sb01 = lambda name, shape, dt: s01ctx.enter_context(
            nc.sbuf_tensor(p.name(name), shape, dt))
        qu = sb01("qu", [128, T], BF16)
        qv = sb01("qv", [128, T], BF16)
        k_sb = sb01("k_sb", [128, T], BF16)
        xn_tm = sb01("xn_tm", [128, 16 * 1024], BF16)

        l0ctx = ExitStack()
        with l0ctx:
            scr = emit_ln_scratch(p, c, l0ctx, sep_stat=True)
            xn_t = l0ctx.enter_context(
                nc.sbuf_tensor(p.name("xn_t"), [128, NCD * 512], BF16))
            ps_qk = l0ctx.enter_context(
                nc.psum_tensor(p.name("ps_qk"), [128, 1024], F32))
            ps_tp = l0ctx.enter_context(
                nc.psum_tensor(p.name("ps_tp"), [128, 1024], BF16))
            qk_read = {}
            tp_read = None
            xn_read = [None]
            for t in range(NTT):
                slot = 0
                base = 0
                if xn_read[slot] is not None:
                    ch.wait("p", "t", xn_read[slot])
                xn_tok = emit_ln_tile(p, c, X, t, cview(xn_t), scr,
                                      rms_fold=True)
                for which, (wbf, bank) in enumerate(
                        ((a["wq_bf"], 0), (a["wk_bf"], 1))):
                    sl = ps_qk[:, bank * 512: bank * 512 + 512]
                    ch.wait("t", "p", xn_tok)
                    if bank in qk_read:
                        ch.wait("t", qk_read[bank][0], qk_read[bank][1])
                    for dc in range(NCD):
                        i = TE.matmul(out=sl, lhsT=wbf[:, dc * 128:(dc + 1) * 128],
                                      rhs=xn_t[:, base + dc * 512: base + dc * 512 + 512],
                                      start=(dc == 0), stop=(dc == NCD - 1))
                    mt = ch.mark("t", i)
                    ch.wait("s", "t", mt)
                    if which == 0:
                        S.activation(out=qu[:, t * 512:(t + 1) * 512], in_=sl,
                                     func=AF.Identity, bias=a["xlu"][:])
                        i = S.activation(out=qv[:, t * 512:(t + 1) * 512], in_=sl,
                                         func=AF.Identity, bias=a["xlv"][:])
                    else:
                        i = S.activation(out=k_sb[:, t * 512:(t + 1) * 512],
                                         in_=sl, func=AF.Identity)
                    qk_read[bank] = ("s", ch.mark("s", i))
                for sl4 in range(4):
                    if tp_read is not None:
                        ch.wait("t", "v", tp_read)
                    ch.wait("t", "p", xn_tok)
                    for dc in range(NCD):
                        i = TE.transpose(
                            out=ps_tp[:, dc * 128:(dc + 1) * 128],
                            in_=xn_t[:, base + dc * 512 + sl4 * 128:
                                     base + dc * 512 + sl4 * 128 + 128],
                            identity=c["ident_bf"][:])
                    tt = ch.mark("t", i)
                    ch.wait("v", "t", tt)
                    tp_read = ch.mark("v", V.tensor_copy(
                        out=xn_tm[:, (t * 4 + sl4) * 1024:(t * 4 + sl4 + 1) * 1024],
                        in_=ps_tp[:]))
                xn_read[slot] = ch.counts["t"]
            ch.barrier()

        l1ctx = ExitStack()
        with l1ctx:
            sb1 = lambda name, shape, dt: l1ctx.enter_context(
                nc.sbuf_tensor(p.name(name), shape, dt))
            raw = sb1("raw", [128, 2560], BF16)
            shifted = sb1("shifted", [128, T], BF16)
            s_sb = sb1("s_sb", [128, 2 * T], F32)
            w_sb = sb1("w_sb", [128, 2 * T], BF16)
            wT_sb = sb1("wT_sb", [128, 16 * 128], BF16)
            a_tm = sb1("a_tm", [128, 1024], BF16)
            rows = sb1("rows", [128, 4], F32)
            ps_pool = l1ctx.enter_context(
                nc.psum_tensor(p.name("ps_pool"), [128, 2048], F32))
            ps_tp1 = l1ctx.enter_context(
                nc.psum_tensor(p.name("ps_tp1"), [128, 1024], BF16))
            ps_a = l1ctx.enter_context(
                nc.psum_tensor(p.name("ps_a"), [128, 1024], F32))

            zt = ch.mark("v", V.memset(raw[:, 2048:2560], 0.0))
            st1 = {
                "bank_read": {}, "shift_read": None, "wT_read": None,
                "wsb_read": {}, "tp1_read": None, "a_read": None,
                "atm_read": None, "raw_dma": None, "s_tok": {}, "exp": {},
            }

            def emit_scores(qi):
                I0 = qi * 128
                ng = qi // 4 + 1
                g0 = (2047 - I0 - 127) // 512
                buf = qi % 2
                br = st1["bank_read"]
                for g in range(g0, 4):
                    if g in br:
                        ch.wait("t", br[g][0], br[g][1])
                    i = TE.matmul(out=ps_pool[:, g * 512:(g + 1) * 512],
                                  lhsT=qv[:, I0:I0 + 128],
                                  rhs=a["r_sb"][:, g * 512:(g + 1) * 512],
                                  start=True, stop=True)
                    br.pop(g, None)
                bd_tok = ch.mark("t", i)
                ch.wait("v", "t", bd_tok)
                ch.wait("v", "v", zt)
                if st1["raw_dma"] is not None:
                    ch.wait("v", *st1["raw_dma"])
                for g in range(g0, 4):
                    i = V.tensor_copy(out=raw[:, g * 512:(g + 1) * 512],
                                      in_=ps_pool[:, g * 512:(g + 1) * 512])
                raw_tok = ch.mark("v", i)
                ch.wait("qs", "v", raw_tok)
                if st1["shift_read"] is not None:
                    ch.wait("qs", "v", st1["shift_read"])
                dtok = ch.dma("qs", shifted[:, 0:ng * 512],
                              diag_ap(raw[:], 2560, 2047 - I0, 128, ng * 512))
                st1["raw_dma"] = dtok
                for g in range(ng):
                    ch.wait("t", "v", raw_tok)
                    if g in br:
                        ch.wait("t", br[g][0], br[g][1])
                    i = TE.matmul(out=ps_pool[:, g * 512:(g + 1) * 512],
                                  lhsT=qu[:, I0:I0 + 128],
                                  rhs=k_sb[:, g * 512:(g + 1) * 512],
                                  start=True, stop=True)
                ac_tok = ch.mark("t", i)
                ch.wait("v", "t", ac_tok)
                ch.wait("v", *dtok)
                ch.wait("v", "s")
                for g in range(ng):
                    i = V.tensor_tensor(
                        out=s_sb[:, buf * T + g * 512: buf * T + (g + 1) * 512],
                        in0=ps_pool[:, g * 512:(g + 1) * 512],
                        in1=shifted[:, g * 512:(g + 1) * 512],
                        op=ALU.add)
                    br[g] = ("v", ch.mark("v", i))
                st1["s_tok"][qi] = ch.counts["v"]
                st1["shift_read"] = ch.counts["v"]

            def emit_out(qi):
                I0 = qi * 128
                ng = qi // 4 + 1
                nj = qi + 1
                buf = qi % 2
                ssl = s_sb[:, buf * T: buf * T + ng * 512]
                wsl = w_sb[:, buf * T: buf * T + ng * 512]
                nmax = rows[:, 2 * buf: 2 * buf + 1]
                rden = rows[:, 2 * buf + 1: 2 * buf + 2]
                ch.wait("v", "v", st1["s_tok"][qi])
                ch.wait("v", "s")
                nm_tok = ch.mark("v", V.reduce_max(out=nmax, in_=ssl,
                                                   axis=AX.X, negate=True))
                ch.wait("s", "v", nm_tok)
                wg = st1["wsb_read"].get(buf)
                if wg is not None:
                    ch.wait("s", "t", wg)
                e_tok = ch.mark("s", S.activation(out=wsl, in_=ssl,
                                                  func=AF.Exp, bias=nmax))
                st1["exp"][qi] = e_tok
                ch.wait("v", "s", e_tok)
                m_tok = ch.mark("v", V.tensor_tensor(
                    out=w_sb[:, buf * T + (ng - 1) * 512: buf * T + ng * 512],
                    in0=w_sb[:, buf * T + (ng - 1) * 512: buf * T + ng * 512],
                    in1=a["binmask"][:, (qi % 4) * 512:(qi % 4 + 1) * 512],
                    op=ALU.mult))
                ch.wait("v", "v", m_tok)
                ch.mark("v", V.reduce_sum(out=rden, in_=wsl, axis=AX.X))
                ch.wait("v", "v")
                rd_tok = ch.mark("v", V.reciprocal(out=rden, in_=rden))
                for j0 in range(0, nj, 8):
                    jn = min(8, nj - j0)
                    ch.wait("t", "v", m_tok)
                    ch.wait("t", "s", e_tok)
                    if st1["tp1_read"] is not None:
                        ch.wait("t", "v", st1["tp1_read"])
                    for jj in range(jn):
                        i = TE.transpose(
                            out=ps_tp1[:, jj * 128:(jj + 1) * 128],
                            in_=w_sb[:, buf * T + (j0 + jj) * 128:
                                     buf * T + (j0 + jj + 1) * 128],
                            identity=c["ident_bf"][:])
                    tt = ch.mark("t", i)
                    st1["wsb_read"][buf] = tt
                    ch.wait("v", "t", tt)
                    st1["wT_read"] = ch.mark("v", V.tensor_copy(
                        out=wT_sb[:, j0 * 128:(j0 + jn) * 128],
                        in_=ps_tp1[:, 0:jn * 128]))
                    st1["tp1_read"] = st1["wT_read"]
                ch.wait("t", "v", st1["wT_read"])
                if st1["atm_read"] is not None:
                    ch.wait("t", "s", st1["atm_read"])
                for dh in range(2):
                    for j in range(nj):
                        i = TE.matmul(
                            out=ps_a[:, dh * 512:(dh + 1) * 512],
                            lhsT=wT_sb[:, j * 128:(j + 1) * 128],
                            rhs=xn_tm[:, j * 1024 + dh * 512:
                                      j * 1024 + dh * 512 + 512],
                            start=(j == 0), stop=(j == nj - 1))
                a_tok = ch.mark("t", i)
                ch.wait("s", "t", a_tok)
                ch.wait("s", "v", rd_tok)
                if st1["a_read"] is not None:
                    ch.wait("s", "t", st1["a_read"])
                i = S.activation(out=a_tm[:, 0:512], in_=ps_a[:, 0:512],
                                 func=AF.Copy, scale=rden)
                i = S.activation(out=a_tm[:, 512:1024], in_=ps_a[:, 512:1024],
                                 func=AF.Copy, scale=rden)
                st1["atm_read"] = ch.mark("s", i)
                ch.wait("t", "s", st1["atm_read"])
                if st1["tp1_read"] is not None:
                    ch.wait("t", "v", st1["tp1_read"])
                for dc in range(NCD):
                    i = TE.transpose(out=ps_tp1[:, dc * 128:(dc + 1) * 128],
                                     in_=a_tm[:, dc * 128:(dc + 1) * 128],
                                     identity=c["ident_bf"][:])
                st1["a_read"] = ch.mark("t", i)
                ch.wait("v", "t", st1["a_read"])
                dst = dataclasses.replace(
                    A_fm[:], offset=qi * 128,
                    ap=[[NCD * T, 128], [T, NCD], [1, 128]])
                st1["tp1_read"] = ch.mark("v", V.tensor_copy(out=dst,
                                                             in_=ps_tp1[:]))
            emit_scores(0)
            for qi in range(1, 16):
                emit_scores(qi)
                emit_out(qi - 1)
            emit_out(15)
            ch.barrier()
        s01ctx.close()

        l2ctx = ExitStack()
        with l2ctx:
            scr = emit_ln_scratch(p, c, l2ctx)
            xn_t = l2ctx.enter_context(
                nc.sbuf_tensor(p.name("xn_t2"), [128, NCD * 512], BF16))
            gt_sb = l2ctx.enter_context(
                nc.sbuf_tensor(p.name("gt_sb"), [128, 2 * 512], BF16))
            wvg = l2ctx.enter_context(
                nc.sbuf_tensor(p.name("wvg"), [128, 16 * 512], BF16))
            ps_wv = l2ctx.enter_context(
                nc.psum_tensor(p.name("ps_wv"), [128, 1024], F32))
            ps_gt = l2ctx.enter_context(
                nc.psum_tensor(p.name("ps_gt"), [128, 1024], F32))
            wl_wv = WLoader(p, l2ctx, "wwv", NCD * 128)
            wl_wg = WLoader(p, l2ctx, "wwg", NCD * 128)
            wl_wo = WLoader(p, l2ctx, "wwo", 8 * 128)
            ps_o = scr["ps_b"]
            gscale = [a["gln"][:, dc:dc + 1] for dc in range(NCD)]

            def vg_src(vc):
                return (aps["wv"][:, vc * 128:(vc + 1) * 128].rearrange(
                            "(c p) f -> p c f", p=128),
                        aps["wg"][:, vc * 128:(vc + 1) * 128].rearrange(
                            "(c p) f -> p c f", p=128))

            def wo_src(idx):
                dc_, vh_ = idx // 2, idx % 2
                return aps["wo"][vh_ * 1024:(vh_ + 1) * 1024,
                                 dc_ * 128:(dc_ + 1) * 128].rearrange(
                    "(c p) f -> p c f", p=128)

            v0_, g0_ = vg_src(0)
            vg_nxt = [(wl_wv.load(v0_, cast="s", scale_aps=gscale),
                       wl_wg.load(g0_, cast="s", scale_aps=gscale))]
            wo_nxt = [wl_wo.load(wo_src(0))]
            wv_read = {}
            gt_read = {}
            gts_read = {}
            o_read = {}
            wvg_rd = None
            xn2_read = None
            for t in range(NTT):
                if xn2_read is not None:
                    ch.wait("p", "t", xn2_read)
                xn_tok = emit_ln_tile(p, c, X, t, cview(xn_t), scr,
                                      rms_fold=True)
                step = 0
                for vc in range(16):
                    (sv, vtok), (sg, gtok) = vg_nxt[0]
                    vn_, gn_ = vg_src((vc + 1) % 16)
                    vg_nxt[0] = (wl_wv.load(vn_, cast="s", scale_aps=gscale),
                                 wl_wg.load(gn_, cast="s", scale_aps=gscale))
                    bk = step % 2
                    wsl = ps_wv[:, bk * 512: bk * 512 + 512]
                    gsl = ps_gt[:, bk * 512: bk * 512 + 512]
                    ch.wait("t", vtok[0], vtok[1])
                    if bk in wv_read:
                        ch.wait("t", wv_read[bk][0], wv_read[bk][1])
                    for dc in range(NCD):
                        i = TE.matmul(
                            out=wsl,
                            lhsT=wl_wv.view(sv)[:, dc * 128:(dc + 1) * 128],
                            rhs=A_fm[:, dc * T + t * 512: dc * T + t * 512 + 512],
                            start=(dc == 0), stop=(dc == NCD - 1))
                    wv_tok = ch.mark("t", i)
                    ch.wait("t", gtok[0], gtok[1])
                    ch.wait("t", "p", xn_tok)
                    if bk in gt_read:
                        ch.wait("t", gt_read[bk][0], gt_read[bk][1])
                    for dc in range(NCD):
                        i = TE.matmul(
                            out=gsl,
                            lhsT=wl_wg.view(sg)[:, dc * 128:(dc + 1) * 128],
                            rhs=xn_t[:, dc * 512:(dc + 1) * 512],
                            start=(dc == 0), stop=(dc == NCD - 1))
                    gt_tok = ch.mark("t", i)
                    wl_wv.set_read(sv, "t", wv_tok)
                    wl_wg.set_read(sg, "t", gt_tok)
                    ch.wait("s", "t", gt_tok)
                    if bk in gts_read:
                        ch.wait("s", gts_read[bk][0], gts_read[bk][1])
                    st = ch.mark("s", S.activation(
                        out=gt_sb[:, bk * 512: bk * 512 + 512], in_=gsl,
                        func=AF.Silu))
                    ch.wait("v", "s", st)
                    ch.wait("v", "t", wv_tok)
                    if wvg_rd is not None:
                        ch.wait("v", "t", wvg_rd)
                    i = V.tensor_tensor(out=wvg[:, vc * 512:(vc + 1) * 512],
                                        in0=gt_sb[:, bk * 512: bk * 512 + 512],
                                        in1=wsl, op=ALU.mult)
                    fv = ch.mark("v", i)
                    wv_read[bk] = ("v", fv)
                    gt_read[bk] = ("v", fv)
                    gts_read[bk] = ("v", fv)
                    step += 1
                wvg_tok = ch.counts["v"]
                xn2_read = ch.counts["t"]
                for dc in range(NCD):
                    bk = dc % 2
                    osl = ps_o[:, bk * 512: bk * 512 + 512]
                    if dc < 2:
                        o_read.setdefault(bk, ("v", scr["bcast_read_tok"]))
                    for vh in range(2):
                        so, otok = wo_nxt[0]
                        wo_nxt[0] = wl_wo.load(wo_src((dc * 2 + vh + 1) % 16))
                        ch.wait("t", otok[0], otok[1])
                        ch.wait("t", "v", wvg_tok)
                        if vh == 0 and bk in o_read:
                            ch.wait("t", o_read[bk][0], o_read[bk][1])
                        for vcl in range(8):
                            vc = vh * 8 + vcl
                            i = TE.matmul(
                                out=osl,
                                lhsT=wl_wo.view(so)[:, vcl * 128:(vcl + 1) * 128],
                                rhs=wvg[:, vc * 512:(vc + 1) * 512],
                                start=(vc == 0), stop=(vc == 15))
                        o_tok = ch.mark("t", i)
                        wl_wo.set_read(so, "t", o_tok)
                    ch.wait("v", "t", o_tok)
                    xc = X[:, dc * T + t * 512: dc * T + t * 512 + 512]
                    i = V.tensor_tensor(out=xc, in0=xc, in1=osl, op=ALU.add)
                    o_read[bk] = ("v", ch.mark("v", i))
                wvg_rd = ch.counts["t"]
                scr["bcast_read_tok"] = ch.counts["v"]
                scr["stat_read_tok"] = ch.counts["v"]
                o_read = {}
            ch.barrier()


def build_l0attn_prog(host_map):
    nc = bass.Bass("TRN2", target_bir_lowering=False)
    names = ["x", "ident_bf", "gln", "xlu", "xlv", "binmask", "sinembT",
             "wq", "wk", "wv", "wg", "wr", "wo"]
    aps = declare_inputs(nc, names, host_map)
    y = nc.dram_tensor("y", [NSEQ, D, T], F32, kind="ExternalOutput").ap()
    ctx = ExitStack()
    with ctx:
        ch = Chains(nc, ctx)
        p = Prog(nc, ctx, ch)
        c = emit_consts(p)
        c["ident_bf"] = ctx.enter_context(nc.sbuf_tensor("identb", [128, 128], BF16))
        c["consts_tok"] = [ch.dma("qs", c["ident_bf"][:], aps["ident_bf"][:])]
        X = ctx.enter_context(nc.sbuf_tensor("X", [128, NCD * T], F32))
        st_tok = None
        for seq in range(NSEQ):
            ch.barrier()
            if st_tok is not None:
                ch.wait("qs", *st_tok)
            xt = ch.dma("qs", X[:],
                        aps["x"][seq].rearrange("(c p) t -> p c t", p=128))
            for e in ("v", "s", "t", "p"):
                ch.wait(e, *xt)
            emit_l0attn(p, c, X, aps)
            ch.wait("qs", "v")
            ch.wait("qs", "p")
            st_tok = ch.dma("qs", y[seq].rearrange("(c p) t -> p c t", p=128),
                            X[:])
        nc.sync.wait_ge(ch.sems[st_tok[0]], st_tok[1])
    return nc



def emit_l1attn_setup(p, c, aps, sctx):
    nc, ch = p.nc, p.ch
    sb = lambda name, shape, dt: sctx.enter_context(
        nc.sbuf_tensor(p.name(name), shape, dt))
    a = {}
    for nm in ("bq", "bk", "bv"):
        a[nm] = sb(nm, [128, NCD], F32)
        tk = ch.dma("qs", a[nm][:], aps["l1_" + nm][:])
        ch.wait("s", *tk)
    a["carry"] = sb("carry", [128, 16], F32)
    a["zeros"] = sb("zeros", [128, 512], BF16)
    a["eps8"] = sb("eps8", [128, 1], F32)
    nc.vector.memset(a["zeros"][:], 0.0)
    ch.mark("v", nc.vector.memset(a["eps8"][:], 1e-8))
    ch.wait("s", "v")
    return a


def emit_l1attn(p, c, X, aps):
    nc, ch = p.nc, p.ch
    V, S, TE, P = nc.vector, nc.scalar, nc.tensor, nc.gpsimd
    KSEG = NKC * 512

    sctx = ExitStack()
    with sctx:
        sb = lambda name, shape, dt: sctx.enter_context(
            nc.sbuf_tensor(p.name(name), shape, dt))
        a = emit_l1attn_setup(p, c, aps, sctx)
        tb = sb("dft_tb", [128, 10240], BF16)
        cf = tb[:, 0:NCD * NFREQ]
        sf = tb[:, NCD * NFREQ:2 * NCD * NFREQ]
        ci = tb[:, 0:NKC * 1024]
        si = tb[:, NKC * 1024:2 * NKC * 1024]
        qkv = sb("qkv", [128, 3 * NCD * 512], BF16)
        wl_p = WLoader(p, sctx, "wp", NCD * 128)
        scr = emit_ln_scratch(p, c, sctx, sep_stat=False)
        xn_t = sb("xn_t1a", [128, NCD * 512], BF16)
        G = [sb(f"G{i_}", [128, 2 * KSEG], BF16) for i_ in range(3)]
        m2 = sb("m2", [128, KSEG], F32)
        tt2 = sb("tt2", [128, 1024], F32)
        sq_scr = tt2[:, 512:1024]
        cs = sb("cs", [128, 1], F32)
        ps_p = sctx.enter_context(nc.psum_tensor(p.name("ps_p"), [128, 1024], F32))
        ps_f = sctx.enter_context(nc.psum_tensor(p.name("ps_f"), [128, 2048], F32))

        sigs = (("l1_wqT", a["bq"]), ("l1_wkT", a["bk"]), ("l1_wvT", a["bv"]))

        def pj_src(idx):
            return aps[sigs[idx // NCD][0]][
                :, (idx % NCD) * 128:(idx % NCD + 1) * 128].rearrange(
                "(c p) f -> p c f", p=128)

        st = {"p_read": {}, "f_read": {}, "xn_read": None, "qkv_read": None,
              "proj_tok": None, "inv_tok": None, "z_tok": None, "pnxt": None}

        def emit_lnproj(t):
            if st["xn_read"] is not None:
                ch.wait("p", "t", st["xn_read"])
            xn_tok = emit_ln_tile(p, c, X, t, cview(xn_t), scr)
            if st["pnxt"] is None:
                st["pnxt"] = wl_p.load(pj_src(0))
            step = 0
            for sig, (wnm, bias) in enumerate(sigs):
                for mc in range(NCD):
                    sw, wtok = st["pnxt"]
                    nidx = sig * NCD + mc + 1
                    st["pnxt"] = wl_p.load(pj_src(nidx % (3 * NCD)))
                    bk = step % 2
                    sl = ps_p[:, bk * 512: bk * 512 + 512]
                    ch.wait("t", wtok[0], wtok[1])
                    ch.wait("t", "p", xn_tok)
                    pr = st["p_read"].get(bk)
                    if pr is not None:
                        ch.wait("t", pr[0], pr[1])
                    for dc in range(NCD):
                        i = TE.matmul(
                            out=sl,
                            lhsT=wl_p.view(sw)[:, dc * 128:(dc + 1) * 128],
                            rhs=xn_t[:, dc * 512:(dc + 1) * 512],
                            start=(dc == 0), stop=(dc == NCD - 1))
                    mt = ch.mark("t", i)
                    wl_p.set_read(sw, "t", mt)
                    ch.wait("s", "t", mt)
                    if st["qkv_read"] is not None:
                        ch.wait("s", "t", st["qkv_read"])
                    i = S.activation(
                        out=qkv[:, sig * NCD * 512 + mc * 512:
                                sig * NCD * 512 + (mc + 1) * 512],
                        in_=sl, func=AF.Identity, bias=bias[:, mc:mc + 1])
                    st["p_read"][bk] = ("s", ch.mark("s", i))
                    step += 1
            st["proj_tok"] = ch.counts["s"]
            st["xn_read"] = ch.counts["t"]

        def emit_fwd(t):
            ch.wait("qs", "t")
            ttok1 = ch.dma("qs", cf, aps["dft_c"].rearrange(
                "(c p) k -> p c k", p=128))
            ttok2 = ch.dma("qs", sf, aps["dft_s"].rearrange(
                "(c p) k -> p c k", p=128))
            ch.wait("t", *ttok1)
            ch.wait("t", *ttok2)
            ch.wait("t", "s", st["proj_tok"])
            if st["inv_tok"] is not None:
                ch.wait("v", "t", st["inv_tok"])
            for sig in range(3):
                sbase = sig * NCD * 512
                for kc in range(NKC):
                    bk = kc % 2
                    pre = ps_f[:, bk * 1024: bk * 1024 + 512]
                    pim = ps_f[:, bk * 1024 + 512: bk * 1024 + 1024]
                    for bsl, tbl in ((pre, cf), (pim, sf)):
                        fr = st["f_read"].get(bk)
                        if fr is not None:
                            ch.wait("t", fr[0], fr[1])
                        for dc in range(NCD):
                            i = TE.matmul(
                                out=bsl,
                                lhsT=tbl[:, dc * NFREQ + kc * 128:
                                         dc * NFREQ + (kc + 1) * 128],
                                rhs=qkv[:, sbase + dc * 512:
                                        sbase + (dc + 1) * 512],
                                start=(dc == 0), stop=(dc == NCD - 1))
                    ft = ch.mark("t", i)
                    ch.wait("v", "t", ft)
                    V.tensor_copy(out=G[sig][:, kc * 512:(kc + 1) * 512],
                                  in_=pre)
                    ev = ch.mark("v", V.tensor_copy(
                        out=G[sig][:, KSEG + kc * 512: KSEG + (kc + 1) * 512],
                        in_=pim))
                    ch.wait("s", "t", ft)
                    ch.wait("s", "v")
                    S.activation(out=m2[:, kc * 512:(kc + 1) * 512],
                                 in_=pre, func=AF.Square)
                    stok = ch.mark("s", S.activation(out=sq_scr,
                                                     in_=pim, func=AF.Square))
                    ch.wait("v", "s", stok)
                    mv = ch.mark("v", V.tensor_tensor(
                        out=m2[:, kc * 512:(kc + 1) * 512],
                        in0=m2[:, kc * 512:(kc + 1) * 512],
                        in1=sq_scr, op=ALU.add))
                    st["f_read"][bk] = ("v", mv)
                ch.wait("s", "v")
                stok = ch.mark("s", S.activation(out=m2[:], in_=m2[:],
                                                 func=AF.Sqrt))
                ch.wait("s", "s", stok)
                stok = ch.mark("s", S.activation(out=m2[:], in_=m2[:],
                                                 func=AF.Ln, bias=a["eps8"][:]))
                ch.wait("s", "s", stok)
                stok = ch.mark("s", S.activation(out=m2[:], in_=m2[:],
                                                 func=AF.Exp, scale=-1.0))
                ch.wait("v", "s", stok)
                V.tensor_tensor(out=G[sig][:, 0:KSEG], in0=G[sig][:, 0:KSEG],
                                in1=m2[:], op=ALU.mult)
                gt = ch.mark("v", V.tensor_tensor(
                    out=G[sig][:, KSEG:2 * KSEG],
                    in0=G[sig][:, KSEG:2 * KSEG], in1=m2[:], op=ALU.mult))
                ch.wait("s", "v", gt)

        def emit_vphase(t):
            ch.wait("v", "v")
            for kc in range(NKC):
                ksl = slice(kc * 512, (kc + 1) * 512)
                isl = slice(KSEG + kc * 512, KSEG + (kc + 1) * 512)
                kr, ki_ = G[1][:, ksl], G[1][:, isl]
                vr, vi = G[2][:, ksl], G[2][:, isl]
                V.tensor_tensor(out=tt2[:, 0:512], in0=kr, in1=vr, op=ALU.mult)
                i = V.tensor_tensor(out=tt2[:, 512:1024], in0=kr, in1=vi,
                                    op=ALU.mult)
                ch.wait("v", "v", ch.mark("v", i))
                i = V.tensor_tensor(out=kr, in0=ki_, in1=vi, op=ALU.mult)
                ch.wait("v", "v", ch.mark("v", i))
                i = V.tensor_tensor(out=vr, in0=ki_, in1=vr, op=ALU.mult)
                ch.wait("v", "v", ch.mark("v", i))
                i = V.tensor_tensor(out=kr, in0=tt2[:, 0:512], in1=kr,
                                    op=ALU.subtract)
                ch.wait("v", "v", ch.mark("v", i))
                i = V.tensor_tensor(out=ki_, in0=tt2[:, 512:1024], in1=vr,
                                    op=ALU.add)
                ch.wait("v", "v", ch.mark("v", i))
            for idx in range(2 * NKC):
                seg = slice(idx * 512, (idx + 1) * 512)
                init = 0.0 if t == 0 else a["carry"][:, idx:idx + 1]
                i = V.reduce_sum(out=cs[:], in_=G[1][:, seg], axis=AX.X)
                ch.wait("v", "v", ch.mark("v", i))
                i = V.tensor_tensor_scan(
                    out=G[1][:, seg], data0=a["zeros"][:], data1=G[1][:, seg],
                    initial=init, op0=ALU.add, op1=ALU.add)
                ch.wait("v", "v", ch.mark("v", i))
                if t == 0:
                    i = V.tensor_copy(out=a["carry"][:, idx:idx + 1], in_=cs[:])
                else:
                    i = V.tensor_tensor(out=a["carry"][:, idx:idx + 1],
                                        in0=a["carry"][:, idx:idx + 1],
                                        in1=cs[:], op=ALU.add)
                ch.wait("v", "v", ch.mark("v", i))
            for kc in range(NKC):
                ksl = slice(kc * 512, (kc + 1) * 512)
                isl = slice(KSEG + kc * 512, KSEG + (kc + 1) * 512)
                qr, qi_ = G[0][:, ksl], G[0][:, isl]
                sre, sim = G[1][:, ksl], G[1][:, isl]
                V.tensor_tensor(out=tt2[:, 0:512], in0=sre, in1=qr, op=ALU.mult)
                i = V.tensor_tensor(out=tt2[:, 512:1024], in0=sim, in1=qr,
                                    op=ALU.mult)
                ch.wait("v", "v", ch.mark("v", i))
                i = V.tensor_tensor(out=qr, in0=sre, in1=qi_, op=ALU.mult)
                ch.wait("v", "v", ch.mark("v", i))
                i = V.tensor_tensor(out=qi_, in0=sim, in1=qi_, op=ALU.mult)
                ch.wait("v", "v", ch.mark("v", i))
                i = V.tensor_tensor(out=G[2][:, ksl], in0=tt2[:, 0:512],
                                    in1=qi_, op=ALU.add)
                ch.wait("v", "v", ch.mark("v", i))
                i = V.tensor_tensor(out=G[2][:, isl], in0=tt2[:, 512:1024],
                                    in1=qr, op=ALU.subtract)
                ch.wait("v", "v", ch.mark("v", i))
            st["z_tok"] = ch.counts["v"]

        def emit_inv(t):
            ch.wait("qs", "t")
            tti1 = ch.dma("qs", ci, aps["idft_c"].rearrange(
                "(c p) n -> p c n", p=128))
            tti2 = ch.dma("qs", si, aps["idft_s"].rearrange(
                "(c p) n -> p c n", p=128))
            ch.wait("t", *tti1)
            ch.wait("t", *tti2)
            ch.wait("t", "v", st["z_tok"])
            o_read = {}
            for nch in range(NCD):
                bk = nch % 2
                osl = ps_f[:, bk * 512: bk * 512 + 512]
                prev = o_read.get(bk) or st["f_read"].get(bk)
                if prev is not None:
                    ch.wait("t", prev[0], prev[1])
                first = True
                for kc in range(NKC):
                    ksl = slice(kc * 512, (kc + 1) * 512)
                    isl = slice(KSEG + kc * 512, KSEG + (kc + 1) * 512)
                    TE.matmul(out=osl,
                              lhsT=ci[:, kc * 1024 + nch * 128:
                                      kc * 1024 + (nch + 1) * 128],
                              rhs=G[2][:, ksl], start=first, stop=False)
                    first = False
                    i = TE.matmul(out=osl,
                                  lhsT=si[:, kc * 1024 + nch * 128:
                                          kc * 1024 + (nch + 1) * 128],
                                  rhs=G[2][:, isl], start=False,
                                  stop=(kc == NKC - 1))
                ot = ch.mark("t", i)
                ch.wait("v", "t", ot)
                xc = X[:, nch * T + t * 512: nch * T + t * 512 + 512]
                i = V.tensor_tensor(out=xc, in0=xc, in1=osl, op=ALU.add)
                o_read[bk] = ("v", ch.mark("v", i))
            mx = ("v", max(o_read[0][1], o_read[1][1]))
            st["f_read"][0] = st["f_read"][1] = mx
            st["inv_tok"] = ch.counts["t"]

        emit_lnproj(0)
        for t in range(NTT):
            emit_fwd(t)
            if t + 1 < NTT:
                emit_lnproj(t + 1)
            emit_vphase(t)
            emit_inv(t)
        ch.barrier()


def build_l1attn_prog(host_map):
    nc = bass.Bass("TRN2", target_bir_lowering=False)
    names = ["x", "l1_bq", "l1_bk", "l1_bv", "l1_wqT", "l1_wkT", "l1_wvT",
             "dft_c", "dft_s", "idft_c", "idft_s"]
    aps = declare_inputs(nc, names, host_map)
    y = nc.dram_tensor("y", [NSEQ, D, T], F32, kind="ExternalOutput").ap()
    ctx = ExitStack()
    with ctx:
        ch = Chains(nc, ctx)
        p = Prog(nc, ctx, ch)
        c = emit_consts(p)
        X = ctx.enter_context(nc.sbuf_tensor("X", [128, NCD * T], F32))
        st_tok = None
        for seq in range(NSEQ):
            ch.barrier()
            if st_tok is not None:
                ch.wait("qs", *st_tok)
            xt = ch.dma("qs", X[:],
                        aps["x"][seq].rearrange("(c p) t -> p c t", p=128))
            for e in ("v", "s", "t", "p"):
                ch.wait(e, *xt)
            emit_l1attn(p, c, X, aps)
            ch.wait("qs", "v")
            ch.wait("qs", "p")
            st_tok = ch.dma("qs", y[seq].rearrange("(c p) t -> p c t", p=128),
                            X[:])
        nc.sync.wait_ge(ch.sems[st_tok[0]], st_tok[1])
    return nc



def _bf16(a):
    return np.ascontiguousarray(a.astype(ml_dtypes.bfloat16))


def _f32(a):
    return np.ascontiguousarray(a.astype(np.float32))


def host_consts():
    c = {}
    c["ident_bf"] = _bf16(np.eye(128))
    c["ident_f"] = _f32(np.eye(128))
    for nm, E in (("selc8", 8), ("selc16", 16)):
        sel = np.zeros((E, E * 128), np.float32)
        for e in range(E):
            sel[e, e * 128:(e + 1) * 128] = 1.0
        c[nm] = _bf16(sel)
    pos = np.arange(T, dtype=np.float64)
    inv = 1.0 / (10000.0 ** (np.arange(0, D, 2, dtype=np.float64) / D))
    pre = pos[:, None] * inv[None, :]
    cat = np.concatenate([np.sin(pre), np.cos(pre)], axis=-1)
    sinemb = cat[::-1]
    c["sinembT"] = _bf16(sinemb.T)
    m = np.zeros((4, 128, 512), np.float32)
    for r in range(4):
        for i in range(128):
            m[r, i, :r * 128 + i + 1] = 1.0
    c["binmask"] = _bf16(m.reshape(4 * 128, 512))
    N = D
    n = np.arange(N, dtype=np.float64)
    k = np.arange(NFREQ, dtype=np.float64)
    ang = 2.0 * np.pi * np.outer(n, k) / N
    c["dft_c"] = _bf16(np.cos(ang))
    c["dft_s"] = _bf16(-np.sin(ang))
    wk = np.zeros(NFREQ)
    wk[0] = 1.0
    wk[1:512] = 2.0
    wk[512] = 1.0
    angi = 2.0 * np.pi * np.outer(k, n) / N
    c["idft_c"] = _bf16((wk[:, None] / N) * np.cos(angi))
    c["idft_s"] = _bf16(-(wk[:, None] / N) * np.sin(angi))
    return c


def shard_inputs(inputs):
    x = np.asarray(inputs["L0_attn_x"])
    cores = []
    consts = host_consts()
    w = {}
    w["rtr"] = _f32(inputs["L0_ffn_router_kernel_DE"])
    w["l0_gk"] = _f32(inputs["L0_ffn_kernel_gating_EDF"])
    w["l0_uk"] = _f32(inputs["L0_ffn_kernel_up_EDF"])
    w["l0_dk"] = _f32(inputs["L0_ffn_kernel_down_EFD"])
    w["gln"] = _f32(np.asarray(inputs["L0_attn_g_ln"]).reshape(NCD, 128).T)
    w["wq"] = _f32(inputs["L0_attn_W_q"])
    w["wk"] = _f32(inputs["L0_attn_W_k"])
    w["wv"] = _f32(inputs["L0_attn_W_v"])
    w["wg"] = _f32(inputs["L0_attn_W_g"])
    w["wr"] = _f32(inputs["L0_attn_W_r"])
    w["xlu"] = _f32(np.asarray(inputs["L0_attn_xl_u"]).reshape(128, 1))
    w["xlv"] = _f32(np.asarray(inputs["L0_attn_xl_v"]).reshape(128, 1))
    w["wo"] = _f32(inputs["L0_attn_W_o"])
    w["l1_wqT"] = _f32(np.asarray(inputs["L1_attn_Wq"]).T)
    w["l1_wkT"] = _f32(np.asarray(inputs["L1_attn_Wk"]).T)
    w["l1_wvT"] = _f32(np.asarray(inputs["L1_attn_Wv"]).T)
    w["l1_bq"] = _f32(np.asarray(inputs["L1_attn_bq"]).reshape(NCD, 128).T)
    w["l1_bk"] = _f32(np.asarray(inputs["L1_attn_bk"]).reshape(NCD, 128).T)
    w["l1_bv"] = _f32(np.asarray(inputs["L1_attn_bv"]).reshape(NCD, 128).T)
    w["l1_gwT"] = _f32(np.asarray(inputs["L1_ffn_gate_w"]).T)
    w["l1_ebias"] = _f32(np.asarray(inputs["L1_ffn_expert_bias"]).reshape(1, 16))
    w["l1_gk"] = _f32(np.transpose(np.asarray(inputs["L1_ffn_w_gate"]), (0, 2, 1)))
    w["l1_uk"] = _f32(np.transpose(np.asarray(inputs["L1_ffn_w_up"]), (0, 2, 1)))
    w["l1_dk"] = _f32(np.transpose(np.asarray(inputs["L1_ffn_w_down"]), (0, 2, 1)))
    w["l1_sg"] = _f32(np.asarray(inputs["L1_ffn_shared_gate_w"]).T)
    w["l1_su"] = _f32(np.asarray(inputs["L1_ffn_shared_up_w"]).T)
    w["l1_sd"] = _f32(np.asarray(inputs["L1_ffn_shared_down_w"]).T)
    for ci in range(NCORE):
        xs = x[ci * NSEQ:(ci + 1) * NSEQ]
        x_fm = np.ascontiguousarray(xs.transpose(0, 2, 1))
        m = {"x": _f32(x_fm)}
        m.update(consts)
        m.update(w)
        cores.append(m)
    return cores


def unshard_output(results):
    outs = []
    for r in results:
        y = r["y"]
        outs.append(np.ascontiguousarray(np.transpose(y, (0, 2, 1))))
    return np.concatenate(outs, axis=0).astype(np.float32)



def declare_inputs(nc, names, host_map):
    aps = {}
    for nm in names:
        a = host_map[nm]
        dt = BF16 if a.dtype == ml_dtypes.bfloat16 else F32
        aps[nm] = nc.dram_tensor(nm, list(a.shape), dt, kind="ExternalInput").ap()
    return aps


def build_ffn_prog(layer, host_map):
    nc = bass.Bass("TRN2", target_bir_lowering=False)
    if layer == "l0":
        wnames = ["rtr", "l0_gk", "l0_uk", "l0_dk"]
    else:
        wnames = ["l1_gwT", "l1_ebias", "l1_gk", "l1_uk", "l1_dk",
                  "l1_sg", "l1_su", "l1_sd"]
    names = ["x", "ident_bf", "ident_f", "selc8", "selc16"] + wnames
    aps = declare_inputs(nc, names, host_map)
    y = nc.dram_tensor("y", [NSEQ, D, T], F32, kind="ExternalOutput").ap()

    ctx = ExitStack()
    with ctx:
        ch = Chains(nc, ctx)
        p = Prog(nc, ctx, ch)
        c = emit_consts(p)
        c["ident_bf"] = ctx.enter_context(nc.sbuf_tensor("identb", [128, 128], BF16))
        ctoks = [ch.dma("qs", c["ident_bf"][:], aps["ident_bf"][:])]
        c["ident_f"] = ctx.enter_context(nc.sbuf_tensor("identf", [128, 128], F32))
        ctoks.append(ch.dma("qs", c["ident_f"][:], aps["ident_f"][:]))
        if layer == "l0":
            wd = {"gk": aps["l0_gk"], "uk": aps["l0_uk"], "dk": aps["l0_dk"],
                  "selc": aps["selc8"], "rw": aps["rtr"]}
        else:
            wd = {"gk": aps["l1_gk"], "uk": aps["l1_uk"], "dk": aps["l1_dk"],
                  "sg": aps["l1_sg"], "su": aps["l1_su"], "sd": aps["l1_sd"],
                  "selc": aps["selc16"], "rw": aps["l1_gwT"],
                  "ebias": aps["l1_ebias"]}

        X = ctx.enter_context(nc.sbuf_tensor("X", [128, NCD * T], F32))
        c["consts_tok"] = ctoks
        st_tok = None
        for seq in range(NSEQ):
            ch.barrier()
            if st_tok is not None:
                ch.wait("qs", *st_tok)
            xt = ch.dma("qs", X[:],
                        aps["x"][seq].rearrange("(c p) t -> p c t", p=128))
            ch.wait("v", *xt)
            ch.wait("s", *xt)
            ch.wait("t", *xt)
            ch.wait("p", *xt)
            emit_ffn(p, c, X, wd, layer)
            ch.wait("qs", "p")
            ch.wait("qs", "v")
            st_tok = ch.dma("qs", y[seq].rearrange("(c p) t -> p c t", p=128),
                            X[:])
        nc.sync.wait_ge(ch.sems[st_tok[0]], st_tok[1])
    return nc



def build_full_prog(host_map):
    nc = bass.Bass("TRN2", target_bir_lowering=False)
    names = sorted(set(
        ["x", "ident_bf", "ident_f", "selc8", "selc16", "rtr", "l0_gk",
         "l0_uk", "l0_dk",
         "gln", "xlu", "xlv", "binmask", "sinembT", "wq", "wk", "wv", "wg",
         "wr", "wo", "l1_gwT", "l1_ebias", "l1_gk", "l1_uk", "l1_dk", "l1_sg",
         "l1_su", "l1_sd", "l1_bq", "l1_bk", "l1_bv", "l1_wqT", "l1_wkT",
         "l1_wvT", "dft_c", "dft_s", "idft_c", "idft_s"]))
    aps = declare_inputs(nc, names, host_map)
    y = nc.dram_tensor("y", [NSEQ, D, T], F32, kind="ExternalOutput").ap()

    ctx = ExitStack()
    with ctx:
        ch = Chains(nc, ctx)
        p = Prog(nc, ctx, ch)
        c = emit_consts(p)
        c["ident_bf"] = ctx.enter_context(nc.sbuf_tensor("identb", [128, 128], BF16))
        ctoks = [ch.dma("qs", c["ident_bf"][:], aps["ident_bf"][:])]
        c["ident_f"] = ctx.enter_context(nc.sbuf_tensor("identf", [128, 128], F32))
        ctoks.append(ch.dma("qs", c["ident_f"][:], aps["ident_f"][:]))
        c["consts_tok"] = ctoks
        wd0 = {"gk": aps["l0_gk"], "uk": aps["l0_uk"], "dk": aps["l0_dk"],
               "selc": aps["selc8"], "rw": aps["rtr"]}
        wd1 = {"gk": aps["l1_gk"], "uk": aps["l1_uk"], "dk": aps["l1_dk"],
               "sg": aps["l1_sg"], "su": aps["l1_su"], "sd": aps["l1_sd"],
               "selc": aps["selc16"], "rw": aps["l1_gwT"],
               "ebias": aps["l1_ebias"]}

        X = ctx.enter_context(nc.sbuf_tensor("X", [128, NCD * T], F32))
        st_tok = None
        for seq in range(NSEQ):
            ch.barrier()
            if st_tok is not None:
                ch.wait("qs", *st_tok)
            xt = ch.dma("qs", X[:],
                        aps["x"][seq].rearrange("(c p) t -> p c t", p=128))
            for e in ("v", "s", "t", "p"):
                ch.wait(e, *xt)
            emit_ffn(p, c, X, wd0, "l0")
            emit_l0attn(p, c, X, aps)
            emit_ffn(p, c, X, wd0, "l0")
            emit_ffn(p, c, X, wd1, "l1")
            emit_l1attn(p, c, X, aps)
            emit_ffn(p, c, X, wd1, "l1")
            ch.wait("qs", "v")
            ch.wait("qs", "p")
            st_tok = ch.dma("qs", y[seq].rearrange("(c p) t -> p c t", p=128),
                            X[:])
        nc.sync.wait_ge(ch.sems[st_tok[0]], st_tok[1])
    return nc



FFN_L0_NAMES = ["x", "ident_bf", "ident_f", "selc8", "selc16", "rtr",
                "l0_gk", "l0_uk", "l0_dk"]
FFN_L1_NAMES = ["x", "ident_bf", "ident_f", "selc8", "selc16", "l1_gwT",
                "l1_ebias", "l1_gk", "l1_uk", "l1_dk", "l1_sg", "l1_su",
                "l1_sd"]
ATTN_L0_NAMES = ["x", "ident_bf", "gln", "xlu", "xlv", "binmask", "sinembT",
                 "wq", "wk", "wv", "wg", "wr", "wo"]
ATTN_L1_NAMES = ["x", "l1_bq", "l1_bk", "l1_bv", "l1_wqT", "l1_wkT", "l1_wvT",
                 "dft_c", "dft_s", "idft_c", "idft_s"]

_PROGS = {}


def _get_prog(kind, hm):
    if kind in _PROGS:
        return _PROGS[kind]
    if kind == "ffn_l0":
        nc, names = build_ffn_prog("l0", hm), FFN_L0_NAMES
    elif kind == "ffn_l1":
        nc, names = build_ffn_prog("l1", hm), FFN_L1_NAMES
    elif kind == "attn_l0":
        nc, names = build_l0attn_prog(hm), ATTN_L0_NAMES
    else:
        nc, names = build_l1attn_prog(hm), ATTN_L1_NAMES
    _PROGS[kind] = (nc, names)
    return _PROGS[kind]


FULL_NAMES = sorted(set(
    ["x", "ident_bf", "ident_f", "selc8", "selc16", "rtr", "l0_gk",
     "l0_uk", "l0_dk",
     "gln", "xlu", "xlv", "binmask", "sinembT", "wq", "wk", "wv", "wg",
     "wr", "wo", "l1_gwT", "l1_ebias", "l1_gk", "l1_uk", "l1_dk", "l1_sg",
     "l1_su", "l1_sd", "l1_bq", "l1_bk", "l1_bv", "l1_wqT", "l1_wkT",
     "l1_wvT", "dft_c", "dft_s", "idft_c", "idft_s"]))

FUSED = True


def kernel(**inputs):
    cores = shard_inputs(inputs)
    if FUSED:
        if "full" not in _PROGS:
            _PROGS["full"] = (build_full_prog(cores[0]), FULL_NAMES)
        nc, names = _PROGS["full"]
        in_maps = [{nm: cores[ci][nm] for nm in names} for ci in range(NCORE)]
        res = run_bass_kernel_spmd(nc, in_maps, core_ids=list(range(NCORE)))
        return unshard_output([{"y": np.asarray(r["y"])} for r in res.results])
    xs = [m["x"] for m in cores]
    seqplan = ["ffn_l0", "attn_l0", "ffn_l0", "ffn_l1", "attn_l1", "ffn_l1"]
    for kind in seqplan:
        nc, names = _get_prog(kind, cores[0])
        in_maps = []
        for ci in range(NCORE):
            m = {nm: cores[ci][nm] for nm in names if nm != "x"}
            m["x"] = xs[ci]
            in_maps.append(m)
        res = run_bass_kernel_spmd(nc, in_maps, core_ids=list(range(NCORE)))
        xs = [np.asarray(r["y"]) for r in res.results]
    return unshard_output([{"y": x} for x in xs])
